# version 06
# baseline (speedup 1.0000x reference)
import numpy as np
import ml_dtypes
import concourse.bass as bass
import concourse.mybir as mybir
from concourse.bass_utils import run_bass_kernel_spmd

F32 = mybir.dt.float32
BF16 = mybir.dt.bfloat16
I32 = mybir.dt.int32
U8 = mybir.dt.uint8
AF = mybir.ActivationFunctionType
ALU = mybir.AluOpType
AXX = mybir.AxisListType.X

D = 1024
T = 2048
B = 2
NP = 8
KC = 8
QB = 4
LN2C = float(np.log(2.0))


class Sem:
    def __init__(self, name):
        self.name, self.n, self.h = name, 0, None


class K:
    def __init__(self):
        self.q = {"t": [], "v": [], "s": [], "g": [], "y": []}
        self.sems = []
        self.nbar = 0

    def sem(self, name):
        s = Sem(name)
        self.sems.append(s)
        return s

    def op(self, eng, fn, inc=None, drain=False):
        def run(e):
            ins = fn(e)
            if inc is not None:
                ins.then_inc(inc[0].h, inc[1])
            if drain:
                e.drain()
        self.q[eng].append(run)
        if inc is not None:
            inc[0].n += inc[1]

    def wait(self, eng, s, val):
        self.q[eng].append(lambda e, s=s, val=val: e.wait_ge(s.h, val))

    def inc(self, eng, s, kk=1):
        s.n += kk
        self.q[eng].append(lambda e, s=s, kk=kk: e.sem_inc(s.h, kk))

    def barrier(self, bar):
        self.nbar += 1
        tgt = 5 * self.nbar
        for eng in ("t", "v", "s", "g", "y"):
            self.inc(eng, bar)
            self.wait(eng, bar, tgt)

    def t(self, fn, inc=None):
        self.op("t", fn, inc)

    def v(self, fn, inc=None):
        self.op("v", fn, inc, drain=True)

    def s(self, fn, inc=None):
        self.op("s", fn, inc, drain=True)

    def load(self, dst_fn, src_fn, sem):
        prev = sem.n
        def run(e, d=dst_fn, s=src_fn, m=sem, prev=prev):
            e.wait_ge(m.h, prev)
            e.dma_start(out=d(), in_=s()).then_inc(m.h, 16)
        self.q["y"].append(run)
        sem.n += 16
        return sem.n

    def store(self, dst_fn, src_fn, sem):
        prev = sem.n
        def run(e, d=dst_fn, s=src_fn, m=sem, prev=prev):
            e.wait_ge(m.h, prev)
            e.dma_start(out=d(), in_=s()).then_inc(m.h, 16)
        self.q["g"].append(run)
        sem.n += 16
        return sem.n


def build_program():
    nc = bass.Bass("TRN2", target_bir_lowering=False)
    ap = {}

    def din(name, shape, dt):
        ap[name] = nc.dram_tensor(name, list(shape), dt, kind="ExternalInput").ap()

    din("x_in", (B * T, D), F32)
    din("w_qkv", (D, 3 * D), F32)
    din("w_out", (D, D), F32)
    din("b_out", (1, D), F32)
    din("g1r", (1, D), F32)
    din("b1r", (1, D), F32)
    din("w1", (4 * D, D), F32)
    din("bh1", (1, 4 * D), F32)
    din("g2r", (1, 4 * D), F32)
    din("b2r", (1, 4 * D), F32)
    din("w2", (D, 4 * D), F32)
    din("bh2", (1, D), F32)
    din("c0", (B, D), F32)
    din("wmod0", (D, 6 * D), F32)
    din("Wq", (D, D), F32)
    din("bq", (D, 1), F32)
    din("Wk", (D, D), F32)
    din("bk", (D, 1), F32)
    din("Wv", (D, D), F32)
    din("bv", (D, 1), F32)
    din("Wo", (D, D), F32)
    din("bo", (1, D), F32)
    din("slopes", (1, 16), F32)
    din("w8", (D, D), U8)
    din("wsc", (1, 1), F32)
    din("bf8", (1, D), F32)
    din("c1", (B, D), F32)
    din("wmod1", (D, 6 * D), F32)
    din("identf", (128, 128), F32)
    din("identb", (128, 128), BF16)
    din("maskc", (128, 4 * 512), BF16)
    din("ramp", (128, 16), F32)

    y_out = nc.dram_tensor("y", [B * T, D], F32, kind="ExternalOutput").ap()

    def scr(name, shape, dt):
        ap[name] = nc.dram_tensor(name, list(shape), dt, kind="Internal").ap()

    scr("xs", (B * T, D), F32)
    scr("hfm", (D, T), F32)
    scr("ctx", (D, T), F32)
    scr("wqf", (D, D), F32)
    scr("wkf", (D, D), F32)
    scr("wvf", (D, D), F32)
    scr("wof", (D, D), F32)
    scr("w1q", (D, 4 * D), BF16)
    scr("w2q", (4 * D, D), BF16)
    scr("wf8", (D, D), BF16)
    scr("mods", (2 * B, 6 * D), F32)

    k = K()
    lds = [k.sem(f"ld{i}") for i in range(8)]
    sts = [k.sem(f"st{i}") for i in range(4)]
    idx = {"l": 0, "s": 0}
    sT, sV, sS, BAR = k.sem("sT"), k.sem("sV"), k.sem("sS"), k.sem("bar")

    def next_ld():
        idx["l"] = (idx["l"] + 1) % 8
        return lds[idx["l"]]

    def next_st():
        idx["s"] = (idx["s"] + 1) % 4
        return sts[idx["s"]]

    from contextlib import ExitStack
    es = ExitStack()

    def sb(name, shape, dt):
        return es.enter_context(nc.sbuf_tensor(name, list(shape), dt))

    FP = sb("FP", [128, 13824], F32)
    W1Q = sb("W1Q", [128, 32768], BF16)
    W2Q = sb("W2Q", [128, 4096], BF16)
    XQb = sb("XQb", [128, 10240], BF16)
    BCP = sb("BCP", [128, 12288], BF16)
    CB = sb("CBc", [128, 2304], BF16)
    CF = sb("CFc", [128, 160], F32)
    SV = sb("SVV", [128, 640], F32)
    OO = sb("OOr", [1, 1664], F32)
    OB = sb("OBr", [1, 5248], BF16)
    U8B = sb("U8B", [128, 1024], U8)
    IT32 = sb("IT32", [128, 1024], I32)
    ONB = sb("ONB", [1, 128], BF16)
    CST = sb("CST", [128, 16], F32)
    LNV = sb("LNV", [128, 512], F32)
    ONES128 = sb("ONES128", [128, 128], F32)
    PS = [es.enter_context(nc.psum_tensor(f"PS{i}", [128, 512], F32)) for i in range(7)]
    PSB = es.enter_context(nc.psum_tensor("PSB", [128, 1024], BF16))
    for s_ in k.sems:
        s_.h = es.enter_context(nc.semaphore(s_.name))

    cvals = [-7.0 * LN2C, LN2C, -3.0 * LN2C, 1e-5, 1.0 / 1024.0, 1.0 / 4096.0,
             0.125, -1.0, 1.0, 0.0, 0.5, 60.0]
    for ci_, val in enumerate(cvals):
        nc.const_aps.register(F32, val, CST[:, ci_:ci_ + 1])

    identf = CF[:, 0:128]
    rampc = CF[:, 128:144]
    identb = CB[:, 0:128]
    maskb = CB[:, 128:2176]
    onesf = OO[0:1, 0:128]
    onescol = SV[:, 15:16]
    ALB0 = 16

    def bcast_f32(dst, row_ap, width, bias=0.0, scale=1.0):
        for i in range((width + 511) // 512):
            w = min(512, width - 512 * i)
            k.t(lambda t, i=i, w=w, row_ap=row_ap: t.matmul(PS[6][:, 0:w], onesf, row_ap[0:1, 512 * i:512 * i + w], start=True, stop=True), inc=(sT, 1))
            k.wait("s", sT, sT.n)
            k.s(lambda s, i=i, w=w, dst=dst: s.activation(out=dst[:, 512 * i:512 * i + w], in_=PS[6][:, 0:w], func=AF.Copy, bias=bias, scale=scale), inc=(sS, 1))
            k.wait("t", sS, sS.n)
            k.wait("y", sT, sT.n)

    def bcast_bf16(dst, row_bf, width):
        for i in range((width + 511) // 512):
            w = min(512, width - 512 * i)
            k.t(lambda t, i=i, w=w, row_bf=row_bf: t.matmul(PS[6][:, 0:w], ONB[0:1, :], row_bf[0:1, 512 * i:512 * i + w], start=True, stop=True), inc=(sT, 1))
            k.wait("s", sT, sT.n)
            k.s(lambda s, i=i, w=w, dst=dst: s.activation(out=dst[:, 512 * i:512 * i + w], in_=PS[6][:, 0:w], func=AF.Copy), inc=(sS, 1))
            k.wait("t", sS, sS.n)
            k.wait("y", sT, sT.n)

    def vfloor(dst, src, width, gbuf):
        k.v(lambda v: v.tensor_copy(out=IT32[:, 0:width], in_=src))
        k.v(lambda v: v.tensor_copy(out=gbuf[:, 0:width], in_=IT32[:, 0:width]))
        k.v(lambda v: v.tensor_tensor(out=dst, in0=gbuf[:, 0:width], in1=src, op=ALU.is_gt))
        k.v(lambda v: v.tensor_tensor(out=dst, in0=gbuf[:, 0:width], in1=dst, op=ALU.subtract))

    L = next_ld()
    k.load(lambda: identf, lambda: ap["identf"], L)
    k.load(lambda: identb, lambda: ap["identb"], L)
    k.load(lambda: maskb, lambda: ap["maskc"], L)
    k.load(lambda: rampc, lambda: ap["ramp"], L)
    k.load(lambda: OO[0:1, 128:144], lambda: ap["slopes"], L)
    k.load(lambda: OO[0:1, 144:145], lambda: ap["wsc"], L)
    tA = L.n
    for eng in ("t", "v", "s"):
        k.wait(eng, L, tA)
    k.v(lambda v: v.memset(onesf, 1.0))
    k.v(lambda v: v.memset(ONES128[:], 1.0))
    for ci_, val in enumerate(cvals):
        k.v(lambda v, ci_=ci_, val=val: v.memset(CST[:, ci_:ci_ + 1], val))
    k.v(lambda v: v.memset(ONB[:], 1.0))
    k.v(lambda v: v.memset(onescol, 1.0))
    k.v(lambda v: v.memset(SV[:, 13:15], 0.0))
    k.v(lambda v: v.memset(SV[0:64, 13:14], 1.0))
    k.v(lambda v: v.memset(SV[64:128, 14:15], 1.0), inc=(sV, 1))
    k.wait("t", sV, sV.n)
    k.barrier(BAR)

    cw = FP[:, 0:1024]
    cts = FP[:, 1024:1024 + B * KC]
    wt0 = FP[:, 2048:2560]
    wt1 = FP[:, 2560:3072]
    mst = FP[:, 3072:3584]
    mst2 = FP[:, 3584:4096]
    for lay in range(2):
        L = next_ld()
        k.wait("y", sT, sT.n)
        tw = k.load(lambda: cw[0:B, :], lambda lay=lay: ap["c0" if lay == 0 else "c1"], L)
        k.wait("s", L, tw)
        k.s(lambda s: s.activation(out=cw[0:B, :], in_=cw[0:B, :], func=AF.Silu), inc=(sS, 1))
        k.wait("t", sS, sS.n)
        for c in range(KC):
            k.t(lambda t, c=c: t.transpose(PS[6][:, 0:B], cw[0:B, 128 * c:128 * (c + 1)], identf[0:B, 0:B]), inc=(sT, 1))
            k.wait("v", sT, sT.n)
            k.v(lambda v, c=c: v.tensor_copy(out=cts[:, B * c:B * (c + 1)], in_=PS[6][:, 0:B]), inc=(sV, 1))
            k.wait("t", sV, sV.n)
        for blk in range(12):
            wm = "wmod0" if lay == 0 else "wmod1"
            for c in range(KC):
                Lw = next_ld()
                dst = wt0 if c % 2 == 0 else wt1
                tw = k.load(lambda dst=dst: dst, lambda c=c, blk=blk, wm=wm: ap[wm][128 * c:128 * (c + 1), 512 * blk:512 * (blk + 1)], Lw)
                k.wait("t", Lw, tw)
                k.t(lambda t, c=c, dst=dst: t.matmul(PS[6][0:B, 0:512], cts[:, B * c:B * (c + 1)], dst, start=(c == 0), stop=(c == KC - 1)), inc=(sT, 1))
                if c < KC - 1:
                    k.wait("y", sT, sT.n)
            k.wait("v", sT, sT.n)
            j = blk // 2
            if j in (2, 5):
                k.v(lambda v: v.tensor_scalar(out=mst[0:B, :], in0=PS[6][0:B, 0:512], scalar1=0.0, scalar2=None, op0=ALU.is_ge))
                k.v(lambda v: v.tensor_scalar(out=mst[0:B, :], in0=mst[0:B, :], scalar1=2.0, scalar2=-1.0, op0=ALU.mult, op1=ALU.add))
                k.v(lambda v: v.tensor_scalar(out=mst2[0:B, :], in0=PS[6][0:B, 0:512], scalar1=-1.0, scalar2=None, op0=ALU.mult))
                k.v(lambda v: v.tensor_tensor(out=mst2[0:B, :], in0=mst2[0:B, :], in1=PS[6][0:B, 0:512], op=ALU.max))
                k.v(lambda v: v.tensor_scalar(out=mst2[0:B, :], in0=mst2[0:B, :], scalar1=0.25, scalar2=None, op0=ALU.max))
                k.v(lambda v: v.tensor_tensor(out=mst[0:B, :], in0=mst[0:B, :], in1=mst2[0:B, :], op=ALU.mult), inc=(sV, 1))
            else:
                k.v(lambda v: v.tensor_copy(out=mst[0:B, :], in_=PS[6][0:B, 0:512]), inc=(sV, 1))
            k.wait("t", sV, sV.n)
            k.wait("g", sV, sV.n)
            S = next_st()
            ts = k.store(lambda blk=blk, lay=lay: ap["mods"][B * lay:B * (lay + 1), 512 * blk:512 * (blk + 1)], lambda: mst[0:B, :], S)
            k.wait("v", S, ts)
    k.barrier(BAR)

    wload = FP[:, 0:1024]
    wy = FP[:, 1024:2048]
    wg = FP[:, 2048:3072]
    wc = FP[:, 3072:4096]
    qb = XQb[:, 0:1024]
    tstage = XQb[:, 1024:1152]
    fstage = FP[:, 4096:4224]

    def wquant(wname, dst, nchunk, width, n_elem):
        k.v(lambda v: v.memset(SV[:, 3:4], 0.0), inc=(sV, 1))
        nsub = width // 1024
        for ch in range(nchunk):
            for sbk in range(nsub):
                L = next_ld()
                tw = k.load(lambda: wload, lambda ch=ch, sbk=sbk, wname=wname: ap[wname][128 * ch:128 * (ch + 1), 1024 * sbk:1024 * (sbk + 1)], L)
                k.wait("v", L, tw)
                k.v(lambda v: v.tensor_reduce(out=SV[:, 4:5], in_=wload, axis=AXX, op=ALU.add, apply_absolute_value=True))
                k.v(lambda v: v.tensor_tensor(out=SV[:, 3:4], in0=SV[:, 3:4], in1=SV[:, 4:5], op=ALU.add), inc=(sV, 1))
                k.wait("y", sV, sV.n)
        k.wait("t", sV, sV.n)
        k.t(lambda t: t.matmul(PS[6][0:1, 0:1], onescol, SV[:, 3:4], start=True, stop=True), inc=(sT, 1))
        k.wait("v", sT, sT.n)
        k.v(lambda v: v.tensor_scalar(out=OO[0:1, 150:151], in0=PS[6][0:1, 0:1], scalar1=1.0 / n_elem, scalar2=1e-5, op0=ALU.mult, op1=ALU.max))
        k.v(lambda v: v.reciprocal(out=OO[0:1, 151:152], in_=OO[0:1, 150:151]), inc=(sV, 1))
        k.wait("t", sV, sV.n)
        k.t(lambda t: t.matmul(PS[6][:, 0:1], onesf, OO[0:1, 151:152], start=True, stop=True), inc=(sT, 1))
        k.wait("v", sT, sT.n)
        k.v(lambda v: v.tensor_copy(out=SV[:, 8:9], in_=PS[6][:, 0:1]), inc=(sV, 1))
        k.wait("t", sV, sV.n)
        for ch in range(nchunk):
            for sbk in range(nsub):
                L = next_ld()
                tw = k.load(lambda: wload, lambda ch=ch, sbk=sbk, wname=wname: ap[wname][128 * ch:128 * (ch + 1), 1024 * sbk:1024 * (sbk + 1)], L)
                k.wait("v", L, tw)
                k.wait("v", sT, sT.n)
                k.v(lambda v: v.tensor_scalar(out=wy, in0=wload, scalar1=SV[:, 8:9], scalar2=0.5, op0=ALU.mult, op1=ALU.add))
                vfloor(wy, wy, 1024, wg)
                k.v(lambda v: v.tensor_scalar(out=qb, in0=wy, scalar1=-1.0, scalar2=1.0, op0=ALU.max, op1=ALU.min), inc=(sV, 1))
                k.wait("t", sV, sV.n)
                k.wait("y", sV, sV.n)
                for c in range(8):
                    k.t(lambda t, c=c: t.transpose(PSB[:, 0:128], qb[:, 128 * c:128 * (c + 1)], identb), inc=(sT, 1))
                    k.wait("s", sT, sT.n)
                    k.s(lambda s: s.activation(out=tstage, in_=PSB[:, 0:128], func=AF.Copy), inc=(sS, 1))
                    k.wait("t", sS, sS.n)
                    k.wait("g", sS, sS.n)
                    S = next_st()
                    ts = k.store(lambda ch=ch, sbk=sbk, c=c, dst=dst: dst[1024 * sbk + 128 * c:1024 * sbk + 128 * (c + 1), 128 * ch:128 * (ch + 1)], lambda: tstage, S)
                    k.wait("s", S, ts)

    wquant("w1", ap["w1q"], 32, 1024, 4096.0 * 1024.0)
    wquant("w2", ap["w2q"], 8, 4096, 1024.0 * 4096.0)

    for wname, dname in (("Wq", "wqf"), ("Wk", "wkf"), ("Wv", "wvf"), ("Wo", "wof")):
        for ch in range(KC):
            L = next_ld()
            tw = k.load(lambda: wload, lambda ch=ch, wname=wname: ap[wname][128 * ch:128 * (ch + 1), 0:1024], L)
            k.wait("t", L, tw)
            for c in range(KC):
                k.t(lambda t, c=c: t.transpose(PS[6][:, 0:128], wload[:, 128 * c:128 * (c + 1)], identf), inc=(sT, 1))
                k.wait("v", sT, sT.n)
                k.v(lambda v: v.tensor_copy(out=fstage, in_=PS[6][:, 0:128]), inc=(sV, 1))
                k.wait("t", sV, sV.n)
                k.wait("y", sV, sV.n)
                k.wait("g", sV, sV.n)
                S = next_st()
                ts = k.store(lambda ch=ch, c=c, dname=dname: ap[dname][128 * c:128 * (c + 1), 128 * ch:128 * (ch + 1)], lambda: fstage, S)
                k.wait("v", S, ts)

    for ch in range(KC):
        L = next_ld()
        tw = k.load(lambda: U8B[:, 0:1024], lambda ch=ch: ap["w8"][128 * ch:128 * (ch + 1), :], L)
        k.wait("v", L, tw)
        k.wait("v", sT, sT.n)
        u, sg, e_, m_ = wload, wy, wg, wc
        k.v(lambda v: v.tensor_copy(out=u, in_=U8B[:, 0:1024]))
        k.v(lambda v: v.tensor_scalar(out=sg, in0=u, scalar1=128.0, scalar2=None, op0=ALU.is_ge))
        k.v(lambda v: v.scalar_tensor_tensor(out=u, in0=sg, scalar=-128.0, in1=u, op0=ALU.mult, op1=ALU.add))
        k.v(lambda v: v.tensor_scalar(out=e_, in0=u, scalar1=0.125, scalar2=None, op0=ALU.mult))
        vfloor(e_, e_, 1024, m_)
        k.v(lambda v: v.scalar_tensor_tensor(out=m_, in0=e_, scalar=-8.0, in1=u, op0=ALU.mult, op1=ALU.add))
        k.v(lambda v: v.tensor_scalar(out=u, in0=e_, scalar1=1.0, scalar2=None, op0=ALU.is_ge))
        k.v(lambda v: v.scalar_tensor_tensor(out=m_, in0=m_, scalar=0.125, in1=u, op0=ALU.mult, op1=ALU.add))
        k.v(lambda v: v.tensor_scalar(out=e_, in0=e_, scalar1=1.0, scalar2=None, op0=ALU.max), inc=(sV, 1))
        k.wait("s", sV, sV.n)
        k.s(lambda s: s.activation(out=e_, in_=e_, func=AF.Exp, bias=-7.0 * LN2C, scale=LN2C), inc=(sS, 1))
        k.wait("v", sS, sS.n)
        k.v(lambda v: v.tensor_scalar(out=sg, in0=sg, scalar1=-2.0, scalar2=1.0, op0=ALU.mult, op1=ALU.add))
        k.v(lambda v: v.tensor_tensor(out=m_, in0=m_, in1=e_, op=ALU.mult))
        k.v(lambda v: v.tensor_tensor(out=qb, in0=m_, in1=sg, op=ALU.mult), inc=(sV, 1))
        k.wait("t", sV, sV.n)
        k.wait("y", sV, sV.n)
        for c in range(KC):
            k.t(lambda t, c=c: t.transpose(PSB[:, 0:128], qb[:, 128 * c:128 * (c + 1)], identb), inc=(sT, 1))
            k.wait("s", sT, sT.n)
            k.s(lambda s: s.activation(out=tstage, in_=PSB[:, 0:128], func=AF.Copy), inc=(sS, 1))
            k.wait("t", sS, sS.n)
            k.wait("g", sS, sS.n)
            S = next_st()
            ts = k.store(lambda ch=ch, c=c: ap["wf8"][128 * c:128 * (c + 1), 128 * ch:128 * (ch + 1)], lambda: tstage, S)
            k.wait("s", S, ts)

    for h in range(16):
        k.t(lambda t, h=h: t.matmul(PS[6][:, 0:1], onesf, OO[0:1, 128 + h:129 + h], start=True, stop=True), inc=(sT, 1))
        k.wait("v", sT, sT.n)
        k.v(lambda v: v.tensor_copy(out=SV[:, 9:10], in_=PS[6][:, 0:1]), inc=(sV, 1))
        k.wait("t", sV, sV.n)
        k.v(lambda v, h=h: v.tensor_scalar(out=SV[:, ALB0 + 16 * h:ALB0 + 16 * (h + 1)], in0=rampc, scalar1=SV[:, 9:10], scalar2=None, op0=ALU.mult), inc=(sV, 1))
    k.barrier(BAR)

    def ln_mod_phase(lay, b, half):
        xt = FP[:, 0:1024]
        ht = FP[:, 1024:2048]
        sq = FP[:, 2048:3072]
        bc1 = FP[:, 3072:4096]
        bc2 = FP[:, 4096:5120]
        hs2 = FP[:, 5120:5632]
        j0 = 1 if half == 0 else 4
        Lr = next_ld()
        t2 = k.load(lambda: OO[0:1, 256:256 + 1024], lambda: ap["mods"][B * lay + b:B * lay + b + 1, D * j0:D * (j0 + 1)], Lr)
        k.wait("t", Lr, t2)
        bcast_f32(bc1, OO[0:1, 256:256 + 1024], 1024, bias=1.0)
        Lr = next_ld()
        t2 = k.load(lambda: OO[0:1, 256:256 + 1024], lambda: ap["mods"][B * lay + b:B * lay + b + 1, D * (j0 - 1):D * j0], Lr)
        k.wait("t", Lr, t2)
        bcast_f32(bc2, OO[0:1, 256:256 + 1024], 1024)
        src = ap["x_in"] if (lay == 0 and half == 0) else ap["xs"]
        for tt in range(16):
            row = T * b + 128 * tt
            L = next_ld()
            k.wait("y", sV, sV.n)
            tw = k.load(lambda: xt, lambda row=row, src=src: src[row:row + 128, :], L)
            k.wait("v", L, tw)
            k.v(lambda v: v.tensor_reduce(out=SV[:, 0:1], in_=xt, axis=AXX, op=ALU.add))
            k.v(lambda v: v.tensor_scalar(out=SV[:, 0:1], in0=SV[:, 0:1], scalar1=1.0 / D, scalar2=None, op0=ALU.mult))
            k.v(lambda v: v.tensor_scalar(out=ht, in0=xt, scalar1=SV[:, 0:1], scalar2=None, op0=ALU.subtract), inc=(sV, 1))
            k.wait("s", sV, sV.n)
            k.s(lambda s: s.activation(out=sq, in_=ht, func=AF.Square), inc=(sS, 1))
            k.wait("v", sS, sS.n)
            k.v(lambda v: v.tensor_reduce(out=SV[:, 1:2], in_=sq, axis=AXX, op=ALU.add), inc=(sV, 1))
            k.wait("s", sV, sV.n)
            k.s(lambda s: s.activation(out=SV[:, 1:2], in_=SV[:, 1:2], func=AF.Sqrt, bias=1e-5, scale=1.0 / D), inc=(sS, 1))
            k.wait("v", sS, sS.n)
            k.v(lambda v: v.reciprocal(out=SV[:, 2:3], in_=SV[:, 1:2]))
            k.v(lambda v: v.tensor_scalar(out=ht, in0=ht, scalar1=SV[:, 2:3], scalar2=None, op0=ALU.mult))
            k.v(lambda v: v.tensor_tensor(out=ht, in0=ht, in1=bc1, op=ALU.mult))
            k.v(lambda v: v.tensor_tensor(out=ht, in0=ht, in1=bc2, op=ALU.add), inc=(sV, 1))
            k.wait("t", sV, sV.n)
            for g4 in range(2):
                for cc in range(4):
                    c = 4 * g4 + cc
                    k.t(lambda t, c=c, cc=cc: t.transpose(PS[6][:, 128 * cc:128 * (cc + 1)], ht[:, 128 * c:128 * (c + 1)], identf, ),
                        inc=(sT, 1) if cc == 3 else None)
                k.wait("v", sT, sT.n)
                k.v(lambda v: v.tensor_copy(out=hs2, in_=PS[6][:, 0:512]), inc=(sV, 1))
                k.wait("t", sV, sV.n)
                k.wait("g", sV, sV.n)
                tss = []
                for cc in range(4):
                    c = 4 * g4 + cc
                    S = next_st()
                    tss.append((S, k.store(lambda c=c, tt=tt: ap["hfm"][128 * c:128 * (c + 1), 128 * tt:128 * (tt + 1)], lambda cc=cc: hs2[:, 128 * cc:128 * (cc + 1)], S)))
                for S, ts in tss:
                    k.wait("v", S, ts)
        k.barrier(BAR)

    def attn_phase(lay, b):
        QP = FP[:, 0:2048]
        KP = FP[:, 2048:4096]
        HST = FP[:, 4096:8192]
        VP_A = FP[:, 8192:9232]
        VP_B = FP[:, 9232:10272]
        VS = FP[:, 10272:10784]
        PPa = FP[:, 10784:11296]
        PPb = FP[:, 11296:11808]
        WSL = FP[:, 11808:12192]
        for pair in range(NP):
            if lay == 1:
                Lb = next_ld()
                k.wait("y", sV, sV.n)
                k.load(lambda: SV[:, 10:11], lambda pair=pair: ap["bq"][128 * pair:128 * (pair + 1), :], Lb)
                k.load(lambda: SV[:, 11:12], lambda pair=pair: ap["bk"][128 * pair:128 * (pair + 1), :], Lb)
                tb = k.load(lambda: SV[:, 12:13], lambda pair=pair: ap["bv"][128 * pair:128 * (pair + 1), :], Lb)
                k.wait("v", Lb, tb)
                k.wait("s", Lb, tb)
            for n in range(QB):
                Lh = next_ld()
                tws = []
                for c in range(KC):
                    tws.append(k.load(lambda c=c: HST[:, 512 * c:512 * (c + 1)], lambda c=c, n=n: ap["hfm"][128 * c:128 * (c + 1), 512 * n:512 * (n + 1)], Lh))
                for c in range(KC):
                    Lw = next_ld()
                    if lay == 0:
                        k.load(lambda: WSL[:, 0:128], lambda c=c, pair=pair: ap["w_qkv"][128 * c:128 * (c + 1), 128 * pair:128 * (pair + 1)], Lw)
                        k.load(lambda: WSL[:, 128:256], lambda c=c, pair=pair: ap["w_qkv"][128 * c:128 * (c + 1), D + 128 * pair:D + 128 * (pair + 1)], Lw)
                        tww = k.load(lambda: WSL[:, 256:384], lambda c=c, pair=pair: ap["w_qkv"][128 * c:128 * (c + 1), 2 * D + 128 * pair:2 * D + 128 * (pair + 1)], Lw)
                    else:
                        k.load(lambda: WSL[:, 0:128], lambda c=c, pair=pair: ap["wqf"][128 * c:128 * (c + 1), 128 * pair:128 * (pair + 1)], Lw)
                        k.load(lambda: WSL[:, 128:256], lambda c=c, pair=pair: ap["wkf"][128 * c:128 * (c + 1), 128 * pair:128 * (pair + 1)], Lw)
                        tww = k.load(lambda: WSL[:, 256:384], lambda c=c, pair=pair: ap["wvf"][128 * c:128 * (c + 1), 128 * pair:128 * (pair + 1)], Lw)
                    k.wait("t", Lw, tww)
                    k.wait("t", Lh, tws[c])
                    h_ap = HST[:, 512 * c:512 * (c + 1)]
                    k.t(lambda t, h_ap=h_ap, c=c: t.matmul(PS[0][:, 0:512], WSL[:, 0:128], h_ap, start=(c == 0), stop=(c == KC - 1)))
                    k.t(lambda t, h_ap=h_ap, c=c: t.matmul(PS[1][:, 0:512], WSL[:, 128:256], h_ap, start=(c == 0), stop=(c == KC - 1)))
                    k.t(lambda t, h_ap=h_ap, c=c: t.matmul(PS[2][:, 0:512], WSL[:, 256:384], h_ap, start=(c == 0), stop=(c == KC - 1)), inc=(sT, 1))
                    k.wait("y", sT, sT.n)
                k.wait("v", sT, sT.n)
                k.wait("v", sS, sS.n)
                k.wait("s", sT, sT.n)
                if lay == 0:
                    k.v(lambda v, n=n: v.tensor_copy(out=QP[:, 512 * n:512 * (n + 1)], in_=PS[0][:, 0:512]), inc=(sV, 1))
                    k.s(lambda s, n=n: s.activation(out=KP[:, 512 * n:512 * (n + 1)], in_=PS[1][:, 0:512], func=AF.Copy, scale=0.125), inc=(sS, 1))
                    k.v(lambda v: v.tensor_copy(out=VS, in_=PS[2][:, 0:512]), inc=(sV, 1))
                else:
                    k.v(lambda v, n=n: v.tensor_scalar(out=QP[:, 512 * n:512 * (n + 1)], in0=PS[0][:, 0:512], scalar1=SV[:, 10:11], scalar2=None, op0=ALU.add), inc=(sV, 1))
                    k.v(lambda v, n=n: v.tensor_scalar(out=KP[:, 512 * n:512 * (n + 1)], in0=PS[1][:, 0:512], scalar1=SV[:, 11:12], scalar2=0.125, op0=ALU.add, op1=ALU.mult))
                    k.inc("s", sS)
                    k.v(lambda v: v.tensor_scalar(out=VS, in0=PS[2][:, 0:512], scalar1=SV[:, 12:13], scalar2=None, op0=ALU.add), inc=(sV, 1))
                k.wait("t", sV, sV.n)
                k.wait("t", sS, sS.n)
                for st4 in range(4):
                    k.t(lambda t, st4=st4: t.transpose(PS[3][:, 0:128], VS[:, 128 * st4:128 * (st4 + 1)], identf), inc=(sT, 1))
                    k.wait("v", sT, sT.n)
                    ci = 4 * n + st4
                    k.v(lambda v, ci=ci: v.tensor_copy(out=VP_A[:, 65 * ci:65 * ci + 64], in_=PS[3][:, 0:64]))
                    k.v(lambda v, ci=ci: v.tensor_copy(out=VP_B[:, 65 * ci:65 * ci + 64], in_=PS[3][:, 64:128]), inc=(sV, 1))
                    k.wait("t", sV, sV.n)
            for ci_ in range(16):
                k.v(lambda v, ci_=ci_: v.memset(VP_A[:, 65 * ci_ + 64:65 * ci_ + 65], 1.0))
                k.v(lambda v, ci_=ci_: v.memset(VP_B[:, 65 * ci_ + 64:65 * ci_ + 65], 1.0))
            k.inc("v", sV)
            k.wait("t", sV, sV.n)
            k.wait("s", sV, sV.n)
            for src, oi in ((QP, 0), (KP, 2)):
                for blk in range(4):
                    k.s(lambda s, blk=blk, src=src: s.activation(out=VS, in_=src[:, 512 * blk:512 * (blk + 1)], func=AF.Square), inc=(sS, 1))
                    k.wait("t", sS, sS.n)
                    for hh in range(2):
                        k.t(lambda t, hh=hh: t.matmul(PS[3][0:1, 0:512], SV[:, 13 + hh:14 + hh], VS, start=True, stop=True), inc=(sT, 1))
                        k.wait("v", sT, sT.n)
                        if blk == 0:
                            k.v(lambda v, oi=oi, hh=hh: v.tensor_reduce(out=OO[0:1, 160 + oi + hh:161 + oi + hh], in_=PS[3][0:1, 0:512], axis=AXX, op=ALU.max), inc=(sV, 1))
                        else:
                            k.v(lambda v: v.tensor_reduce(out=OO[0:1, 166:167], in_=PS[3][0:1, 0:512], axis=AXX, op=ALU.max))
                            k.v(lambda v, oi=oi, hh=hh: v.tensor_tensor(out=OO[0:1, 160 + oi + hh:161 + oi + hh], in0=OO[0:1, 160 + oi + hh:161 + oi + hh], in1=OO[0:1, 166:167], op=ALU.max), inc=(sV, 1))
                        k.wait("t", sV, sV.n)
                    k.wait("s", sV, sV.n)
            k.wait("s", sV, sV.n)
            k.s(lambda s: s.activation(out=OO[0:1, 160:164], in_=OO[0:1, 160:164], func=AF.Sqrt), inc=(sS, 1))
            k.wait("v", sS, sS.n)
            k.v(lambda v: v.tensor_tensor(out=OO[0:1, 164:165], in0=OO[0:1, 160:161], in1=OO[0:1, 162:163], op=ALU.mult))
            k.v(lambda v: v.tensor_tensor(out=OO[0:1, 165:166], in0=OO[0:1, 161:162], in1=OO[0:1, 163:164], op=ALU.mult), inc=(sV, 1))
            k.wait("t", sV, sV.n)
            for hh in range(2):
                k.t(lambda t, hh=hh: t.matmul(PS[3][:, 0:1], onesf, OO[0:1, 164 + hh:165 + hh], start=True, stop=True), inc=(sT, 1))
                k.wait("s", sT, sT.n)
                k.s(lambda s, hh=hh: s.activation(out=SV[:, 6 + hh:7 + hh], in_=PS[3][:, 0:1], func=AF.Copy, bias=60.0, scale=-1.0), inc=(sS, 1))
                k.wait("t", sS, sS.n)
                k.wait("v", sS, sS.n)
            if lay == 1:
                for hh in range(2):
                    hid = 2 * pair + hh
                    k.v(lambda v, hh=hh, hid=hid: v.tensor_scalar(out=SV[:, 272 + 16 * hh:288 + 16 * hh], in0=SV[:, ALB0 + 16 * hid:ALB0 + 16 * (hid + 1)], scalar1=SV[:, 6 + hh:7 + hh], scalar2=None, op0=ALU.add), inc=(sV, 1))
                k.wait("s", sV, sV.n)
            for n in range(QB):
                clist = list(range(16)) if lay == 1 else list(range(4 * n + 4))
                k.wait("t", sV, sV.n)
                for ci, c in enumerate(clist):
                    psA = PS[0] if (c % 2 == 0) else PS[1]
                    psB = PS[4] if (c % 2 == 0) else PS[5]
                    crossing = (lay == 0) and (c >= 4 * n)
                    k.t(lambda t, c=c, n=n, psA=psA, crossing=crossing: t.matmul(psA[:, 0:512], KP[0:64, 128 * c:128 * (c + 1)], QP[0:64, 512 * n:512 * (n + 1)], start=True, stop=not crossing))
                    k.t(lambda t, c=c, n=n, psB=psB, crossing=crossing: t.matmul(psB[:, 0:512], KP[64:128, 128 * c:128 * (c + 1)], QP[64:128, 512 * n:512 * (n + 1)], start=True, stop=not crossing),
                        inc=None if crossing else (sT, 1))
                    if crossing:
                        m = c - 4 * n
                        k.t(lambda t, m=m, psA=psA: t.matmul(psA[:, 0:512], identb, maskb[:, 512 * m:512 * (m + 1)], start=False, stop=True))
                        k.t(lambda t, m=m, psB=psB: t.matmul(psB[:, 0:512], identb, maskb[:, 512 * m:512 * (m + 1)], start=False, stop=True), inc=(sT, 1))
                    k.wait("s", sT, sT.n)
                    if lay == 0:
                        biasA, biasB = SV[:, 6:7], SV[:, 7:8]
                    else:
                        biasA, biasB = SV[:, 272 + c:273 + c], SV[:, 288 + c:289 + c]
                    k.s(lambda s, psA=psA, biasA=biasA: s.activation(out=PPa, in_=psA[:, 0:512], func=AF.Exp, bias=biasA, scale=1.0))
                    k.s(lambda s, psB=psB, biasB=biasB: s.activation(out=PPb, in_=psB[:, 0:512], func=AF.Exp, bias=biasB, scale=1.0), inc=(sS, 1))
                    k.wait("t", sS, sS.n)
                    k.t(lambda t, c=c, ci=ci, last=(ci == len(clist) - 1): t.matmul(PS[2][0:65, 0:512], VP_A[:, 65 * c:65 * (c + 1)], PPa, start=(ci == 0), stop=last))
                    k.t(lambda t, c=c, ci=ci, last=(ci == len(clist) - 1): t.matmul(PS[3][0:65, 0:512], VP_B[:, 65 * c:65 * (c + 1)], PPb, start=(ci == 0), stop=last), inc=(sT, 1))
                    k.wait("s", sT, sT.n)
                k.wait("v", sT, sT.n)
                for hh, psx in ((0, PS[2]), (1, PS[3])):
                    k.v(lambda v, psx=psx: v.tensor_copy(out=LNV[64:65, 0:512], in_=psx[64:65, 0:512]), inc=(sV, 1))
                    k.v(lambda v: v.reciprocal(out=LNV[64:65, 0:512], in_=LNV[64:65, 0:512]), inc=(sV, 1))
                    k.wait("t", sV, sV.n)
                    k.t(lambda t: t.matmul(PS[6][:, 0:512], ONES128[64:65, 0:128], LNV[64:65, 0:512], start=True, stop=True), inc=(sT, 1))
                    k.wait("v", sT, sT.n)
                    k.v(lambda v: v.tensor_copy(out=LNV[0:64, 0:512], in_=PS[6][0:64, 0:512]), inc=(sV, 1))
                    k.v(lambda v, psx=psx: v.tensor_tensor(out=VS[0:64, 0:512], in0=psx[0:64, 0:512], in1=LNV[0:64, 0:512], op=ALU.mult), inc=(sV, 1))
                    k.wait("t", sV, sV.n)
                    k.wait("g", sV, sV.n)
                    S = next_st()
                    row = 128 * pair + 64 * hh
                    ts = k.store(lambda row=row, n=n: ap["ctx"][row:row + 64, 512 * n:512 * (n + 1)], lambda: VS[0:64, 0:512], S)
                    k.wait("v", S, ts)
        k.barrier(BAR)

    def outproj_phase(lay, b):
        OSb = FP[:, 0:4096]
        CXT = FP[:, 4096:8192]
        WS2 = FP[:, 8192:8448]
        XT2 = FP[:, 8448:9472]
        TT2 = FP[:, 9472:10496]
        BOC = FP[:, 10496:11520]
        GAC = FP[:, 11520:12544]
        Lb = next_ld()
        bname = "b_out" if lay == 0 else "bo"
        tb = k.load(lambda: OO[0:1, 256:256 + 1024], lambda bname=bname: ap[bname], Lb)
        k.wait("t", Lb, tb)
        bcast_f32(BOC, OO[0:1, 256:256 + 1024], 1024)
        Lr = next_ld()
        t2 = k.load(lambda: OO[0:1, 256:256 + 1024], lambda: ap["mods"][B * lay + b:B * lay + b + 1, 2 * D:3 * D], Lr)
        k.wait("t", Lr, t2)
        bcast_f32(GAC, OO[0:1, 256:256 + 1024], 1024)
        src_x = ap["x_in"] if lay == 0 else ap["xs"]
        wsrc = ap["w_out"] if lay == 0 else ap["wof"]
        for n in range(QB):
            Lc = next_ld()
            tcs = []
            for c in range(KC):
                tcs.append(k.load(lambda c=c: CXT[:, 512 * c:512 * (c + 1)], lambda c=c, n=n: ap["ctx"][128 * c:128 * (c + 1), 512 * n:512 * (n + 1)], Lc))
            for m in range(KC):
                psm = PS[0] if m % 2 == 0 else PS[1]
                for c in range(KC):
                    Lw = next_ld()
                    wap = WS2[:, 0:128] if c % 2 == 0 else WS2[:, 128:256]
                    tww = k.load(lambda wap=wap: wap, lambda c=c, m=m, wsrc=wsrc: wsrc[128 * c:128 * (c + 1), 128 * m:128 * (m + 1)], Lw)
                    k.wait("t", Lw, tww)
                    k.wait("t", Lc, tcs[c])
                    k.t(lambda t, c=c, wap=wap, psm=psm: t.matmul(psm[:, 0:512], wap, CXT[:, 512 * c:512 * (c + 1)], start=(c == 0), stop=(c == KC - 1)), inc=(sT, 1))
                    k.wait("y", sT, sT.n - 1)
                k.wait("v", sT, sT.n)
                k.v(lambda v, m=m, psm=psm: v.tensor_copy(out=OSb[:, 512 * m:512 * (m + 1)], in_=psm[:, 0:512]), inc=(sV, 1))
                k.wait("t", sV, sV.n)
            k.wait("y", sT, sT.n)
            for st4 in range(4):
                tt = 4 * n + st4
                row = T * b + 128 * tt
                Lx = next_ld()
                k.wait("y", sV, sV.n)
                tx = k.load(lambda: XT2, lambda row=row, src_x=src_x: src_x[row:row + 128, :], Lx)
                for m2 in range(2):
                    for cc in range(4):
                        mm = 4 * m2 + cc
                        k.t(lambda t, mm=mm, st4=st4, m2=m2, cc=cc: t.transpose(PS[2 + m2][:, 128 * cc:128 * (cc + 1)], OSb[:, 512 * mm + 128 * st4:512 * mm + 128 * (st4 + 1)], identf),
                            inc=(sT, 1) if cc == 3 else None)
                k.wait("v", sT, sT.n)
                k.wait("v", Lx, tx)
                k.v(lambda v: v.tensor_tensor(out=TT2[:, 0:512], in0=PS[2][:, 0:512], in1=BOC[:, 0:512], op=ALU.add))
                k.v(lambda v: v.tensor_tensor(out=TT2[:, 512:1024], in0=PS[3][:, 0:512], in1=BOC[:, 512:1024], op=ALU.add), inc=(sV, 1))
                k.wait("t", sV, sV.n)
                k.v(lambda v: v.tensor_tensor(out=TT2, in0=TT2, in1=GAC, op=ALU.mult))
                k.v(lambda v: v.tensor_tensor(out=XT2, in0=XT2, in1=TT2, op=ALU.add), inc=(sV, 1))
                k.wait("g", sV, sV.n)
                S = next_st()
                ts = k.store(lambda row=row: ap["xs"][row:row + 128, :], lambda: XT2, S)
                k.wait("v", S, ts)
                k.wait("y", S, ts)
        k.barrier(BAR)

    def ffn0_phase():
        xt = FP[:, 0:1024]
        zt = FP[:, 1024:2048]
        H1 = FP[:, 2048:6144]
        SQF = FP[:, 6144:10240]
        bc1 = FP[:, 10240:11264]
        bc2 = FP[:, 11264:12288]
        bcg = FP[:, 12288:13312]
        OS1 = FP[:, 13312:13824]
        g1b = BCP[:, 0:1024]
        b1b = BCP[:, 1024:2048]
        bh1b = BCP[:, 2048:6144]
        g2b = BCP[:, 6144:10240]
        b2b = BCP[:, 10240:11264]
        xq = XQb[:, 0:1024]
        xqf = XQb[:, 1024:2048]
        xq2 = XQb[:, 2048:6144]
        xq2f = XQb[:, 6144:10240]
        Lw = next_ld()
        tws = []
        for c in range(KC):
            tws.append(k.load(lambda c=c: W1Q[:, 4096 * c:4096 * (c + 1)], lambda c=c: ap["w1q"][128 * c:128 * (c + 1), :], Lw))
        for nmsrc, dst, wid in (("g1r", g1b, 1024), ("b1r", b1b, 1024)):
            Lb = next_ld()
            k.wait("y", sV, sV.n)
            tb = k.load(lambda: OO[0:1, 256:256 + 1024], lambda nmsrc=nmsrc: ap[nmsrc], Lb)
            k.wait("v", Lb, tb)
            k.v(lambda v: v.tensor_copy(out=OB[0:1, 0:1024], in_=OO[0:1, 256:256 + 1024]), inc=(sV, 1))
            k.wait("t", sV, sV.n)
            bcast_bf16(dst, OB[0:1, 0:1024], wid)
        for part in range(4):
            Lb = next_ld()
            k.wait("y", sV, sV.n)
            tb = k.load(lambda: OO[0:1, 256:256 + 1024], lambda part=part: ap["bh1"][0:1, 1024 * part:1024 * (part + 1)], Lb)
            k.wait("v", Lb, tb)
            k.v(lambda v: v.tensor_copy(out=OB[0:1, 0:1024], in_=OO[0:1, 256:256 + 1024]), inc=(sV, 1))
            k.wait("t", sV, sV.n)
            bcast_bf16(bh1b[:, 1024 * part:1024 * (part + 1)], OB[0:1, 0:1024], 1024)
        for part in range(4):
            Lb = next_ld()
            k.wait("y", sV, sV.n)
            tb = k.load(lambda: OO[0:1, 256:256 + 1024], lambda part=part: ap["g2r"][0:1, 1024 * part:1024 * (part + 1)], Lb)
            k.wait("v", Lb, tb)
            k.v(lambda v: v.tensor_copy(out=OB[0:1, 0:1024], in_=OO[0:1, 256:256 + 1024]), inc=(sV, 1))
            k.wait("t", sV, sV.n)
            bcast_bf16(g2b[:, 1024 * part:1024 * (part + 1)], OB[0:1, 0:1024], 1024)
        b2parts = []
        for part in range(4):
            Lb = next_ld()
            k.wait("y", sV, sV.n)
            tb = k.load(lambda: OO[0:1, 256:256 + 1024], lambda part=part: ap["b2r"][0:1, 1024 * part:1024 * (part + 1)], Lb)
            k.wait("v", Lb, tb)
            k.v(lambda v, part=part: v.tensor_copy(out=OB[0:1, 1024 * (part + 1):1024 * (part + 2)], in_=OO[0:1, 256:256 + 1024]), inc=(sV, 1))
            k.wait("t", sV, sV.n)
            k.v(lambda v, part=part: v.tensor_copy(out=OB[0:1, 1024 + 128 * 0:1024 + 128 * 0] if False else OB[0:1, 0:1024], in_=OO[0:1, 256:256 + 1024]), inc=(sV, 1))
        wm1 = SV[:, 304:305]
        wm2 = SV[:, 305:306]
        for wname, dstv, n_elem, nch, wid in (("w1", wm1, 4096.0 * 1024.0, 32, 1024), ("w2", wm2, 1024.0 * 4096.0, 8, 4096)):
            k.v(lambda v: v.memset(SV[:, 3:4], 0.0), inc=(sV, 1))
            for ch in range(nch):
                for sbk in range(wid // 1024):
                    L = next_ld()
                    tw = k.load(lambda: SQF[:, 0:1024], lambda ch=ch, sbk=sbk, wname=wname: ap[wname][128 * ch:128 * (ch + 1), 1024 * sbk:1024 * (sbk + 1)], L)
                    k.wait("v", L, tw)
                    k.v(lambda v: v.tensor_reduce(out=SV[:, 4:5], in_=SQF[:, 0:1024], axis=AXX, op=ALU.add, apply_absolute_value=True))
                    k.v(lambda v: v.tensor_tensor(out=SV[:, 3:4], in0=SV[:, 3:4], in1=SV[:, 4:5], op=ALU.add), inc=(sV, 1))
                    k.wait("y", sV, sV.n)
            k.wait("t", sV, sV.n)
            k.t(lambda t: t.matmul(PS[6][0:1, 0:1], onescol, SV[:, 3:4], start=True, stop=True), inc=(sT, 1))
            k.wait("v", sT, sT.n)
            k.v(lambda v, n_elem=n_elem: v.tensor_scalar(out=OO[0:1, 150:151], in0=PS[6][0:1, 0:1], scalar1=1.0 / n_elem, scalar2=1e-5, op0=ALU.mult, op1=ALU.max), inc=(sV, 1))
            k.wait("t", sV, sV.n)
            k.t(lambda t: t.matmul(PS[6][:, 0:1], onesf, OO[0:1, 150:151], start=True, stop=True), inc=(sT, 1))
            k.wait("v", sT, sT.n)
            k.v(lambda v, dstv=dstv: v.tensor_copy(out=dstv, in_=PS[6][:, 0:1]), inc=(sV, 1))
            k.wait("t", sV, sV.n)
        for c in range(KC):
            k.wait("t", Lw, tws[c])
        for b in range(B):
            Lr = next_ld()
            t2 = k.load(lambda: OO[0:1, 256:1280], lambda b=b: ap["mods"][b:b + 1, 4 * D:5 * D], Lr)
            k.wait("t", Lr, t2)
            bcast_f32(bc1, OO[0:1, 256:1280], 1024, bias=1.0)
            Lr = next_ld()
            t2 = k.load(lambda: OO[0:1, 256:1280], lambda b=b: ap["mods"][b:b + 1, 3 * D:4 * D], Lr)
            k.wait("t", Lr, t2)
            bcast_f32(bc2, OO[0:1, 256:1280], 1024)
            Lr = next_ld()
            t2 = k.load(lambda: OO[0:1, 256:1280], lambda b=b: ap["mods"][b:b + 1, 5 * D:6 * D], Lr)
            k.wait("t", Lr, t2)
            bcast_f32(bcg, OO[0:1, 256:1280], 1024)
            for tt in range(16):
                row = T * b + 128 * tt
                L = next_ld()
                k.wait("y", sV, sV.n)
                tw = k.load(lambda: xt, lambda row=row: ap["xs"][row:row + 128, :], L)
                k.wait("v", L, tw)
                k.v(lambda v: v.tensor_reduce(out=SV[:, 0:1], in_=xt, axis=AXX, op=ALU.add))
                k.v(lambda v: v.tensor_scalar(out=SV[:, 0:1], in0=SV[:, 0:1], scalar1=1.0 / D, scalar2=None, op0=ALU.mult))
                k.v(lambda v: v.tensor_scalar(out=zt, in0=xt, scalar1=SV[:, 0:1], scalar2=None, op0=ALU.subtract), inc=(sV, 1))
                k.wait("s", sV, sV.n)
                k.s(lambda s: s.activation(out=SQF[:, 0:1024], in_=zt, func=AF.Square), inc=(sS, 1))
                k.wait("v", sS, sS.n)
                k.v(lambda v: v.tensor_reduce(out=SV[:, 1:2], in_=SQF[:, 0:1024], axis=AXX, op=ALU.add), inc=(sV, 1))
                k.wait("s", sV, sV.n)
                k.s(lambda s: s.activation(out=SV[:, 1:2], in_=SV[:, 1:2], func=AF.Sqrt, bias=1e-5, scale=1.0 / D), inc=(sS, 1))
                k.wait("v", sS, sS.n)
                k.v(lambda v: v.reciprocal(out=SV[:, 2:3], in_=SV[:, 1:2]))
                k.v(lambda v: v.tensor_scalar(out=zt, in0=zt, scalar1=SV[:, 2:3], scalar2=None, op0=ALU.mult))
                k.v(lambda v: v.tensor_tensor(out=zt, in0=zt, in1=bc1, op=ALU.mult))
                k.v(lambda v: v.tensor_tensor(out=zt, in0=zt, in1=bc2, op=ALU.add))
                k.v(lambda v: v.tensor_reduce(out=SV[:, 0:1], in_=zt, axis=AXX, op=ALU.add))
                k.v(lambda v: v.tensor_scalar(out=SV[:, 0:1], in0=SV[:, 0:1], scalar1=1.0 / D, scalar2=None, op0=ALU.mult))
                k.v(lambda v: v.tensor_scalar(out=zt, in0=zt, scalar1=SV[:, 0:1], scalar2=None, op0=ALU.subtract), inc=(sV, 1))
                k.wait("s", sV, sV.n)
                k.s(lambda s: s.activation(out=SQF[:, 0:1024], in_=zt, func=AF.Square), inc=(sS, 1))
                k.wait("v", sS, sS.n)
                k.v(lambda v: v.tensor_reduce(out=SV[:, 1:2], in_=SQF[:, 0:1024], axis=AXX, op=ALU.add), inc=(sV, 1))
                k.wait("s", sV, sV.n)
                k.s(lambda s: s.activation(out=SV[:, 1:2], in_=SV[:, 1:2], func=AF.Sqrt, bias=1e-5, scale=1.0 / D), inc=(sS, 1))
                k.wait("v", sS, sS.n)
                k.v(lambda v: v.reciprocal(out=SV[:, 2:3], in_=SV[:, 1:2]))
                k.v(lambda v: v.tensor_scalar(out=zt, in0=zt, scalar1=SV[:, 2:3], scalar2=None, op0=ALU.mult))
                k.v(lambda v: v.tensor_tensor(out=zt, in0=zt, in1=g1b, op=ALU.mult))
                k.v(lambda v: v.tensor_tensor(out=zt, in0=zt, in1=b1b, op=ALU.add))
                k.v(lambda v: v.tensor_reduce(out=SV[:, 3:4], in_=zt, axis=AXX, op=ALU.max, apply_absolute_value=True))
                k.v(lambda v: v.tensor_scalar(out=SV[:, 3:4], in0=SV[:, 3:4], scalar1=1e-5, scalar2=None, op0=ALU.max))
                k.v(lambda v: v.reciprocal(out=SV[:, 4:5], in_=SV[:, 3:4]))
                k.v(lambda v: v.tensor_scalar(out=SV[:, 4:5], in0=SV[:, 4:5], scalar1=127.0, scalar2=None, op0=ALU.mult))
                k.v(lambda v: v.reciprocal(out=SV[:, 5:6], in_=SV[:, 4:5]))
                k.v(lambda v: v.tensor_scalar(out=zt, in0=zt, scalar1=SV[:, 4:5], scalar2=0.5, op0=ALU.mult, op1=ALU.add))
                vfloor(zt, zt, 1024, SQF[:, 0:1024])
                k.v(lambda v: v.tensor_scalar(out=xq, in0=zt, scalar1=-128.0, scalar2=127.0, op0=ALU.max, op1=ALU.min), inc=(sV, 1))
                k.v(lambda v: v.tensor_tensor(out=SV[:, 9:10], in0=SV[:, 5:6], in1=wm1, op=ALU.mult), inc=(sV, 1))
                k.wait("t", sV, sV.n)
                for c in range(KC):
                    k.t(lambda t, c=c: t.transpose(PSB[:, 0:128], xq[:, 128 * c:128 * (c + 1)], identb), inc=(sT, 1))
                    k.wait("v", sT, sT.n)
                    k.v(lambda v, c=c: v.tensor_copy(out=xqf[:, 128 * c:128 * (c + 1)], in_=PSB[:, 0:128]), inc=(sV, 1))
                    k.wait("t", sV, sV.n)
                for mg in range(8):
                    for cc in range(4):
                        m = 4 * mg + cc
                        for c in range(KC):
                            k.t(lambda t, m=m, cc=cc, c=c: t.matmul(PS[0][:, 128 * cc:128 * (cc + 1)], W1Q[:, 4096 * c + 128 * m:4096 * c + 128 * (m + 1)], xqf[:, 128 * c:128 * (c + 1)], start=(c == 0 and cc == 0), stop=(c == KC - 1 and cc == 3)),
                                inc=(sT, 1) if (c == KC - 1 and cc == 3) else None)
                    k.wait("v", sT, sT.n)
                    k.v(lambda v: v.tensor_copy(out=OS1, in_=PS[0][:, 0:512]), inc=(sV, 1))
                    k.wait("t", sV, sV.n)
                    for cc in range(4):
                        k.t(lambda t, cc=cc: t.transpose(PS[1][:, 128 * cc:128 * (cc + 1)], OS1[:, 128 * cc:128 * (cc + 1)], identf), inc=(sT, 1) if cc == 3 else None)
                    k.wait("v", sT, sT.n)
                    k.v(lambda v, mg=mg: v.scalar_tensor_tensor(out=H1[:, 512 * mg:512 * (mg + 1)], in0=PS[1][:, 0:512], scalar=SV[:, 9:10], in1=bh1b[:, 512 * mg:512 * (mg + 1)], op0=ALU.mult, op1=ALU.add), inc=(sV, 1))
                    k.wait("t", sV, sV.n)
                k.wait("s", sV, sV.n)
                k.s(lambda s: s.activation(out=H1, in_=H1, func=AF.Gelu), inc=(sS, 1))
                k.wait("v", sS, sS.n)
                k.v(lambda v: v.tensor_reduce(out=SV[:, 0:1], in_=H1, axis=AXX, op=ALU.add))
                k.v(lambda v: v.tensor_scalar(out=SV[:, 0:1], in0=SV[:, 0:1], scalar1=1.0 / 4096.0, scalar2=None, op0=ALU.mult))
                k.v(lambda v: v.tensor_scalar(out=H1, in0=H1, scalar1=SV[:, 0:1], scalar2=None, op0=ALU.subtract), inc=(sV, 1))
                k.wait("s", sV, sV.n)
                k.s(lambda s: s.activation(out=SQF, in_=H1, func=AF.Square), inc=(sS, 1))
                k.wait("v", sS, sS.n)
                k.v(lambda v: v.tensor_reduce(out=SV[:, 1:2], in_=SQF, axis=AXX, op=ALU.add), inc=(sV, 1))
                k.wait("s", sV, sV.n)
                k.s(lambda s: s.activation(out=SV[:, 1:2], in_=SV[:, 1:2], func=AF.Sqrt, bias=1e-5, scale=1.0 / 4096.0), inc=(sS, 1))
                k.wait("v", sS, sS.n)
                k.v(lambda v: v.reciprocal(out=SV[:, 2:3], in_=SV[:, 1:2]))
                k.v(lambda v: v.tensor_scalar(out=H1, in0=H1, scalar1=SV[:, 2:3], scalar2=None, op0=ALU.mult))
                k.v(lambda v: v.tensor_tensor(out=H1, in0=H1, in1=g2b, op=ALU.mult), inc=(sV, 1))
                for part in range(4):
                    bcast_bf16(b2b, OB[0:1, 1024 * (part + 1):1024 * (part + 2)], 1024)
                    k.wait("v", sS, sS.n)
                    k.v(lambda v, part=part: v.tensor_tensor(out=H1[:, 1024 * part:1024 * (part + 1)], in0=H1[:, 1024 * part:1024 * (part + 1)], in1=b2b, op=ALU.add), inc=(sV, 1))
                    k.wait("s", sV, sV.n)
                k.v(lambda v: v.tensor_reduce(out=SV[:, 3:4], in_=H1, axis=AXX, op=ALU.max, apply_absolute_value=True))
                k.v(lambda v: v.tensor_scalar(out=SV[:, 3:4], in0=SV[:, 3:4], scalar1=1e-5, scalar2=None, op0=ALU.max))
                k.v(lambda v: v.reciprocal(out=SV[:, 4:5], in_=SV[:, 3:4]))
                k.v(lambda v: v.tensor_scalar(out=SV[:, 4:5], in0=SV[:, 4:5], scalar1=127.0, scalar2=None, op0=ALU.mult))
                k.v(lambda v: v.reciprocal(out=SV[:, 5:6], in_=SV[:, 4:5]))
                k.v(lambda v: v.tensor_scalar(out=H1, in0=H1, scalar1=SV[:, 4:5], scalar2=0.5, op0=ALU.mult, op1=ALU.add))
                for part in range(4):
                    sl = slice(1024 * part, 1024 * (part + 1))
                    vfloor(H1[:, sl], H1[:, sl], 1024, SQF[:, 0:1024])
                k.v(lambda v: v.tensor_scalar(out=xq2, in0=H1, scalar1=-128.0, scalar2=127.0, op0=ALU.max, op1=ALU.min), inc=(sV, 1))
                k.v(lambda v: v.tensor_tensor(out=SV[:, 9:10], in0=SV[:, 5:6], in1=wm2, op=ALU.mult), inc=(sV, 1))
                k.wait("t", sV, sV.n)
                for c in range(32):
                    k.t(lambda t, c=c: t.transpose(PSB[:, 0:128], xq2[:, 128 * c:128 * (c + 1)], identb), inc=(sT, 1))
                    k.wait("v", sT, sT.n)
                    k.v(lambda v, c=c: v.tensor_copy(out=xq2f[:, 128 * c:128 * (c + 1)], in_=PSB[:, 0:128]), inc=(sV, 1))
                    k.wait("t", sV, sV.n)
                for mg in range(2):
                    for cc in range(4):
                        m = 4 * mg + cc
                        Lw2 = next_ld()
                        tws2 = []
                        wslot = W2Q[:, 0:4096]
                        for c in range(32):
                            tws2.append(k.load(lambda c=c, wslot=wslot: wslot[:, 128 * c:128 * (c + 1)], lambda c=c, m=m: ap["w2q"][128 * c:128 * (c + 1), 128 * m:128 * (m + 1)], Lw2))
                        k.wait("t", Lw2, tws2[-1])
                        for c in range(32):
                            k.t(lambda t, c=c, cc=cc, wslot=wslot: t.matmul(PS[0][:, 128 * cc:128 * (cc + 1)], wslot[:, 128 * c:128 * (c + 1)], xq2f[:, 128 * c:128 * (c + 1)], start=(c == 0 and cc == 0), stop=(c == 31 and cc == 3)),
                                inc=(sT, 1) if (c == 31) else None)
                        k.wait("y", sT, sT.n)
                    k.wait("v", sT, sT.n)
                    k.v(lambda v: v.tensor_copy(out=OS1, in_=PS[0][:, 0:512]), inc=(sV, 1))
                    k.wait("t", sV, sV.n)
                    for cc in range(4):
                        k.t(lambda t, cc=cc: t.transpose(PS[1][:, 128 * cc:128 * (cc + 1)], OS1[:, 128 * cc:128 * (cc + 1)], identf), inc=(sT, 1) if cc == 3 else None)
                    k.wait("v", sT, sT.n)
                    k.v(lambda v, mg=mg: v.scalar_tensor_tensor(out=zt[:, 512 * mg:512 * (mg + 1)], in0=PS[1][:, 0:512], scalar=SV[:, 9:10], in1=BCP[:, 11264 + 512 * mg:11264 + 512 * (mg + 1)], op0=ALU.mult, op1=ALU.add), inc=(sV, 1))
                    k.wait("t", sV, sV.n)
                k.v(lambda v: v.tensor_tensor(out=zt, in0=zt, in1=bcg, op=ALU.mult))
                k.v(lambda v: v.tensor_tensor(out=xt, in0=xt, in1=zt, op=ALU.add), inc=(sV, 1))
                k.wait("g", sV, sV.n)
                S = next_st()
                ts = k.store(lambda row=row: ap["xs"][row:row + 128, :], lambda: xt, S)
                k.wait("v", S, ts)
                k.wait("y", S, ts)
        k.barrier(BAR)

    def bh2_bcast():
        Lb = next_ld()
        k.wait("y", sV, sV.n)
        tb = k.load(lambda: OO[0:1, 256:256 + 1024], lambda: ap["bh2"], Lb)
        k.wait("v", Lb, tb)
        k.v(lambda v: v.tensor_copy(out=OB[0:1, 0:1024], in_=OO[0:1, 256:256 + 1024]), inc=(sV, 1))
        k.wait("t", sV, sV.n)
        bcast_bf16(BCP[:, 11264:12288], OB[0:1, 0:1024], 1024)

    def ffn1_phase():
        xt = FP[:, 0:1024]
        yt = FP[:, 1024:2048]
        at = FP[:, 2048:3072]
        et = FP[:, 3072:4096]
        st_ = FP[:, 4096:5120]
        ut = FP[:, 5120:6144]
        cm = FP[:, 6144:7168]
        zt = FP[:, 7168:8192]
        OS1 = FP[:, 8192:8704]
        bc1 = FP[:, 10240:11264]
        bc2 = FP[:, 11264:12288]
        bcg = FP[:, 12288:13312]
        bfb = FP[:, 8704:9728]
        xq = XQb[:, 0:1024]
        xqf = XQb[:, 1024:2048]
        Lw = next_ld()
        tws = []
        for c in range(KC):
            tws.append(k.load(lambda c=c: W2Q[:, 1024 * c:1024 * (c + 1)] if False else W1Q[:, 1024 * c:1024 * (c + 1)], lambda c=c: ap["wf8"][128 * c:128 * (c + 1), :], Lw))
        Lb = next_ld()
        tb = k.load(lambda: OO[0:1, 256:256 + 1024], lambda: ap["bf8"], Lb)
        k.wait("t", Lb, tb)
        bcast_f32(bfb, OO[0:1, 256:256 + 1024], 1024)
        k.t(lambda t: t.matmul(PS[6][:, 0:1], onesf, OO[0:1, 144:145], start=True, stop=True), inc=(sT, 1))
        k.wait("v", sT, sT.n)
        k.v(lambda v: v.tensor_copy(out=SV[:, 12:13], in_=PS[6][:, 0:1]), inc=(sV, 1))
        k.wait("t", sV, sV.n)
        for c in range(KC):
            k.wait("t", Lw, tws[c])
        for b in range(B):
            Lr = next_ld()
            t2 = k.load(lambda: OO[0:1, 256:1280], lambda b=b: ap["mods"][B + b:B + b + 1, 4 * D:5 * D], Lr)
            k.wait("t", Lr, t2)
            bcast_f32(bc1, OO[0:1, 256:1280], 1024, bias=1.0)
            Lr = next_ld()
            t2 = k.load(lambda: OO[0:1, 256:1280], lambda b=b: ap["mods"][B + b:B + b + 1, 3 * D:4 * D], Lr)
            k.wait("t", Lr, t2)
            bcast_f32(bc2, OO[0:1, 256:1280], 1024)
            Lr = next_ld()
            t2 = k.load(lambda: OO[0:1, 256:1280], lambda b=b: ap["mods"][B + b:B + b + 1, 5 * D:6 * D], Lr)
            k.wait("t", Lr, t2)
            bcast_f32(bcg, OO[0:1, 256:1280], 1024)
            for tt in range(16):
                row = T * b + 128 * tt
                L = next_ld()
                k.wait("y", sV, sV.n)
                tw = k.load(lambda: xt, lambda row=row: ap["xs"][row:row + 128, :], L)
                k.wait("v", L, tw)
                k.v(lambda v: v.tensor_reduce(out=SV[:, 0:1], in_=xt, axis=AXX, op=ALU.add))
                k.v(lambda v: v.tensor_scalar(out=SV[:, 0:1], in0=SV[:, 0:1], scalar1=1.0 / D, scalar2=None, op0=ALU.mult))
                k.v(lambda v: v.tensor_scalar(out=zt, in0=xt, scalar1=SV[:, 0:1], scalar2=None, op0=ALU.subtract), inc=(sV, 1))
                k.wait("s", sV, sV.n)
                k.s(lambda s: s.activation(out=cm, in_=zt, func=AF.Square), inc=(sS, 1))
                k.wait("v", sS, sS.n)
                k.v(lambda v: v.tensor_reduce(out=SV[:, 1:2], in_=cm, axis=AXX, op=ALU.add), inc=(sV, 1))
                k.wait("s", sV, sV.n)
                k.s(lambda s: s.activation(out=SV[:, 1:2], in_=SV[:, 1:2], func=AF.Sqrt, bias=1e-5, scale=1.0 / D), inc=(sS, 1))
                k.wait("v", sS, sS.n)
                k.v(lambda v: v.reciprocal(out=SV[:, 2:3], in_=SV[:, 1:2]))
                k.v(lambda v: v.tensor_scalar(out=zt, in0=zt, scalar1=SV[:, 2:3], scalar2=None, op0=ALU.mult))
                k.v(lambda v: v.tensor_tensor(out=zt, in0=zt, in1=bc1, op=ALU.mult))
                k.v(lambda v: v.tensor_tensor(out=zt, in0=zt, in1=bc2, op=ALU.add))
                k.v(lambda v: v.tensor_reduce(out=SV[:, 3:4], in_=zt, axis=AXX, op=ALU.max, apply_absolute_value=True))
                k.v(lambda v: v.tensor_scalar(out=SV[:, 3:4], in0=SV[:, 3:4], scalar1=1e-12, scalar2=1.0 / 448.0, op0=ALU.max, op1=ALU.mult))
                k.v(lambda v: v.reciprocal(out=SV[:, 4:5], in_=SV[:, 3:4]))
                k.v(lambda v: v.tensor_scalar(out=yt, in0=zt, scalar1=SV[:, 4:5], scalar2=None, op0=ALU.mult))
                k.v(lambda v: v.tensor_scalar(out=at, in0=yt, scalar1=-1.0, scalar2=None, op0=ALU.mult))
                k.v(lambda v: v.tensor_tensor(out=at, in0=at, in1=yt, op=ALU.max))
                k.v(lambda v: v.tensor_scalar(out=at, in0=at, scalar1=1e-30, scalar2=None, op0=ALU.max), inc=(sV, 1))
                k.wait("s", sV, sV.n)
                k.s(lambda s: s.activation(out=at, in_=at, func=AF.Ln), inc=(sS, 1))
                k.wait("v", sS, sS.n)
                k.v(lambda v: v.tensor_scalar(out=et, in0=at, scalar1=1.0 / LN2C, scalar2=None, op0=ALU.mult))
                vfloor(et, et, 1024, cm)
                k.v(lambda v: v.tensor_scalar(out=et, in0=et, scalar1=-6.0, scalar2=8.0, op0=ALU.max, op1=ALU.min), inc=(sV, 1))
                k.wait("s", sV, sV.n)
                k.s(lambda s: s.activation(out=st_, in_=et, func=AF.Exp, bias=-3.0 * LN2C, scale=LN2C), inc=(sS, 1))
                k.wait("v", sS, sS.n)
                k.v(lambda v: v.reciprocal(out=ut, in_=st_))
                k.v(lambda v: v.tensor_tensor(out=ut, in0=yt, in1=ut, op=ALU.mult))
                k.v(lambda v: v.tensor_scalar(out=ut, in0=ut, scalar1=0.5, scalar2=None, op0=ALU.add))
                vfloor(ut, ut, 1024, cm)
                k.v(lambda v: v.tensor_tensor(out=xq, in0=ut, in1=st_, op=ALU.mult), inc=(sV, 1))
                k.v(lambda v: v.tensor_tensor(out=SV[:, 9:10], in0=SV[:, 3:4], in1=SV[:, 12:13], op=ALU.mult), inc=(sV, 1))
                k.wait("t", sV, sV.n)
                for c in range(KC):
                    k.t(lambda t, c=c: t.transpose(PSB[:, 0:128], xq[:, 128 * c:128 * (c + 1)], identb), inc=(sT, 1))
                    k.wait("v", sT, sT.n)
                    k.v(lambda v, c=c: v.tensor_copy(out=xqf[:, 128 * c:128 * (c + 1)], in_=PSB[:, 0:128]), inc=(sV, 1))
                    k.wait("t", sV, sV.n)
                for mg in range(2):
                    for cc in range(4):
                        m = 4 * mg + cc
                        for c in range(KC):
                            k.t(lambda t, m=m, cc=cc, c=c: t.matmul(PS[0][:, 128 * cc:128 * (cc + 1)], W1Q[:, 1024 * c + 128 * m:1024 * c + 128 * (m + 1)], xqf[:, 128 * c:128 * (c + 1)], start=(c == 0 and cc == 0), stop=(c == KC - 1 and cc == 3)),
                                inc=(sT, 1) if (c == KC - 1 and cc == 3) else None)
                    k.wait("v", sT, sT.n)
                    k.v(lambda v: v.tensor_copy(out=OS1, in_=PS[0][:, 0:512]), inc=(sV, 1))
                    k.wait("t", sV, sV.n)
                    for cc in range(4):
                        k.t(lambda t, cc=cc: t.transpose(PS[1][:, 128 * cc:128 * (cc + 1)], OS1[:, 128 * cc:128 * (cc + 1)], identf), inc=(sT, 1) if cc == 3 else None)
                    k.wait("v", sT, sT.n)
                    k.v(lambda v, mg=mg: v.scalar_tensor_tensor(out=zt[:, 512 * mg:512 * (mg + 1)], in0=PS[1][:, 0:512], scalar=SV[:, 9:10], in1=bfb[:, 512 * mg:512 * (mg + 1)], op0=ALU.mult, op1=ALU.add), inc=(sV, 1))
                    k.wait("t", sV, sV.n)
                k.v(lambda v: v.tensor_tensor(out=zt, in0=zt, in1=bcg, op=ALU.mult))
                k.v(lambda v: v.tensor_tensor(out=xt, in0=xt, in1=zt, op=ALU.add), inc=(sV, 1))
                k.wait("g", sV, sV.n)
                S = next_st()
                ts = k.store(lambda row=row: y_out[row:row + 128, :], lambda: xt, S)
                k.wait("v", S, ts)
                k.wait("y", S, ts)
        k.barrier(BAR)

    import os
    KPH = int(os.environ.get("KPH", "99"))
    bh2_bcast()
    for lay in range(2):
        if KPH >= (1 if lay == 0 else 3):
            for b in range(B):
                ln_mod_phase(lay, b, 0)
                attn_phase(lay, b)
                outproj_phase(lay, b)
        if lay == 0:
            if KPH >= 2:
                ffn0_phase()
        else:
            if KPH >= 4:
                ffn1_phase()

    for s_ in sts:
        k.wait("g", s_, s_.n)

    with nc.Block() as block:
        @block.tensor
        def _(t):
            for f in k.q["t"]:
                f(t)

        @block.vector
        def _(v):
            for f in k.q["v"]:
                f(v)

        @block.scalar
        def _(s):
            for f in k.q["s"]:
                f(s)

        @block.gpsimd
        def _(g):
            for f in k.q["g"]:
                f(g)

        @block.sync
        def _(y):
            for f in k.q["y"]:
                f(y)

    es.close()
    return nc


_NC_CACHE = [None]


def kernel(**inputs):
    n = 8
    xfull = np.ascontiguousarray(np.asarray(inputs["L0_attn_x"], dtype=np.float32))
    maskc = np.zeros((128, 4 * 512), dtype=np.float64)
    for m in range(4):
        i = np.arange(128)[:, None]
        j = np.arange(512)[None, :]
        maskc[:, 512 * m:512 * (m + 1)] = np.where(i + 128 * m > j, -1e10, 0.0)
    maskc = maskc.astype(ml_dtypes.bfloat16)
    ramp = (np.arange(128)[:, None] + 128 * np.arange(16)[None, :] - 2047).astype(np.float32)
    common = {
        "w_qkv": np.asarray(inputs["L0_attn_w_qkv"], np.float32),
        "w_out": np.asarray(inputs["L0_attn_w_out"], np.float32),
        "b_out": np.asarray(inputs["L0_attn_b_out"], np.float32).reshape(1, D),
        "g1r": np.asarray(inputs["L0_ffn_ln1_g"], np.float32).reshape(1, D),
        "b1r": np.asarray(inputs["L0_ffn_ln1_b"], np.float32).reshape(1, D),
        "w1": np.asarray(inputs["L0_ffn_w1"], np.float32),
        "bh1": np.asarray(inputs["L0_ffn_b1"], np.float32).reshape(1, 4 * D),
        "g2r": np.asarray(inputs["L0_ffn_ln2_g"], np.float32).reshape(1, 4 * D),
        "b2r": np.asarray(inputs["L0_ffn_ln2_b"], np.float32).reshape(1, 4 * D),
        "w2": np.asarray(inputs["L0_ffn_w2"], np.float32),
        "bh2": np.asarray(inputs["L0_ffn_b2"], np.float32).reshape(1, D),
        "wmod0": np.asarray(inputs["L0_wmod"], np.float32),
        "Wq": np.asarray(inputs["L1_attn_Wq"], np.float32),
        "bq": np.asarray(inputs["L1_attn_bq"], np.float32).reshape(D, 1),
        "Wk": np.asarray(inputs["L1_attn_Wk"], np.float32),
        "bk": np.asarray(inputs["L1_attn_bk"], np.float32).reshape(D, 1),
        "Wv": np.asarray(inputs["L1_attn_Wv"], np.float32),
        "bv": np.asarray(inputs["L1_attn_bv"], np.float32).reshape(D, 1),
        "Wo": np.asarray(inputs["L1_attn_Wo"], np.float32),
        "bo": np.asarray(inputs["L1_attn_bo"], np.float32).reshape(1, D),
        "slopes": np.asarray(inputs["L1_attn_slopes"], np.float32).reshape(1, 16),
        "w8": np.asarray(inputs["L1_ffn_weight"]).view(np.uint8),
        "wsc": np.asarray(inputs["L1_ffn_weight_scale"], np.float32).reshape(1, 1),
        "bf8": np.asarray(inputs["L1_ffn_bias"], np.float32).reshape(1, D),
        "wmod1": np.asarray(inputs["L1_wmod"], np.float32),
        "identf": np.eye(128, dtype=np.float32),
        "identb": np.eye(128, dtype=ml_dtypes.bfloat16),
        "maskc": maskc,
        "ramp": ramp,
    }
    c0 = np.asarray(inputs["L0_c"], np.float32)
    c1 = np.asarray(inputs["L1_c"], np.float32)
    in_maps = []
    for i in range(n):
        m = dict(common)
        m["x_in"] = np.ascontiguousarray(xfull[2 * i:2 * i + 2].reshape(B * T, D))
        m["c0"] = np.ascontiguousarray(c0[2 * i:2 * i + 2])
        m["c1"] = np.ascontiguousarray(c1[2 * i:2 * i + 2])
        in_maps.append(m)
    if _NC_CACHE[0] is None:
        _NC_CACHE[0] = build_program()
    nc = _NC_CACHE[0]
    res = run_bass_kernel_spmd(nc, in_maps, core_ids=list(range(n)))
    out = np.concatenate([r["y"].reshape(1, B, T, D) for r in res.results], axis=0)
    return np.ascontiguousarray(out.reshape(16, T, D)).astype(np.float32)


# version 07
# speedup vs baseline: 1.6255x; 1.6255x over previous
import numpy as np
import ml_dtypes
import concourse.bass as bass
import concourse.mybir as mybir
from concourse.bass_utils import run_bass_kernel_spmd

F32 = mybir.dt.float32
BF16 = mybir.dt.bfloat16
I32 = mybir.dt.int32
U8 = mybir.dt.uint8
AF = mybir.ActivationFunctionType
ALU = mybir.AluOpType
AXX = mybir.AxisListType.X

D = 1024
T = 2048
B = 2
NP = 8
KC = 8
QB = 4
LN2C = float(np.log(2.0))


class Sem:
    def __init__(self, name):
        self.name, self.n, self.h = name, 0, None


class K:
    def __init__(self):
        self.q = {"t": [], "v": [], "s": [], "g": [], "y": []}
        self.sems = []
        self.nbar = 0

    def sem(self, name):
        s = Sem(name)
        self.sems.append(s)
        return s

    def op(self, eng, fn, inc=None, drain=False):
        def run(e):
            ins = fn(e)
            if inc is not None:
                ins.then_inc(inc[0].h, inc[1])
            if drain:
                e.drain()
        self.q[eng].append(run)
        if inc is not None:
            inc[0].n += inc[1]

    def wait(self, eng, s, val):
        self.q[eng].append(lambda e, s=s, val=val: e.wait_ge(s.h, val))

    def inc(self, eng, s, kk=1):
        s.n += kk
        self.q[eng].append(lambda e, s=s, kk=kk: e.sem_inc(s.h, kk))

    def barrier(self, bar):
        self.nbar += 1
        tgt = 5 * self.nbar
        for eng in ("t", "v", "s", "g", "y"):
            self.inc(eng, bar)
            self.wait(eng, bar, tgt)

    def t(self, fn, inc=None):
        self.op("t", fn, inc)

    def v(self, fn, inc=None):
        self.op("v", fn, inc, drain=True)

    def s(self, fn, inc=None):
        self.op("s", fn, inc, drain=True)

    def load(self, dst_fn, src_fn, sem):
        prev = sem.n
        def run(e, d=dst_fn, s=src_fn, m=sem, prev=prev):
            e.wait_ge(m.h, prev)
            e.dma_start(out=d(), in_=s()).then_inc(m.h, 16)
        self.q["y"].append(run)
        sem.n += 16
        return sem.n

    def store(self, dst_fn, src_fn, sem):
        prev = sem.n
        def run(e, d=dst_fn, s=src_fn, m=sem, prev=prev):
            e.wait_ge(m.h, prev)
            e.dma_start(out=d(), in_=s()).then_inc(m.h, 16)
        self.q["g"].append(run)
        sem.n += 16
        return sem.n


def build_program():
    nc = bass.Bass("TRN2", target_bir_lowering=False)
    ap = {}

    def din(name, shape, dt):
        ap[name] = nc.dram_tensor(name, list(shape), dt, kind="ExternalInput").ap()

    din("x_in", (B * T, D), F32)
    din("w_qkv", (D, 3 * D), F32)
    din("w_out", (D, D), F32)
    din("b_out", (1, D), F32)
    din("g1r", (1, D), F32)
    din("b1r", (1, D), F32)
    din("w1", (4 * D, D), F32)
    din("bh1", (1, 4 * D), F32)
    din("g2r", (1, 4 * D), F32)
    din("b2r", (1, 4 * D), F32)
    din("w2", (D, 4 * D), F32)
    din("bh2", (1, D), F32)
    din("c0", (B, D), F32)
    din("wmod0", (D, 6 * D), F32)
    din("Wq", (D, D), F32)
    din("bq", (D, 1), F32)
    din("Wk", (D, D), F32)
    din("bk", (D, 1), F32)
    din("Wv", (D, D), F32)
    din("bv", (D, 1), F32)
    din("Wo", (D, D), F32)
    din("bo", (1, D), F32)
    din("slopes", (1, 16), F32)
    din("w8", (D, D), U8)
    din("wsc", (1, 1), F32)
    din("bf8", (1, D), F32)
    din("c1", (B, D), F32)
    din("wmod1", (D, 6 * D), F32)
    din("identf", (128, 128), F32)
    din("identb", (128, 128), BF16)
    din("maskc", (128, 4 * 512), BF16)
    din("ramp", (128, 16), F32)

    y_out = nc.dram_tensor("y", [B * T, D], F32, kind="ExternalOutput").ap()

    def scr(name, shape, dt):
        ap[name] = nc.dram_tensor(name, list(shape), dt, kind="Internal").ap()

    scr("xs", (B * T, D), F32)
    scr("hfm", (D, T), F32)
    scr("ctx", (D, T), F32)
    scr("wqf", (D, D), F32)
    scr("wkf", (D, D), F32)
    scr("wvf", (D, D), F32)
    scr("wof", (D, D), F32)
    scr("w1q", (D, 4 * D), BF16)
    scr("w2q", (4 * D, D), BF16)
    scr("wf8", (D, D), BF16)
    scr("mods", (2 * B, 6 * D), F32)

    k = K()
    lds = [k.sem(f"ld{i}") for i in range(8)]
    sts = [k.sem(f"st{i}") for i in range(4)]
    idx = {"l": 0, "s": 0}
    sT, sV, sS, BAR = k.sem("sT"), k.sem("sV"), k.sem("sS"), k.sem("bar")

    def next_ld():
        idx["l"] = (idx["l"] + 1) % 8
        return lds[idx["l"]]

    def next_st():
        idx["s"] = (idx["s"] + 1) % 4
        return sts[idx["s"]]

    from contextlib import ExitStack
    es = ExitStack()

    def sb(name, shape, dt):
        return es.enter_context(nc.sbuf_tensor(name, list(shape), dt))

    FP = sb("FP", [128, 13824], F32)
    W1Q = sb("W1Q", [128, 32768], BF16)
    W2Q = sb("W2Q", [128, 4096], BF16)
    XQb = sb("XQb", [128, 10240], BF16)
    BCP = sb("BCP", [128, 12288], BF16)
    CB = sb("CBc", [128, 2304], BF16)
    CF = sb("CFc", [128, 160], F32)
    SV = sb("SVV", [128, 640], F32)
    OO = sb("OOr", [1, 1664], F32)
    OB = sb("OBr", [1, 5248], BF16)
    U8B = sb("U8B", [128, 1024], U8)
    IT32 = sb("IT32", [128, 1024], I32)
    ONB = sb("ONB", [1, 128], BF16)
    CST = sb("CST", [128, 16], F32)
    LNV = sb("LNV", [128, 512], F32)
    ONES128 = sb("ONES128", [128, 128], F32)
    PS = [es.enter_context(nc.psum_tensor(f"PS{i}", [128, 512], F32)) for i in range(7)]
    PSB = es.enter_context(nc.psum_tensor("PSB", [128, 1024], BF16))
    for s_ in k.sems:
        s_.h = es.enter_context(nc.semaphore(s_.name))

    cvals = [-7.0 * LN2C, LN2C, -3.0 * LN2C, 1e-5, 1.0 / 1024.0, 1.0 / 4096.0,
             0.125, -1.0, 1.0, 0.0, 0.5, 60.0]
    for ci_, val in enumerate(cvals):
        nc.const_aps.register(F32, val, CST[:, ci_:ci_ + 1])

    identf = CF[:, 0:128]
    rampc = CF[:, 128:144]
    identb = CB[:, 0:128]
    maskb = CB[:, 128:2176]
    onesf = OO[0:1, 0:128]
    onescol = SV[:, 15:16]
    ALB0 = 16

    def bcast_f32(dst, row_ap, width, bias=0.0, scale=1.0):
        for i in range((width + 511) // 512):
            w = min(512, width - 512 * i)
            k.t(lambda t, i=i, w=w, row_ap=row_ap: t.matmul(PS[6][:, 0:w], onesf, row_ap[0:1, 512 * i:512 * i + w], start=True, stop=True), inc=(sT, 1))
            k.wait("s", sT, sT.n)
            k.s(lambda s, i=i, w=w, dst=dst: s.activation(out=dst[:, 512 * i:512 * i + w], in_=PS[6][:, 0:w], func=AF.Copy, bias=bias, scale=scale), inc=(sS, 1))
            k.wait("t", sS, sS.n)
            k.wait("y", sT, sT.n)

    def bcast_bf16(dst, row_bf, width):
        for i in range((width + 511) // 512):
            w = min(512, width - 512 * i)
            k.t(lambda t, i=i, w=w, row_bf=row_bf: t.matmul(PS[6][:, 0:w], ONB[0:1, :], row_bf[0:1, 512 * i:512 * i + w], start=True, stop=True), inc=(sT, 1))
            k.wait("s", sT, sT.n)
            k.s(lambda s, i=i, w=w, dst=dst: s.activation(out=dst[:, 512 * i:512 * i + w], in_=PS[6][:, 0:w], func=AF.Copy), inc=(sS, 1))
            k.wait("t", sS, sS.n)
            k.wait("y", sT, sT.n)

    def vfloor(dst, src, width, gbuf):
        k.v(lambda v: v.tensor_copy(out=IT32[:, 0:width], in_=src))
        k.v(lambda v: v.tensor_copy(out=gbuf[:, 0:width], in_=IT32[:, 0:width]))
        k.v(lambda v: v.tensor_tensor(out=dst, in0=gbuf[:, 0:width], in1=src, op=ALU.is_gt))
        k.v(lambda v: v.tensor_tensor(out=dst, in0=gbuf[:, 0:width], in1=dst, op=ALU.subtract))

    L = next_ld()
    k.load(lambda: identf, lambda: ap["identf"], L)
    k.load(lambda: identb, lambda: ap["identb"], L)
    k.load(lambda: maskb, lambda: ap["maskc"], L)
    k.load(lambda: rampc, lambda: ap["ramp"], L)
    k.load(lambda: OO[0:1, 128:144], lambda: ap["slopes"], L)
    k.load(lambda: OO[0:1, 144:145], lambda: ap["wsc"], L)
    tA = L.n
    for eng in ("t", "v", "s"):
        k.wait(eng, L, tA)
    k.v(lambda v: v.memset(onesf, 1.0))
    k.v(lambda v: v.memset(ONES128[:], 1.0))
    for ci_, val in enumerate(cvals):
        k.v(lambda v, ci_=ci_, val=val: v.memset(CST[:, ci_:ci_ + 1], val))
    k.v(lambda v: v.memset(ONB[:], 1.0))
    k.v(lambda v: v.memset(onescol, 1.0))
    k.v(lambda v: v.memset(SV[:, 13:15], 0.0))
    k.v(lambda v: v.memset(SV[0:64, 13:14], 1.0))
    k.v(lambda v: v.memset(SV[64:128, 14:15], 1.0), inc=(sV, 1))
    k.wait("t", sV, sV.n)
    k.barrier(BAR)

    cw = FP[:, 0:1024]
    cts = FP[:, 1024:1024 + B * KC]
    wt0 = FP[:, 2048:2560]
    wt1 = FP[:, 2560:3072]
    mst = FP[:, 3072:3584]
    mst2 = FP[:, 3584:4096]
    for lay in range(2):
        L = next_ld()
        k.wait("y", sT, sT.n)
        tw = k.load(lambda: cw[0:B, :], lambda lay=lay: ap["c0" if lay == 0 else "c1"], L)
        k.wait("s", L, tw)
        k.s(lambda s: s.activation(out=cw[0:B, :], in_=cw[0:B, :], func=AF.Silu), inc=(sS, 1))
        k.wait("t", sS, sS.n)
        for c in range(KC):
            k.t(lambda t, c=c: t.transpose(PS[6][:, 0:B], cw[0:B, 128 * c:128 * (c + 1)], identf[0:B, 0:B]), inc=(sT, 1))
            k.wait("v", sT, sT.n)
            k.v(lambda v, c=c: v.tensor_copy(out=cts[:, B * c:B * (c + 1)], in_=PS[6][:, 0:B]), inc=(sV, 1))
            k.wait("t", sV, sV.n)
        for blk in range(12):
            wm = "wmod0" if lay == 0 else "wmod1"
            for c in range(KC):
                Lw = next_ld()
                dst = wt0 if c % 2 == 0 else wt1
                tw = k.load(lambda dst=dst: dst, lambda c=c, blk=blk, wm=wm: ap[wm][128 * c:128 * (c + 1), 512 * blk:512 * (blk + 1)], Lw)
                k.wait("t", Lw, tw)
                k.t(lambda t, c=c, dst=dst: t.matmul(PS[6][0:B, 0:512], cts[:, B * c:B * (c + 1)], dst, start=(c == 0), stop=(c == KC - 1)), inc=(sT, 1))
                if c < KC - 1:
                    k.wait("y", sT, sT.n)
            k.wait("v", sT, sT.n)
            j = blk // 2
            if j in (2, 5):
                k.v(lambda v: v.tensor_scalar(out=mst[0:B, :], in0=PS[6][0:B, 0:512], scalar1=0.0, scalar2=None, op0=ALU.is_ge))
                k.v(lambda v: v.tensor_scalar(out=mst[0:B, :], in0=mst[0:B, :], scalar1=2.0, scalar2=-1.0, op0=ALU.mult, op1=ALU.add))
                k.v(lambda v: v.tensor_scalar(out=mst2[0:B, :], in0=PS[6][0:B, 0:512], scalar1=-1.0, scalar2=None, op0=ALU.mult))
                k.v(lambda v: v.tensor_tensor(out=mst2[0:B, :], in0=mst2[0:B, :], in1=PS[6][0:B, 0:512], op=ALU.max))
                k.v(lambda v: v.tensor_scalar(out=mst2[0:B, :], in0=mst2[0:B, :], scalar1=0.25, scalar2=None, op0=ALU.max))
                k.v(lambda v: v.tensor_tensor(out=mst[0:B, :], in0=mst[0:B, :], in1=mst2[0:B, :], op=ALU.mult), inc=(sV, 1))
            else:
                k.v(lambda v: v.tensor_copy(out=mst[0:B, :], in_=PS[6][0:B, 0:512]), inc=(sV, 1))
            k.wait("t", sV, sV.n)
            k.wait("g", sV, sV.n)
            S = next_st()
            ts = k.store(lambda blk=blk, lay=lay: ap["mods"][B * lay:B * (lay + 1), 512 * blk:512 * (blk + 1)], lambda: mst[0:B, :], S)
            k.wait("v", S, ts)
    k.barrier(BAR)

    wload = FP[:, 0:1024]
    wy = FP[:, 1024:2048]
    wg = FP[:, 2048:3072]
    wc = FP[:, 3072:4096]
    qb = XQb[:, 0:1024]
    tstage = XQb[:, 1024:1152]
    fstage = FP[:, 4096:4224]

    def wquant(wname, dst, nchunk, width, n_elem):
        k.v(lambda v: v.memset(SV[:, 3:4], 0.0), inc=(sV, 1))
        nsub = width // 1024
        for ch in range(nchunk):
            for sbk in range(nsub):
                L = next_ld()
                tw = k.load(lambda: wload, lambda ch=ch, sbk=sbk, wname=wname: ap[wname][128 * ch:128 * (ch + 1), 1024 * sbk:1024 * (sbk + 1)], L)
                k.wait("v", L, tw)
                k.v(lambda v: v.tensor_reduce(out=SV[:, 4:5], in_=wload, axis=AXX, op=ALU.add, apply_absolute_value=True))
                k.v(lambda v: v.tensor_tensor(out=SV[:, 3:4], in0=SV[:, 3:4], in1=SV[:, 4:5], op=ALU.add), inc=(sV, 1))
                k.wait("y", sV, sV.n)
        k.wait("t", sV, sV.n)
        k.t(lambda t: t.matmul(PS[6][0:1, 0:1], onescol, SV[:, 3:4], start=True, stop=True), inc=(sT, 1))
        k.wait("v", sT, sT.n)
        k.v(lambda v: v.tensor_scalar(out=OO[0:1, 150:151], in0=PS[6][0:1, 0:1], scalar1=1.0 / n_elem, scalar2=1e-5, op0=ALU.mult, op1=ALU.max))
        k.v(lambda v: v.reciprocal(out=OO[0:1, 151:152], in_=OO[0:1, 150:151]), inc=(sV, 1))
        k.wait("t", sV, sV.n)
        k.t(lambda t: t.matmul(PS[6][:, 0:1], onesf, OO[0:1, 151:152], start=True, stop=True), inc=(sT, 1))
        k.wait("v", sT, sT.n)
        k.v(lambda v: v.tensor_copy(out=SV[:, 8:9], in_=PS[6][:, 0:1]), inc=(sV, 1))
        k.wait("t", sV, sV.n)
        for ch in range(nchunk):
            for sbk in range(nsub):
                L = next_ld()
                tw = k.load(lambda: wload, lambda ch=ch, sbk=sbk, wname=wname: ap[wname][128 * ch:128 * (ch + 1), 1024 * sbk:1024 * (sbk + 1)], L)
                k.wait("v", L, tw)
                k.wait("v", sT, sT.n)
                k.v(lambda v: v.tensor_scalar(out=wy, in0=wload, scalar1=SV[:, 8:9], scalar2=0.5, op0=ALU.mult, op1=ALU.add))
                vfloor(wy, wy, 1024, wg)
                k.v(lambda v: v.tensor_scalar(out=qb, in0=wy, scalar1=-1.0, scalar2=1.0, op0=ALU.max, op1=ALU.min), inc=(sV, 1))
                k.wait("t", sV, sV.n)
                k.wait("y", sV, sV.n)
                for c in range(8):
                    k.t(lambda t, c=c: t.transpose(PSB[:, 0:128], qb[:, 128 * c:128 * (c + 1)], identb), inc=(sT, 1))
                    k.wait("s", sT, sT.n)
                    k.s(lambda s: s.activation(out=tstage, in_=PSB[:, 0:128], func=AF.Copy), inc=(sS, 1))
                    k.wait("t", sS, sS.n)
                    k.wait("g", sS, sS.n)
                    S = next_st()
                    ts = k.store(lambda ch=ch, sbk=sbk, c=c, dst=dst: dst[1024 * sbk + 128 * c:1024 * sbk + 128 * (c + 1), 128 * ch:128 * (ch + 1)], lambda: tstage, S)
                    k.wait("s", S, ts)

    wquant("w1", ap["w1q"], 32, 1024, 4096.0 * 1024.0)
    wquant("w2", ap["w2q"], 8, 4096, 1024.0 * 4096.0)

    for wname, dname in (("Wq", "wqf"), ("Wk", "wkf"), ("Wv", "wvf"), ("Wo", "wof")):
        for ch in range(KC):
            L = next_ld()
            tw = k.load(lambda: wload, lambda ch=ch, wname=wname: ap[wname][128 * ch:128 * (ch + 1), 0:1024], L)
            k.wait("t", L, tw)
            for c in range(KC):
                k.t(lambda t, c=c: t.transpose(PS[6][:, 0:128], wload[:, 128 * c:128 * (c + 1)], identf), inc=(sT, 1))
                k.wait("v", sT, sT.n)
                k.v(lambda v: v.tensor_copy(out=fstage, in_=PS[6][:, 0:128]), inc=(sV, 1))
                k.wait("t", sV, sV.n)
                k.wait("y", sV, sV.n)
                k.wait("g", sV, sV.n)
                S = next_st()
                ts = k.store(lambda ch=ch, c=c, dname=dname: ap[dname][128 * c:128 * (c + 1), 128 * ch:128 * (ch + 1)], lambda: fstage, S)
                k.wait("v", S, ts)

    for ch in range(KC):
        L = next_ld()
        tw = k.load(lambda: U8B[:, 0:1024], lambda ch=ch: ap["w8"][128 * ch:128 * (ch + 1), :], L)
        k.wait("v", L, tw)
        k.wait("v", sT, sT.n)
        u, sg, e_, m_ = wload, wy, wg, wc
        k.v(lambda v: v.tensor_copy(out=u, in_=U8B[:, 0:1024]))
        k.v(lambda v: v.tensor_scalar(out=sg, in0=u, scalar1=128.0, scalar2=None, op0=ALU.is_ge))
        k.v(lambda v: v.scalar_tensor_tensor(out=u, in0=sg, scalar=-128.0, in1=u, op0=ALU.mult, op1=ALU.add))
        k.v(lambda v: v.tensor_scalar(out=e_, in0=u, scalar1=0.125, scalar2=None, op0=ALU.mult))
        vfloor(e_, e_, 1024, m_)
        k.v(lambda v: v.scalar_tensor_tensor(out=m_, in0=e_, scalar=-8.0, in1=u, op0=ALU.mult, op1=ALU.add))
        k.v(lambda v: v.tensor_scalar(out=u, in0=e_, scalar1=1.0, scalar2=None, op0=ALU.is_ge))
        k.v(lambda v: v.scalar_tensor_tensor(out=m_, in0=m_, scalar=0.125, in1=u, op0=ALU.mult, op1=ALU.add))
        k.v(lambda v: v.tensor_scalar(out=e_, in0=e_, scalar1=1.0, scalar2=None, op0=ALU.max), inc=(sV, 1))
        k.wait("s", sV, sV.n)
        k.s(lambda s: s.activation(out=e_, in_=e_, func=AF.Exp, bias=-7.0 * LN2C, scale=LN2C), inc=(sS, 1))
        k.wait("v", sS, sS.n)
        k.v(lambda v: v.tensor_scalar(out=sg, in0=sg, scalar1=-2.0, scalar2=1.0, op0=ALU.mult, op1=ALU.add))
        k.v(lambda v: v.tensor_tensor(out=m_, in0=m_, in1=e_, op=ALU.mult))
        k.v(lambda v: v.tensor_tensor(out=qb, in0=m_, in1=sg, op=ALU.mult), inc=(sV, 1))
        k.wait("t", sV, sV.n)
        k.wait("y", sV, sV.n)
        for c in range(KC):
            k.t(lambda t, c=c: t.transpose(PSB[:, 0:128], qb[:, 128 * c:128 * (c + 1)], identb), inc=(sT, 1))
            k.wait("s", sT, sT.n)
            k.s(lambda s: s.activation(out=tstage, in_=PSB[:, 0:128], func=AF.Copy), inc=(sS, 1))
            k.wait("t", sS, sS.n)
            k.wait("g", sS, sS.n)
            S = next_st()
            ts = k.store(lambda ch=ch, c=c: ap["wf8"][128 * c:128 * (c + 1), 128 * ch:128 * (ch + 1)], lambda: tstage, S)
            k.wait("s", S, ts)

    for h in range(16):
        k.t(lambda t, h=h: t.matmul(PS[6][:, 0:1], onesf, OO[0:1, 128 + h:129 + h], start=True, stop=True), inc=(sT, 1))
        k.wait("v", sT, sT.n)
        k.v(lambda v: v.tensor_copy(out=SV[:, 9:10], in_=PS[6][:, 0:1]), inc=(sV, 1))
        k.wait("t", sV, sV.n)
        k.v(lambda v, h=h: v.tensor_scalar(out=SV[:, ALB0 + 16 * h:ALB0 + 16 * (h + 1)], in0=rampc, scalar1=SV[:, 9:10], scalar2=None, op0=ALU.mult), inc=(sV, 1))
    k.barrier(BAR)

    def ln_mod_phase(lay, b, half):
        xt = FP[:, 0:1024]
        ht = FP[:, 1024:2048]
        sq = FP[:, 2048:3072]
        bc1 = FP[:, 3072:4096]
        bc2 = FP[:, 4096:5120]
        hs2 = FP[:, 5120:5632]
        j0 = 1 if half == 0 else 4
        Lr = next_ld()
        t2 = k.load(lambda: OO[0:1, 256:256 + 1024], lambda: ap["mods"][B * lay + b:B * lay + b + 1, D * j0:D * (j0 + 1)], Lr)
        k.wait("t", Lr, t2)
        bcast_f32(bc1, OO[0:1, 256:256 + 1024], 1024, bias=1.0)
        Lr = next_ld()
        t2 = k.load(lambda: OO[0:1, 256:256 + 1024], lambda: ap["mods"][B * lay + b:B * lay + b + 1, D * (j0 - 1):D * j0], Lr)
        k.wait("t", Lr, t2)
        bcast_f32(bc2, OO[0:1, 256:256 + 1024], 1024)
        src = ap["x_in"] if (lay == 0 and half == 0) else ap["xs"]
        for tt in range(16):
            row = T * b + 128 * tt
            L = next_ld()
            k.wait("y", sV, sV.n)
            tw = k.load(lambda: xt, lambda row=row, src=src: src[row:row + 128, :], L)
            k.wait("v", L, tw)
            k.v(lambda v: v.tensor_reduce(out=SV[:, 0:1], in_=xt, axis=AXX, op=ALU.add))
            k.v(lambda v: v.tensor_scalar(out=SV[:, 0:1], in0=SV[:, 0:1], scalar1=1.0 / D, scalar2=None, op0=ALU.mult))
            k.v(lambda v: v.tensor_scalar(out=ht, in0=xt, scalar1=SV[:, 0:1], scalar2=None, op0=ALU.subtract), inc=(sV, 1))
            k.wait("s", sV, sV.n)
            k.s(lambda s: s.activation(out=sq, in_=ht, func=AF.Square), inc=(sS, 1))
            k.wait("v", sS, sS.n)
            k.v(lambda v: v.tensor_reduce(out=SV[:, 1:2], in_=sq, axis=AXX, op=ALU.add), inc=(sV, 1))
            k.wait("s", sV, sV.n)
            k.s(lambda s: s.activation(out=SV[:, 1:2], in_=SV[:, 1:2], func=AF.Sqrt, bias=1e-5, scale=1.0 / D), inc=(sS, 1))
            k.wait("v", sS, sS.n)
            k.v(lambda v: v.reciprocal(out=SV[:, 2:3], in_=SV[:, 1:2]))
            k.v(lambda v: v.tensor_scalar(out=ht, in0=ht, scalar1=SV[:, 2:3], scalar2=None, op0=ALU.mult))
            k.v(lambda v: v.tensor_tensor(out=ht, in0=ht, in1=bc1, op=ALU.mult))
            k.v(lambda v: v.tensor_tensor(out=ht, in0=ht, in1=bc2, op=ALU.add), inc=(sV, 1))
            k.wait("t", sV, sV.n)
            for g4 in range(2):
                for cc in range(4):
                    c = 4 * g4 + cc
                    k.t(lambda t, c=c, cc=cc: t.transpose(PS[6][:, 128 * cc:128 * (cc + 1)], ht[:, 128 * c:128 * (c + 1)], identf, ),
                        inc=(sT, 1) if cc == 3 else None)
                k.wait("v", sT, sT.n)
                k.v(lambda v: v.tensor_copy(out=hs2, in_=PS[6][:, 0:512]), inc=(sV, 1))
                k.wait("t", sV, sV.n)
                k.wait("g", sV, sV.n)
                S = next_st()
                ts = k.store(lambda g4=g4, tt=tt: ap["hfm"].rearrange("(c p) n -> p c n", p=128)[:, 4 * g4:4 * g4 + 4, 128 * tt:128 * (tt + 1)],
                             lambda: hs2.rearrange("p (c n) -> p c n", n=128), S)
                k.wait("v", S, ts)
        k.barrier(BAR)

    def attn_phase(lay, b):
        QP = FP[:, 0:2048]
        KP = FP[:, 2048:4096]
        HST = FP[:, 4096:8192]
        VP_A = FP[:, 8192:9232]
        VP_B = FP[:, 9232:10272]
        VS = FP[:, 10272:10784]
        PPa = FP[:, 10784:11296]
        PPb = FP[:, 11296:11808]
        WSL = FP[:, 11808:12192]
        for pair in range(NP):
            if lay == 1:
                Lb = next_ld()
                k.wait("y", sV, sV.n)
                k.load(lambda: SV[:, 10:11], lambda pair=pair: ap["bq"][128 * pair:128 * (pair + 1), :], Lb)
                k.load(lambda: SV[:, 11:12], lambda pair=pair: ap["bk"][128 * pair:128 * (pair + 1), :], Lb)
                tb = k.load(lambda: SV[:, 12:13], lambda pair=pair: ap["bv"][128 * pair:128 * (pair + 1), :], Lb)
                k.wait("v", Lb, tb)
                k.wait("s", Lb, tb)
            WQS = FP[:, 12192:13216]
            KQS = FP[:, 10784:11808]
            for n in range(QB):
                Lh = next_ld()
                tws = k.load(lambda: HST.rearrange("p (c n) -> p c n", n=512),
                             lambda n=n: ap["hfm"].rearrange("(c p) n -> p c n", p=128)[:, :, 512 * n:512 * (n + 1)], Lh)
                Lw = next_ld()
                k.wait("y", sS, sS.n)
                k.wait("y", sT, sT.n)
                wq_src = "w_qkv" if lay == 0 else "wqf"
                wk_src = "w_qkv" if lay == 0 else "wkf"
                wv_src = "w_qkv" if lay == 0 else "wvf"
                qo = 128 * pair
                ko = D + 128 * pair if lay == 0 else 128 * pair
                vo = 2 * D + 128 * pair if lay == 0 else 128 * pair
                k.load(lambda: WQS.rearrange("p (c n) -> p c n", n=128),
                       lambda qo=qo, wq_src=wq_src: ap[wq_src].rearrange("(c p) n -> p c n", p=128)[:, :, qo:qo + 128], Lw)
                tww = k.load(lambda: KQS.rearrange("p (c n) -> p c n", n=128),
                             lambda ko=ko, wk_src=wk_src: ap[wk_src].rearrange("(c p) n -> p c n", p=128)[:, :, ko:ko + 128], Lw)
                k.wait("t", Lw, tww)
                k.wait("t", Lh, tws)
                for c in range(KC):
                    h_ap = HST[:, 512 * c:512 * (c + 1)]
                    k.t(lambda t, h_ap=h_ap, c=c: t.matmul(PS[0][:, 0:512], WQS[:, 128 * c:128 * (c + 1)], h_ap, start=(c == 0), stop=(c == KC - 1)))
                    k.t(lambda t, h_ap=h_ap, c=c: t.matmul(PS[1][:, 0:512], KQS[:, 128 * c:128 * (c + 1)], h_ap, start=(c == 0), stop=(c == KC - 1)), inc=(sT, 1))
                k.wait("y", sT, sT.n)
                Lv = next_ld()
                tww = k.load(lambda: WQS.rearrange("p (c n) -> p c n", n=128),
                             lambda vo=vo, wv_src=wv_src: ap[wv_src].rearrange("(c p) n -> p c n", p=128)[:, :, vo:vo + 128], Lv)
                k.wait("t", Lv, tww)
                for c in range(KC):
                    h_ap = HST[:, 512 * c:512 * (c + 1)]
                    k.t(lambda t, h_ap=h_ap, c=c: t.matmul(PS[2][:, 0:512], WQS[:, 128 * c:128 * (c + 1)], h_ap, start=(c == 0), stop=(c == KC - 1)), inc=(sT, 1))
                k.wait("y", sT, sT.n)
                k.wait("v", sT, sT.n)
                k.wait("v", sS, sS.n)
                k.wait("s", sT, sT.n)
                if lay == 0:
                    k.v(lambda v, n=n: v.tensor_copy(out=QP[:, 512 * n:512 * (n + 1)], in_=PS[0][:, 0:512]), inc=(sV, 1))
                    k.s(lambda s, n=n: s.activation(out=KP[:, 512 * n:512 * (n + 1)], in_=PS[1][:, 0:512], func=AF.Copy, scale=0.125), inc=(sS, 1))
                    k.v(lambda v: v.tensor_copy(out=VS, in_=PS[2][:, 0:512]), inc=(sV, 1))
                else:
                    k.v(lambda v, n=n: v.tensor_scalar(out=QP[:, 512 * n:512 * (n + 1)], in0=PS[0][:, 0:512], scalar1=SV[:, 10:11], scalar2=None, op0=ALU.add), inc=(sV, 1))
                    k.v(lambda v, n=n: v.tensor_scalar(out=KP[:, 512 * n:512 * (n + 1)], in0=PS[1][:, 0:512], scalar1=SV[:, 11:12], scalar2=0.125, op0=ALU.add, op1=ALU.mult))
                    k.inc("s", sS)
                    k.v(lambda v: v.tensor_scalar(out=VS, in0=PS[2][:, 0:512], scalar1=SV[:, 12:13], scalar2=None, op0=ALU.add), inc=(sV, 1))
                k.wait("t", sV, sV.n)
                k.wait("t", sS, sS.n)
                for st4 in range(4):
                    k.t(lambda t, st4=st4: t.transpose(PS[3][:, 0:128], VS[:, 128 * st4:128 * (st4 + 1)], identf), inc=(sT, 1))
                    k.wait("v", sT, sT.n)
                    ci = 4 * n + st4
                    k.v(lambda v, ci=ci: v.tensor_copy(out=VP_A[:, 65 * ci:65 * ci + 64], in_=PS[3][:, 0:64]))
                    k.v(lambda v, ci=ci: v.tensor_copy(out=VP_B[:, 65 * ci:65 * ci + 64], in_=PS[3][:, 64:128]), inc=(sV, 1))
                    k.wait("t", sV, sV.n)
            for ci_ in range(16):
                k.v(lambda v, ci_=ci_: v.memset(VP_A[:, 65 * ci_ + 64:65 * ci_ + 65], 1.0))
                k.v(lambda v, ci_=ci_: v.memset(VP_B[:, 65 * ci_ + 64:65 * ci_ + 65], 1.0))
            k.inc("v", sV)
            k.wait("t", sV, sV.n)
            k.wait("s", sV, sV.n)
            for src, oi in ((QP, 0), (KP, 2)):
                for blk in range(4):
                    k.s(lambda s, blk=blk, src=src: s.activation(out=VS, in_=src[:, 512 * blk:512 * (blk + 1)], func=AF.Square), inc=(sS, 1))
                    k.wait("t", sS, sS.n)
                    for hh in range(2):
                        k.t(lambda t, hh=hh: t.matmul(PS[3][0:1, 0:512], SV[:, 13 + hh:14 + hh], VS, start=True, stop=True), inc=(sT, 1))
                        k.wait("v", sT, sT.n)
                        if blk == 0:
                            k.v(lambda v, oi=oi, hh=hh: v.tensor_reduce(out=OO[0:1, 160 + oi + hh:161 + oi + hh], in_=PS[3][0:1, 0:512], axis=AXX, op=ALU.max), inc=(sV, 1))
                        else:
                            k.v(lambda v: v.tensor_reduce(out=OO[0:1, 166:167], in_=PS[3][0:1, 0:512], axis=AXX, op=ALU.max))
                            k.v(lambda v, oi=oi, hh=hh: v.tensor_tensor(out=OO[0:1, 160 + oi + hh:161 + oi + hh], in0=OO[0:1, 160 + oi + hh:161 + oi + hh], in1=OO[0:1, 166:167], op=ALU.max), inc=(sV, 1))
                        k.wait("t", sV, sV.n)
                    k.wait("s", sV, sV.n)
            k.wait("s", sV, sV.n)
            k.s(lambda s: s.activation(out=OO[0:1, 160:164], in_=OO[0:1, 160:164], func=AF.Sqrt), inc=(sS, 1))
            k.wait("v", sS, sS.n)
            k.v(lambda v: v.tensor_tensor(out=OO[0:1, 164:165], in0=OO[0:1, 160:161], in1=OO[0:1, 162:163], op=ALU.mult))
            k.v(lambda v: v.tensor_tensor(out=OO[0:1, 165:166], in0=OO[0:1, 161:162], in1=OO[0:1, 163:164], op=ALU.mult), inc=(sV, 1))
            k.wait("t", sV, sV.n)
            for hh in range(2):
                k.t(lambda t, hh=hh: t.matmul(PS[3][:, 0:1], onesf, OO[0:1, 164 + hh:165 + hh], start=True, stop=True), inc=(sT, 1))
                k.wait("s", sT, sT.n)
                k.s(lambda s, hh=hh: s.activation(out=SV[:, 6 + hh:7 + hh], in_=PS[3][:, 0:1], func=AF.Copy, bias=60.0, scale=-1.0), inc=(sS, 1))
                k.wait("t", sS, sS.n)
                k.wait("v", sS, sS.n)
            if lay == 1:
                for hh in range(2):
                    hid = 2 * pair + hh
                    k.v(lambda v, hh=hh, hid=hid: v.tensor_scalar(out=SV[:, 272 + 16 * hh:288 + 16 * hh], in0=SV[:, ALB0 + 16 * hid:ALB0 + 16 * (hid + 1)], scalar1=SV[:, 6 + hh:7 + hh], scalar2=None, op0=ALU.add), inc=(sV, 1))
                k.wait("s", sV, sV.n)
            for n in range(QB):
                clist = list(range(16)) if lay == 1 else list(range(4 * n + 4))
                k.wait("t", sV, sV.n)
                for ci, c in enumerate(clist):
                    psA = PS[0] if (c % 2 == 0) else PS[1]
                    psB = PS[4] if (c % 2 == 0) else PS[5]
                    crossing = (lay == 0) and (c >= 4 * n)
                    k.t(lambda t, c=c, n=n, psA=psA, crossing=crossing: t.matmul(psA[:, 0:512], KP[0:64, 128 * c:128 * (c + 1)], QP[0:64, 512 * n:512 * (n + 1)], start=True, stop=not crossing))
                    k.t(lambda t, c=c, n=n, psB=psB, crossing=crossing: t.matmul(psB[:, 0:512], KP[64:128, 128 * c:128 * (c + 1)], QP[64:128, 512 * n:512 * (n + 1)], start=True, stop=not crossing),
                        inc=None if crossing else (sT, 1))
                    if crossing:
                        m = c - 4 * n
                        k.t(lambda t, m=m, psA=psA: t.matmul(psA[:, 0:512], identb, maskb[:, 512 * m:512 * (m + 1)], start=False, stop=True))
                        k.t(lambda t, m=m, psB=psB: t.matmul(psB[:, 0:512], identb, maskb[:, 512 * m:512 * (m + 1)], start=False, stop=True), inc=(sT, 1))
                    k.wait("s", sT, sT.n)
                    if lay == 0:
                        biasA, biasB = SV[:, 6:7], SV[:, 7:8]
                    else:
                        biasA, biasB = SV[:, 272 + c:273 + c], SV[:, 288 + c:289 + c]
                    k.s(lambda s, psA=psA, biasA=biasA: s.activation(out=PPa, in_=psA[:, 0:512], func=AF.Exp, bias=biasA, scale=1.0))
                    k.s(lambda s, psB=psB, biasB=biasB: s.activation(out=PPb, in_=psB[:, 0:512], func=AF.Exp, bias=biasB, scale=1.0), inc=(sS, 1))
                    k.wait("t", sS, sS.n)
                    k.t(lambda t, c=c, ci=ci, last=(ci == len(clist) - 1): t.matmul(PS[2][0:65, 0:512], VP_A[:, 65 * c:65 * (c + 1)], PPa, start=(ci == 0), stop=last))
                    k.t(lambda t, c=c, ci=ci, last=(ci == len(clist) - 1): t.matmul(PS[3][0:65, 0:512], VP_B[:, 65 * c:65 * (c + 1)], PPb, start=(ci == 0), stop=last), inc=(sT, 1))
                    k.wait("s", sT, sT.n)
                k.wait("v", sT, sT.n)
                for hh, psx in ((0, PS[2]), (1, PS[3])):
                    k.v(lambda v, psx=psx: v.tensor_copy(out=LNV[64:65, 0:512], in_=psx[64:65, 0:512]), inc=(sV, 1))
                    k.v(lambda v: v.reciprocal(out=LNV[64:65, 0:512], in_=LNV[64:65, 0:512]), inc=(sV, 1))
                    k.wait("t", sV, sV.n)
                    k.t(lambda t: t.matmul(PS[6][:, 0:512], ONES128[64:65, 0:128], LNV[64:65, 0:512], start=True, stop=True), inc=(sT, 1))
                    k.wait("v", sT, sT.n)
                    k.v(lambda v: v.tensor_copy(out=LNV[0:64, 0:512], in_=PS[6][0:64, 0:512]), inc=(sV, 1))
                    k.v(lambda v, psx=psx: v.tensor_tensor(out=VS[0:64, 0:512], in0=psx[0:64, 0:512], in1=LNV[0:64, 0:512], op=ALU.mult), inc=(sV, 1))
                    k.wait("t", sV, sV.n)
                    k.wait("g", sV, sV.n)
                    S = next_st()
                    row = 128 * pair + 64 * hh
                    ts = k.store(lambda row=row, n=n: ap["ctx"][row:row + 64, 512 * n:512 * (n + 1)], lambda: VS[0:64, 0:512], S)
                    k.wait("v", S, ts)
        k.barrier(BAR)

    def outproj_phase(lay, b):
        OSb = FP[:, 0:4096]
        CXT = FP[:, 4096:8192]
        WS2 = FP[:, 8192:8448]
        XT2 = FP[:, 8448:9472]
        TT2 = FP[:, 9472:10496]
        BOC = FP[:, 10496:11520]
        GAC = FP[:, 11520:12544]
        Lb = next_ld()
        bname = "b_out" if lay == 0 else "bo"
        tb = k.load(lambda: OO[0:1, 256:256 + 1024], lambda bname=bname: ap[bname], Lb)
        k.wait("t", Lb, tb)
        bcast_f32(BOC, OO[0:1, 256:256 + 1024], 1024)
        Lr = next_ld()
        t2 = k.load(lambda: OO[0:1, 256:256 + 1024], lambda: ap["mods"][B * lay + b:B * lay + b + 1, 2 * D:3 * D], Lr)
        k.wait("t", Lr, t2)
        bcast_f32(GAC, OO[0:1, 256:256 + 1024], 1024)
        src_x = ap["x_in"] if lay == 0 else ap["xs"]
        wsrc = ap["w_out"] if lay == 0 else ap["wof"]
        WOS = FP[:, 12544:13568]
        for n in range(QB):
            Lc = next_ld()
            tcs = k.load(lambda: CXT.rearrange("p (c n) -> p c n", n=512),
                         lambda n=n: ap["ctx"].rearrange("(c p) n -> p c n", p=128)[:, :, 512 * n:512 * (n + 1)], Lc)
            k.wait("t", Lc, tcs)
            for m in range(KC):
                psm = PS[0] if m % 2 == 0 else PS[1]
                Lw = next_ld()
                k.wait("y", sT, sT.n)
                tww = k.load(lambda: WOS.rearrange("p (c n) -> p c n", n=128),
                             lambda m=m, wsrc=wsrc: wsrc.rearrange("(c p) n -> p c n", p=128)[:, :, 128 * m:128 * (m + 1)], Lw)
                k.wait("t", Lw, tww)
                for c in range(KC):
                    k.t(lambda t, c=c, psm=psm, m=m: t.matmul(psm[:, 0:512], WOS[:, 128 * c:128 * (c + 1)], CXT[:, 512 * c:512 * (c + 1)], start=(c == 0), stop=(c == KC - 1)), inc=(sT, 1))
                k.wait("v", sT, sT.n)
                k.v(lambda v, m=m, psm=psm: v.tensor_copy(out=OSb[:, 512 * m:512 * (m + 1)], in_=psm[:, 0:512]), inc=(sV, 1))
                k.wait("t", sV, sV.n)
            k.wait("y", sT, sT.n)
            for st4 in range(4):
                tt = 4 * n + st4
                row = T * b + 128 * tt
                Lx = next_ld()
                k.wait("y", sV, sV.n)
                tx = k.load(lambda: XT2, lambda row=row, src_x=src_x: src_x[row:row + 128, :], Lx)
                for m2 in range(2):
                    for cc in range(4):
                        mm = 4 * m2 + cc
                        k.t(lambda t, mm=mm, st4=st4, m2=m2, cc=cc: t.transpose(PS[2 + m2][:, 128 * cc:128 * (cc + 1)], OSb[:, 512 * mm + 128 * st4:512 * mm + 128 * (st4 + 1)], identf),
                            inc=(sT, 1) if cc == 3 else None)
                k.wait("v", sT, sT.n)
                k.wait("v", Lx, tx)
                k.v(lambda v: v.tensor_tensor(out=TT2[:, 0:512], in0=PS[2][:, 0:512], in1=BOC[:, 0:512], op=ALU.add))
                k.v(lambda v: v.tensor_tensor(out=TT2[:, 512:1024], in0=PS[3][:, 0:512], in1=BOC[:, 512:1024], op=ALU.add), inc=(sV, 1))
                k.wait("t", sV, sV.n)
                k.v(lambda v: v.tensor_tensor(out=TT2, in0=TT2, in1=GAC, op=ALU.mult))
                k.v(lambda v: v.tensor_tensor(out=XT2, in0=XT2, in1=TT2, op=ALU.add), inc=(sV, 1))
                k.wait("g", sV, sV.n)
                S = next_st()
                ts = k.store(lambda row=row: ap["xs"][row:row + 128, :], lambda: XT2, S)
                k.wait("v", S, ts)
                k.wait("y", S, ts)
        k.barrier(BAR)

    def ffn0_phase():
        xt = FP[:, 0:1024]
        zt = FP[:, 1024:2048]
        H1 = FP[:, 2048:6144]
        SQF = FP[:, 6144:10240]
        bc1 = FP[:, 10240:11264]
        bc2 = FP[:, 11264:12288]
        bcg = FP[:, 12288:13312]
        OS1 = FP[:, 13312:13824]
        g1b = BCP[:, 0:1024]
        b1b = BCP[:, 1024:2048]
        bh1b = BCP[:, 2048:6144]
        g2b = BCP[:, 6144:10240]
        b2b = BCP[:, 10240:11264]
        xq = XQb[:, 0:1024]
        xqf = XQb[:, 1024:2048]
        xq2 = XQb[:, 2048:6144]
        xq2f = XQb[:, 6144:10240]
        Lw = next_ld()
        tws = []
        for c in range(KC):
            tws.append(k.load(lambda c=c: W1Q[:, 4096 * c:4096 * (c + 1)], lambda c=c: ap["w1q"][128 * c:128 * (c + 1), :], Lw))
        for nmsrc, dst, wid in (("g1r", g1b, 1024), ("b1r", b1b, 1024)):
            Lb = next_ld()
            k.wait("y", sV, sV.n)
            tb = k.load(lambda: OO[0:1, 256:256 + 1024], lambda nmsrc=nmsrc: ap[nmsrc], Lb)
            k.wait("v", Lb, tb)
            k.v(lambda v: v.tensor_copy(out=OB[0:1, 0:1024], in_=OO[0:1, 256:256 + 1024]), inc=(sV, 1))
            k.wait("t", sV, sV.n)
            bcast_bf16(dst, OB[0:1, 0:1024], wid)
        for part in range(4):
            Lb = next_ld()
            k.wait("y", sV, sV.n)
            tb = k.load(lambda: OO[0:1, 256:256 + 1024], lambda part=part: ap["bh1"][0:1, 1024 * part:1024 * (part + 1)], Lb)
            k.wait("v", Lb, tb)
            k.v(lambda v: v.tensor_copy(out=OB[0:1, 0:1024], in_=OO[0:1, 256:256 + 1024]), inc=(sV, 1))
            k.wait("t", sV, sV.n)
            bcast_bf16(bh1b[:, 1024 * part:1024 * (part + 1)], OB[0:1, 0:1024], 1024)
        for part in range(4):
            Lb = next_ld()
            k.wait("y", sV, sV.n)
            tb = k.load(lambda: OO[0:1, 256:256 + 1024], lambda part=part: ap["g2r"][0:1, 1024 * part:1024 * (part + 1)], Lb)
            k.wait("v", Lb, tb)
            k.v(lambda v: v.tensor_copy(out=OB[0:1, 0:1024], in_=OO[0:1, 256:256 + 1024]), inc=(sV, 1))
            k.wait("t", sV, sV.n)
            bcast_bf16(g2b[:, 1024 * part:1024 * (part + 1)], OB[0:1, 0:1024], 1024)
        b2parts = []
        for part in range(4):
            Lb = next_ld()
            k.wait("y", sV, sV.n)
            tb = k.load(lambda: OO[0:1, 256:256 + 1024], lambda part=part: ap["b2r"][0:1, 1024 * part:1024 * (part + 1)], Lb)
            k.wait("v", Lb, tb)
            k.v(lambda v, part=part: v.tensor_copy(out=OB[0:1, 1024 * (part + 1):1024 * (part + 2)], in_=OO[0:1, 256:256 + 1024]), inc=(sV, 1))
            k.wait("t", sV, sV.n)
            k.v(lambda v, part=part: v.tensor_copy(out=OB[0:1, 1024 + 128 * 0:1024 + 128 * 0] if False else OB[0:1, 0:1024], in_=OO[0:1, 256:256 + 1024]), inc=(sV, 1))
        wm1 = SV[:, 304:305]
        wm2 = SV[:, 305:306]
        for wname, dstv, n_elem, nch, wid in (("w1", wm1, 4096.0 * 1024.0, 32, 1024), ("w2", wm2, 1024.0 * 4096.0, 8, 4096)):
            k.v(lambda v: v.memset(SV[:, 3:4], 0.0), inc=(sV, 1))
            for ch in range(nch):
                for sbk in range(wid // 1024):
                    L = next_ld()
                    tw = k.load(lambda: SQF[:, 0:1024], lambda ch=ch, sbk=sbk, wname=wname: ap[wname][128 * ch:128 * (ch + 1), 1024 * sbk:1024 * (sbk + 1)], L)
                    k.wait("v", L, tw)
                    k.v(lambda v: v.tensor_reduce(out=SV[:, 4:5], in_=SQF[:, 0:1024], axis=AXX, op=ALU.add, apply_absolute_value=True))
                    k.v(lambda v: v.tensor_tensor(out=SV[:, 3:4], in0=SV[:, 3:4], in1=SV[:, 4:5], op=ALU.add), inc=(sV, 1))
                    k.wait("y", sV, sV.n)
            k.wait("t", sV, sV.n)
            k.t(lambda t: t.matmul(PS[6][0:1, 0:1], onescol, SV[:, 3:4], start=True, stop=True), inc=(sT, 1))
            k.wait("v", sT, sT.n)
            k.v(lambda v, n_elem=n_elem: v.tensor_scalar(out=OO[0:1, 150:151], in0=PS[6][0:1, 0:1], scalar1=1.0 / n_elem, scalar2=1e-5, op0=ALU.mult, op1=ALU.max), inc=(sV, 1))
            k.wait("t", sV, sV.n)
            k.t(lambda t: t.matmul(PS[6][:, 0:1], onesf, OO[0:1, 150:151], start=True, stop=True), inc=(sT, 1))
            k.wait("v", sT, sT.n)
            k.v(lambda v, dstv=dstv: v.tensor_copy(out=dstv, in_=PS[6][:, 0:1]), inc=(sV, 1))
            k.wait("t", sV, sV.n)
        for c in range(KC):
            k.wait("t", Lw, tws[c])
        for b in range(B):
            Lr = next_ld()
            t2 = k.load(lambda: OO[0:1, 256:1280], lambda b=b: ap["mods"][b:b + 1, 4 * D:5 * D], Lr)
            k.wait("t", Lr, t2)
            bcast_f32(bc1, OO[0:1, 256:1280], 1024, bias=1.0)
            Lr = next_ld()
            t2 = k.load(lambda: OO[0:1, 256:1280], lambda b=b: ap["mods"][b:b + 1, 3 * D:4 * D], Lr)
            k.wait("t", Lr, t2)
            bcast_f32(bc2, OO[0:1, 256:1280], 1024)
            Lr = next_ld()
            t2 = k.load(lambda: OO[0:1, 256:1280], lambda b=b: ap["mods"][b:b + 1, 5 * D:6 * D], Lr)
            k.wait("t", Lr, t2)
            bcast_f32(bcg, OO[0:1, 256:1280], 1024)
            for tt in range(16):
                row = T * b + 128 * tt
                L = next_ld()
                k.wait("y", sV, sV.n)
                tw = k.load(lambda: xt, lambda row=row: ap["xs"][row:row + 128, :], L)
                k.wait("v", L, tw)
                k.v(lambda v: v.tensor_reduce(out=SV[:, 0:1], in_=xt, axis=AXX, op=ALU.add))
                k.v(lambda v: v.tensor_scalar(out=SV[:, 0:1], in0=SV[:, 0:1], scalar1=1.0 / D, scalar2=None, op0=ALU.mult))
                k.v(lambda v: v.tensor_scalar(out=zt, in0=xt, scalar1=SV[:, 0:1], scalar2=None, op0=ALU.subtract), inc=(sV, 1))
                k.wait("s", sV, sV.n)
                k.s(lambda s: s.activation(out=SQF[:, 0:1024], in_=zt, func=AF.Square), inc=(sS, 1))
                k.wait("v", sS, sS.n)
                k.v(lambda v: v.tensor_reduce(out=SV[:, 1:2], in_=SQF[:, 0:1024], axis=AXX, op=ALU.add), inc=(sV, 1))
                k.wait("s", sV, sV.n)
                k.s(lambda s: s.activation(out=SV[:, 1:2], in_=SV[:, 1:2], func=AF.Sqrt, bias=1e-5, scale=1.0 / D), inc=(sS, 1))
                k.wait("v", sS, sS.n)
                k.v(lambda v: v.reciprocal(out=SV[:, 2:3], in_=SV[:, 1:2]))
                k.v(lambda v: v.tensor_scalar(out=zt, in0=zt, scalar1=SV[:, 2:3], scalar2=None, op0=ALU.mult))
                k.v(lambda v: v.tensor_tensor(out=zt, in0=zt, in1=bc1, op=ALU.mult))
                k.v(lambda v: v.tensor_tensor(out=zt, in0=zt, in1=bc2, op=ALU.add))
                k.v(lambda v: v.tensor_reduce(out=SV[:, 0:1], in_=zt, axis=AXX, op=ALU.add))
                k.v(lambda v: v.tensor_scalar(out=SV[:, 0:1], in0=SV[:, 0:1], scalar1=1.0 / D, scalar2=None, op0=ALU.mult))
                k.v(lambda v: v.tensor_scalar(out=zt, in0=zt, scalar1=SV[:, 0:1], scalar2=None, op0=ALU.subtract), inc=(sV, 1))
                k.wait("s", sV, sV.n)
                k.s(lambda s: s.activation(out=SQF[:, 0:1024], in_=zt, func=AF.Square), inc=(sS, 1))
                k.wait("v", sS, sS.n)
                k.v(lambda v: v.tensor_reduce(out=SV[:, 1:2], in_=SQF[:, 0:1024], axis=AXX, op=ALU.add), inc=(sV, 1))
                k.wait("s", sV, sV.n)
                k.s(lambda s: s.activation(out=SV[:, 1:2], in_=SV[:, 1:2], func=AF.Sqrt, bias=1e-5, scale=1.0 / D), inc=(sS, 1))
                k.wait("v", sS, sS.n)
                k.v(lambda v: v.reciprocal(out=SV[:, 2:3], in_=SV[:, 1:2]))
                k.v(lambda v: v.tensor_scalar(out=zt, in0=zt, scalar1=SV[:, 2:3], scalar2=None, op0=ALU.mult))
                k.v(lambda v: v.tensor_tensor(out=zt, in0=zt, in1=g1b, op=ALU.mult))
                k.v(lambda v: v.tensor_tensor(out=zt, in0=zt, in1=b1b, op=ALU.add))
                k.v(lambda v: v.tensor_reduce(out=SV[:, 3:4], in_=zt, axis=AXX, op=ALU.max, apply_absolute_value=True))
                k.v(lambda v: v.tensor_scalar(out=SV[:, 3:4], in0=SV[:, 3:4], scalar1=1e-5, scalar2=None, op0=ALU.max))
                k.v(lambda v: v.reciprocal(out=SV[:, 4:5], in_=SV[:, 3:4]))
                k.v(lambda v: v.tensor_scalar(out=SV[:, 4:5], in0=SV[:, 4:5], scalar1=127.0, scalar2=None, op0=ALU.mult))
                k.v(lambda v: v.reciprocal(out=SV[:, 5:6], in_=SV[:, 4:5]))
                k.v(lambda v: v.tensor_scalar(out=zt, in0=zt, scalar1=SV[:, 4:5], scalar2=0.5, op0=ALU.mult, op1=ALU.add))
                vfloor(zt, zt, 1024, SQF[:, 0:1024])
                k.v(lambda v: v.tensor_scalar(out=xq, in0=zt, scalar1=-128.0, scalar2=127.0, op0=ALU.max, op1=ALU.min), inc=(sV, 1))
                k.v(lambda v: v.tensor_tensor(out=SV[:, 9:10], in0=SV[:, 5:6], in1=wm1, op=ALU.mult), inc=(sV, 1))
                k.wait("t", sV, sV.n)
                for c in range(KC):
                    k.t(lambda t, c=c: t.transpose(PSB[:, 0:128], xq[:, 128 * c:128 * (c + 1)], identb), inc=(sT, 1))
                    k.wait("v", sT, sT.n)
                    k.v(lambda v, c=c: v.tensor_copy(out=xqf[:, 128 * c:128 * (c + 1)], in_=PSB[:, 0:128]), inc=(sV, 1))
                    k.wait("t", sV, sV.n)
                for mg in range(8):
                    for cc in range(4):
                        m = 4 * mg + cc
                        for c in range(KC):
                            k.t(lambda t, m=m, cc=cc, c=c: t.matmul(PS[0][:, 128 * cc:128 * (cc + 1)], W1Q[:, 4096 * c + 128 * m:4096 * c + 128 * (m + 1)], xqf[:, 128 * c:128 * (c + 1)], start=(c == 0 and cc == 0), stop=(c == KC - 1 and cc == 3)),
                                inc=(sT, 1) if (c == KC - 1 and cc == 3) else None)
                    k.wait("v", sT, sT.n)
                    k.v(lambda v: v.tensor_copy(out=OS1, in_=PS[0][:, 0:512]), inc=(sV, 1))
                    k.wait("t", sV, sV.n)
                    for cc in range(4):
                        k.t(lambda t, cc=cc: t.transpose(PS[1][:, 128 * cc:128 * (cc + 1)], OS1[:, 128 * cc:128 * (cc + 1)], identf), inc=(sT, 1) if cc == 3 else None)
                    k.wait("v", sT, sT.n)
                    k.v(lambda v, mg=mg: v.scalar_tensor_tensor(out=H1[:, 512 * mg:512 * (mg + 1)], in0=PS[1][:, 0:512], scalar=SV[:, 9:10], in1=bh1b[:, 512 * mg:512 * (mg + 1)], op0=ALU.mult, op1=ALU.add), inc=(sV, 1))
                    k.wait("t", sV, sV.n)
                k.wait("s", sV, sV.n)
                k.s(lambda s: s.activation(out=H1, in_=H1, func=AF.Gelu), inc=(sS, 1))
                k.wait("v", sS, sS.n)
                k.v(lambda v: v.tensor_reduce(out=SV[:, 0:1], in_=H1, axis=AXX, op=ALU.add))
                k.v(lambda v: v.tensor_scalar(out=SV[:, 0:1], in0=SV[:, 0:1], scalar1=1.0 / 4096.0, scalar2=None, op0=ALU.mult))
                k.v(lambda v: v.tensor_scalar(out=H1, in0=H1, scalar1=SV[:, 0:1], scalar2=None, op0=ALU.subtract), inc=(sV, 1))
                k.wait("s", sV, sV.n)
                k.s(lambda s: s.activation(out=SQF, in_=H1, func=AF.Square), inc=(sS, 1))
                k.wait("v", sS, sS.n)
                k.v(lambda v: v.tensor_reduce(out=SV[:, 1:2], in_=SQF, axis=AXX, op=ALU.add), inc=(sV, 1))
                k.wait("s", sV, sV.n)
                k.s(lambda s: s.activation(out=SV[:, 1:2], in_=SV[:, 1:2], func=AF.Sqrt, bias=1e-5, scale=1.0 / 4096.0), inc=(sS, 1))
                k.wait("v", sS, sS.n)
                k.v(lambda v: v.reciprocal(out=SV[:, 2:3], in_=SV[:, 1:2]))
                k.v(lambda v: v.tensor_scalar(out=H1, in0=H1, scalar1=SV[:, 2:3], scalar2=None, op0=ALU.mult))
                k.v(lambda v: v.tensor_tensor(out=H1, in0=H1, in1=g2b, op=ALU.mult), inc=(sV, 1))
                for part in range(4):
                    bcast_bf16(b2b, OB[0:1, 1024 * (part + 1):1024 * (part + 2)], 1024)
                    k.wait("v", sS, sS.n)
                    k.v(lambda v, part=part: v.tensor_tensor(out=H1[:, 1024 * part:1024 * (part + 1)], in0=H1[:, 1024 * part:1024 * (part + 1)], in1=b2b, op=ALU.add), inc=(sV, 1))
                    k.wait("s", sV, sV.n)
                k.v(lambda v: v.tensor_reduce(out=SV[:, 3:4], in_=H1, axis=AXX, op=ALU.max, apply_absolute_value=True))
                k.v(lambda v: v.tensor_scalar(out=SV[:, 3:4], in0=SV[:, 3:4], scalar1=1e-5, scalar2=None, op0=ALU.max))
                k.v(lambda v: v.reciprocal(out=SV[:, 4:5], in_=SV[:, 3:4]))
                k.v(lambda v: v.tensor_scalar(out=SV[:, 4:5], in0=SV[:, 4:5], scalar1=127.0, scalar2=None, op0=ALU.mult))
                k.v(lambda v: v.reciprocal(out=SV[:, 5:6], in_=SV[:, 4:5]))
                k.v(lambda v: v.tensor_scalar(out=H1, in0=H1, scalar1=SV[:, 4:5], scalar2=0.5, op0=ALU.mult, op1=ALU.add))
                for part in range(4):
                    sl = slice(1024 * part, 1024 * (part + 1))
                    vfloor(H1[:, sl], H1[:, sl], 1024, SQF[:, 0:1024])
                k.v(lambda v: v.tensor_scalar(out=xq2, in0=H1, scalar1=-128.0, scalar2=127.0, op0=ALU.max, op1=ALU.min), inc=(sV, 1))
                k.v(lambda v: v.tensor_tensor(out=SV[:, 9:10], in0=SV[:, 5:6], in1=wm2, op=ALU.mult), inc=(sV, 1))
                k.wait("t", sV, sV.n)
                for c in range(32):
                    k.t(lambda t, c=c: t.transpose(PSB[:, 0:128], xq2[:, 128 * c:128 * (c + 1)], identb), inc=(sT, 1))
                    k.wait("v", sT, sT.n)
                    k.v(lambda v, c=c: v.tensor_copy(out=xq2f[:, 128 * c:128 * (c + 1)], in_=PSB[:, 0:128]), inc=(sV, 1))
                    k.wait("t", sV, sV.n)
                for mg in range(2):
                    for cc in range(4):
                        m = 4 * mg + cc
                        Lw2 = next_ld()
                        wslot = W2Q[:, 0:4096]
                        tws2 = k.load(lambda wslot=wslot: wslot.rearrange("p (c n) -> p c n", n=128),
                                      lambda m=m: ap["w2q"].rearrange("(c p) n -> p c n", p=128)[:, :, 128 * m:128 * (m + 1)], Lw2)
                        k.wait("t", Lw2, tws2)
                        for c in range(32):
                            k.t(lambda t, c=c, cc=cc, wslot=wslot: t.matmul(PS[0][:, 128 * cc:128 * (cc + 1)], wslot[:, 128 * c:128 * (c + 1)], xq2f[:, 128 * c:128 * (c + 1)], start=(c == 0 and cc == 0), stop=(c == 31 and cc == 3)),
                                inc=(sT, 1) if (c == 31) else None)
                        k.wait("y", sT, sT.n)
                    k.wait("v", sT, sT.n)
                    k.v(lambda v: v.tensor_copy(out=OS1, in_=PS[0][:, 0:512]), inc=(sV, 1))
                    k.wait("t", sV, sV.n)
                    for cc in range(4):
                        k.t(lambda t, cc=cc: t.transpose(PS[1][:, 128 * cc:128 * (cc + 1)], OS1[:, 128 * cc:128 * (cc + 1)], identf), inc=(sT, 1) if cc == 3 else None)
                    k.wait("v", sT, sT.n)
                    k.v(lambda v, mg=mg: v.scalar_tensor_tensor(out=zt[:, 512 * mg:512 * (mg + 1)], in0=PS[1][:, 0:512], scalar=SV[:, 9:10], in1=BCP[:, 11264 + 512 * mg:11264 + 512 * (mg + 1)], op0=ALU.mult, op1=ALU.add), inc=(sV, 1))
                    k.wait("t", sV, sV.n)
                k.v(lambda v: v.tensor_tensor(out=zt, in0=zt, in1=bcg, op=ALU.mult))
                k.v(lambda v: v.tensor_tensor(out=xt, in0=xt, in1=zt, op=ALU.add), inc=(sV, 1))
                k.wait("g", sV, sV.n)
                S = next_st()
                ts = k.store(lambda row=row: ap["xs"][row:row + 128, :], lambda: xt, S)
                k.wait("v", S, ts)
                k.wait("y", S, ts)
        k.barrier(BAR)

    def bh2_bcast():
        Lb = next_ld()
        k.wait("y", sV, sV.n)
        tb = k.load(lambda: OO[0:1, 256:256 + 1024], lambda: ap["bh2"], Lb)
        k.wait("v", Lb, tb)
        k.v(lambda v: v.tensor_copy(out=OB[0:1, 0:1024], in_=OO[0:1, 256:256 + 1024]), inc=(sV, 1))
        k.wait("t", sV, sV.n)
        bcast_bf16(BCP[:, 11264:12288], OB[0:1, 0:1024], 1024)

    def ffn1_phase():
        xt = FP[:, 0:1024]
        yt = FP[:, 1024:2048]
        at = FP[:, 2048:3072]
        et = FP[:, 3072:4096]
        st_ = FP[:, 4096:5120]
        ut = FP[:, 5120:6144]
        cm = FP[:, 6144:7168]
        zt = FP[:, 7168:8192]
        OS1 = FP[:, 8192:8704]
        bc1 = FP[:, 10240:11264]
        bc2 = FP[:, 11264:12288]
        bcg = FP[:, 12288:13312]
        bfb = FP[:, 8704:9728]
        xq = XQb[:, 0:1024]
        xqf = XQb[:, 1024:2048]
        Lw = next_ld()
        tws = []
        for c in range(KC):
            tws.append(k.load(lambda c=c: W2Q[:, 1024 * c:1024 * (c + 1)] if False else W1Q[:, 1024 * c:1024 * (c + 1)], lambda c=c: ap["wf8"][128 * c:128 * (c + 1), :], Lw))
        Lb = next_ld()
        tb = k.load(lambda: OO[0:1, 256:256 + 1024], lambda: ap["bf8"], Lb)
        k.wait("t", Lb, tb)
        bcast_f32(bfb, OO[0:1, 256:256 + 1024], 1024)
        k.t(lambda t: t.matmul(PS[6][:, 0:1], onesf, OO[0:1, 144:145], start=True, stop=True), inc=(sT, 1))
        k.wait("v", sT, sT.n)
        k.v(lambda v: v.tensor_copy(out=SV[:, 12:13], in_=PS[6][:, 0:1]), inc=(sV, 1))
        k.wait("t", sV, sV.n)
        for c in range(KC):
            k.wait("t", Lw, tws[c])
        for b in range(B):
            Lr = next_ld()
            t2 = k.load(lambda: OO[0:1, 256:1280], lambda b=b: ap["mods"][B + b:B + b + 1, 4 * D:5 * D], Lr)
            k.wait("t", Lr, t2)
            bcast_f32(bc1, OO[0:1, 256:1280], 1024, bias=1.0)
            Lr = next_ld()
            t2 = k.load(lambda: OO[0:1, 256:1280], lambda b=b: ap["mods"][B + b:B + b + 1, 3 * D:4 * D], Lr)
            k.wait("t", Lr, t2)
            bcast_f32(bc2, OO[0:1, 256:1280], 1024)
            Lr = next_ld()
            t2 = k.load(lambda: OO[0:1, 256:1280], lambda b=b: ap["mods"][B + b:B + b + 1, 5 * D:6 * D], Lr)
            k.wait("t", Lr, t2)
            bcast_f32(bcg, OO[0:1, 256:1280], 1024)
            for tt in range(16):
                row = T * b + 128 * tt
                L = next_ld()
                k.wait("y", sV, sV.n)
                tw = k.load(lambda: xt, lambda row=row: ap["xs"][row:row + 128, :], L)
                k.wait("v", L, tw)
                k.v(lambda v: v.tensor_reduce(out=SV[:, 0:1], in_=xt, axis=AXX, op=ALU.add))
                k.v(lambda v: v.tensor_scalar(out=SV[:, 0:1], in0=SV[:, 0:1], scalar1=1.0 / D, scalar2=None, op0=ALU.mult))
                k.v(lambda v: v.tensor_scalar(out=zt, in0=xt, scalar1=SV[:, 0:1], scalar2=None, op0=ALU.subtract), inc=(sV, 1))
                k.wait("s", sV, sV.n)
                k.s(lambda s: s.activation(out=cm, in_=zt, func=AF.Square), inc=(sS, 1))
                k.wait("v", sS, sS.n)
                k.v(lambda v: v.tensor_reduce(out=SV[:, 1:2], in_=cm, axis=AXX, op=ALU.add), inc=(sV, 1))
                k.wait("s", sV, sV.n)
                k.s(lambda s: s.activation(out=SV[:, 1:2], in_=SV[:, 1:2], func=AF.Sqrt, bias=1e-5, scale=1.0 / D), inc=(sS, 1))
                k.wait("v", sS, sS.n)
                k.v(lambda v: v.reciprocal(out=SV[:, 2:3], in_=SV[:, 1:2]))
                k.v(lambda v: v.tensor_scalar(out=zt, in0=zt, scalar1=SV[:, 2:3], scalar2=None, op0=ALU.mult))
                k.v(lambda v: v.tensor_tensor(out=zt, in0=zt, in1=bc1, op=ALU.mult))
                k.v(lambda v: v.tensor_tensor(out=zt, in0=zt, in1=bc2, op=ALU.add))
                k.v(lambda v: v.tensor_reduce(out=SV[:, 3:4], in_=zt, axis=AXX, op=ALU.max, apply_absolute_value=True))
                k.v(lambda v: v.tensor_scalar(out=SV[:, 3:4], in0=SV[:, 3:4], scalar1=1e-12, scalar2=1.0 / 448.0, op0=ALU.max, op1=ALU.mult))
                k.v(lambda v: v.reciprocal(out=SV[:, 4:5], in_=SV[:, 3:4]))
                k.v(lambda v: v.tensor_scalar(out=yt, in0=zt, scalar1=SV[:, 4:5], scalar2=None, op0=ALU.mult))
                k.v(lambda v: v.tensor_scalar(out=at, in0=yt, scalar1=-1.0, scalar2=None, op0=ALU.mult))
                k.v(lambda v: v.tensor_tensor(out=at, in0=at, in1=yt, op=ALU.max))
                k.v(lambda v: v.tensor_scalar(out=at, in0=at, scalar1=1e-30, scalar2=None, op0=ALU.max), inc=(sV, 1))
                k.wait("s", sV, sV.n)
                k.s(lambda s: s.activation(out=at, in_=at, func=AF.Ln), inc=(sS, 1))
                k.wait("v", sS, sS.n)
                k.v(lambda v: v.tensor_scalar(out=et, in0=at, scalar1=1.0 / LN2C, scalar2=None, op0=ALU.mult))
                vfloor(et, et, 1024, cm)
                k.v(lambda v: v.tensor_scalar(out=et, in0=et, scalar1=-6.0, scalar2=8.0, op0=ALU.max, op1=ALU.min), inc=(sV, 1))
                k.wait("s", sV, sV.n)
                k.s(lambda s: s.activation(out=st_, in_=et, func=AF.Exp, bias=-3.0 * LN2C, scale=LN2C), inc=(sS, 1))
                k.wait("v", sS, sS.n)
                k.v(lambda v: v.reciprocal(out=ut, in_=st_))
                k.v(lambda v: v.tensor_tensor(out=ut, in0=yt, in1=ut, op=ALU.mult))
                k.v(lambda v: v.tensor_scalar(out=ut, in0=ut, scalar1=0.5, scalar2=None, op0=ALU.add))
                vfloor(ut, ut, 1024, cm)
                k.v(lambda v: v.tensor_tensor(out=xq, in0=ut, in1=st_, op=ALU.mult), inc=(sV, 1))
                k.v(lambda v: v.tensor_tensor(out=SV[:, 9:10], in0=SV[:, 3:4], in1=SV[:, 12:13], op=ALU.mult), inc=(sV, 1))
                k.wait("t", sV, sV.n)
                for c in range(KC):
                    k.t(lambda t, c=c: t.transpose(PSB[:, 0:128], xq[:, 128 * c:128 * (c + 1)], identb), inc=(sT, 1))
                    k.wait("v", sT, sT.n)
                    k.v(lambda v, c=c: v.tensor_copy(out=xqf[:, 128 * c:128 * (c + 1)], in_=PSB[:, 0:128]), inc=(sV, 1))
                    k.wait("t", sV, sV.n)
                for mg in range(2):
                    for cc in range(4):
                        m = 4 * mg + cc
                        for c in range(KC):
                            k.t(lambda t, m=m, cc=cc, c=c: t.matmul(PS[0][:, 128 * cc:128 * (cc + 1)], W1Q[:, 1024 * c + 128 * m:1024 * c + 128 * (m + 1)], xqf[:, 128 * c:128 * (c + 1)], start=(c == 0 and cc == 0), stop=(c == KC - 1 and cc == 3)),
                                inc=(sT, 1) if (c == KC - 1 and cc == 3) else None)
                    k.wait("v", sT, sT.n)
                    k.v(lambda v: v.tensor_copy(out=OS1, in_=PS[0][:, 0:512]), inc=(sV, 1))
                    k.wait("t", sV, sV.n)
                    for cc in range(4):
                        k.t(lambda t, cc=cc: t.transpose(PS[1][:, 128 * cc:128 * (cc + 1)], OS1[:, 128 * cc:128 * (cc + 1)], identf), inc=(sT, 1) if cc == 3 else None)
                    k.wait("v", sT, sT.n)
                    k.v(lambda v, mg=mg: v.scalar_tensor_tensor(out=zt[:, 512 * mg:512 * (mg + 1)], in0=PS[1][:, 0:512], scalar=SV[:, 9:10], in1=bfb[:, 512 * mg:512 * (mg + 1)], op0=ALU.mult, op1=ALU.add), inc=(sV, 1))
                    k.wait("t", sV, sV.n)
                k.v(lambda v: v.tensor_tensor(out=zt, in0=zt, in1=bcg, op=ALU.mult))
                k.v(lambda v: v.tensor_tensor(out=xt, in0=xt, in1=zt, op=ALU.add), inc=(sV, 1))
                k.wait("g", sV, sV.n)
                S = next_st()
                ts = k.store(lambda row=row: y_out[row:row + 128, :], lambda: xt, S)
                k.wait("v", S, ts)
                k.wait("y", S, ts)
        k.barrier(BAR)

    import os
    KPH = int(os.environ.get("KPH", "99"))
    bh2_bcast()
    for lay in range(2):
        if KPH >= (1 if lay == 0 else 3):
            for b in range(B):
                ln_mod_phase(lay, b, 0)
                attn_phase(lay, b)
                outproj_phase(lay, b)
        if lay == 0:
            if KPH >= 2:
                ffn0_phase()
        else:
            if KPH >= 4:
                ffn1_phase()

    for s_ in sts:
        k.wait("g", s_, s_.n)

    with nc.Block() as block:
        @block.tensor
        def _(t):
            for f in k.q["t"]:
                f(t)

        @block.vector
        def _(v):
            for f in k.q["v"]:
                f(v)

        @block.scalar
        def _(s):
            for f in k.q["s"]:
                f(s)

        @block.gpsimd
        def _(g):
            for f in k.q["g"]:
                f(g)

        @block.sync
        def _(y):
            for f in k.q["y"]:
                f(y)

    es.close()
    return nc


_NC_CACHE = [None]


def kernel(**inputs):
    n = 8
    xfull = np.ascontiguousarray(np.asarray(inputs["L0_attn_x"], dtype=np.float32))
    maskc = np.zeros((128, 4 * 512), dtype=np.float64)
    for m in range(4):
        i = np.arange(128)[:, None]
        j = np.arange(512)[None, :]
        maskc[:, 512 * m:512 * (m + 1)] = np.where(i + 128 * m > j, -1e10, 0.0)
    maskc = maskc.astype(ml_dtypes.bfloat16)
    ramp = (np.arange(128)[:, None] + 128 * np.arange(16)[None, :] - 2047).astype(np.float32)
    common = {
        "w_qkv": np.asarray(inputs["L0_attn_w_qkv"], np.float32),
        "w_out": np.asarray(inputs["L0_attn_w_out"], np.float32),
        "b_out": np.asarray(inputs["L0_attn_b_out"], np.float32).reshape(1, D),
        "g1r": np.asarray(inputs["L0_ffn_ln1_g"], np.float32).reshape(1, D),
        "b1r": np.asarray(inputs["L0_ffn_ln1_b"], np.float32).reshape(1, D),
        "w1": np.asarray(inputs["L0_ffn_w1"], np.float32),
        "bh1": np.asarray(inputs["L0_ffn_b1"], np.float32).reshape(1, 4 * D),
        "g2r": np.asarray(inputs["L0_ffn_ln2_g"], np.float32).reshape(1, 4 * D),
        "b2r": np.asarray(inputs["L0_ffn_ln2_b"], np.float32).reshape(1, 4 * D),
        "w2": np.asarray(inputs["L0_ffn_w2"], np.float32),
        "bh2": np.asarray(inputs["L0_ffn_b2"], np.float32).reshape(1, D),
        "wmod0": np.asarray(inputs["L0_wmod"], np.float32),
        "Wq": np.asarray(inputs["L1_attn_Wq"], np.float32),
        "bq": np.asarray(inputs["L1_attn_bq"], np.float32).reshape(D, 1),
        "Wk": np.asarray(inputs["L1_attn_Wk"], np.float32),
        "bk": np.asarray(inputs["L1_attn_bk"], np.float32).reshape(D, 1),
        "Wv": np.asarray(inputs["L1_attn_Wv"], np.float32),
        "bv": np.asarray(inputs["L1_attn_bv"], np.float32).reshape(D, 1),
        "Wo": np.asarray(inputs["L1_attn_Wo"], np.float32),
        "bo": np.asarray(inputs["L1_attn_bo"], np.float32).reshape(1, D),
        "slopes": np.asarray(inputs["L1_attn_slopes"], np.float32).reshape(1, 16),
        "w8": np.asarray(inputs["L1_ffn_weight"]).view(np.uint8),
        "wsc": np.asarray(inputs["L1_ffn_weight_scale"], np.float32).reshape(1, 1),
        "bf8": np.asarray(inputs["L1_ffn_bias"], np.float32).reshape(1, D),
        "wmod1": np.asarray(inputs["L1_wmod"], np.float32),
        "identf": np.eye(128, dtype=np.float32),
        "identb": np.eye(128, dtype=ml_dtypes.bfloat16),
        "maskc": maskc,
        "ramp": ramp,
    }
    c0 = np.asarray(inputs["L0_c"], np.float32)
    c1 = np.asarray(inputs["L1_c"], np.float32)
    in_maps = []
    for i in range(n):
        m = dict(common)
        m["x_in"] = np.ascontiguousarray(xfull[2 * i:2 * i + 2].reshape(B * T, D))
        m["c0"] = np.ascontiguousarray(c0[2 * i:2 * i + 2])
        m["c1"] = np.ascontiguousarray(c1[2 * i:2 * i + 2])
        in_maps.append(m)
    if _NC_CACHE[0] is None:
        _NC_CACHE[0] = build_program()
    nc = _NC_CACHE[0]
    res = run_bass_kernel_spmd(nc, in_maps, core_ids=list(range(n)))
    out = np.concatenate([r["y"].reshape(1, B, T, D) for r in res.results], axis=0)
    return np.ascontiguousarray(out.reshape(16, T, D)).astype(np.float32)


# version 08
# speedup vs baseline: 1.8222x; 1.1211x over previous
import numpy as np
import ml_dtypes
import concourse.bass as bass
import concourse.mybir as mybir
from concourse.bass_utils import run_bass_kernel_spmd

F32 = mybir.dt.float32
BF16 = mybir.dt.bfloat16
I32 = mybir.dt.int32
U8 = mybir.dt.uint8
AF = mybir.ActivationFunctionType
ALU = mybir.AluOpType
AXX = mybir.AxisListType.X

D = 1024
T = 2048
B = 2
NP = 8
KC = 8
QB = 4
LN2C = float(np.log(2.0))


class Sem:
    def __init__(self, name):
        self.name, self.n, self.h = name, 0, None


class K:
    def __init__(self):
        self.q = {"t": [], "v": [], "s": [], "g": [], "y": []}
        self.sems = []
        self.nbar = 0

    def sem(self, name):
        s = Sem(name)
        self.sems.append(s)
        return s

    def op(self, eng, fn, inc=None, drain=False):
        def run(e):
            ins = fn(e)
            if inc is not None:
                ins.then_inc(inc[0].h, inc[1])
            if drain:
                e.drain()
        self.q[eng].append(run)
        if inc is not None:
            inc[0].n += inc[1]

    def wait(self, eng, s, val):
        self.q[eng].append(lambda e, s=s, val=val: e.wait_ge(s.h, val))

    def inc(self, eng, s, kk=1):
        s.n += kk
        self.q[eng].append(lambda e, s=s, kk=kk: e.sem_inc(s.h, kk))

    def barrier(self, bar):
        self.nbar += 1
        tgt = 5 * self.nbar
        for eng in ("t", "v", "s", "g", "y"):
            self.inc(eng, bar)
            self.wait(eng, bar, tgt)

    def t(self, fn, inc=None):
        self.op("t", fn, inc)

    def v(self, fn, inc=None):
        self.op("v", fn, inc, drain=True)

    def s(self, fn, inc=None):
        self.op("s", fn, inc, drain=True)

    def load(self, dst_fn, src_fn, sem):
        prev = sem.n
        def run(e, d=dst_fn, s=src_fn, m=sem, prev=prev):
            e.wait_ge(m.h, prev)
            e.dma_start(out=d(), in_=s()).then_inc(m.h, 16)
        self.q["y"].append(run)
        sem.n += 16
        return sem.n

    def store(self, dst_fn, src_fn, sem):
        prev = sem.n
        def run(e, d=dst_fn, s=src_fn, m=sem, prev=prev):
            e.wait_ge(m.h, prev)
            e.dma_start(out=d(), in_=s()).then_inc(m.h, 16)
        self.q["g"].append(run)
        sem.n += 16
        return sem.n


def build_program():
    nc = bass.Bass("TRN2", target_bir_lowering=False)
    ap = {}

    def din(name, shape, dt):
        ap[name] = nc.dram_tensor(name, list(shape), dt, kind="ExternalInput").ap()

    din("x_in", (B * T, D), F32)
    din("w_qkv", (D, 3 * D), F32)
    din("w_out", (D, D), F32)
    din("b_out", (1, D), F32)
    din("g1r", (1, D), F32)
    din("b1r", (1, D), F32)
    din("w1", (4 * D, D), F32)
    din("bh1", (1, 4 * D), F32)
    din("g2r", (1, 4 * D), F32)
    din("b2r", (1, 4 * D), F32)
    din("w2", (D, 4 * D), F32)
    din("bh2", (1, D), F32)
    din("c0", (B, D), F32)
    din("wmod0", (D, 6 * D), F32)
    din("Wq", (D, D), F32)
    din("bq", (D, 1), F32)
    din("Wk", (D, D), F32)
    din("bk", (D, 1), F32)
    din("Wv", (D, D), F32)
    din("bv", (D, 1), F32)
    din("Wo", (D, D), F32)
    din("bo", (1, D), F32)
    din("slopes", (1, 16), F32)
    din("w8", (D, D), U8)
    din("wsc", (1, 1), F32)
    din("bf8", (1, D), F32)
    din("c1", (B, D), F32)
    din("wmod1", (D, 6 * D), F32)
    din("identf", (128, 128), F32)
    din("identb", (128, 128), BF16)
    din("maskc", (128, 4 * 512), BF16)
    din("ramp", (128, 16), F32)

    y_out = nc.dram_tensor("y", [B * T, D], F32, kind="ExternalOutput").ap()

    def scr(name, shape, dt):
        ap[name] = nc.dram_tensor(name, list(shape), dt, kind="Internal").ap()

    scr("xs", (B * T, D), F32)
    scr("hfm", (D, T), F32)
    scr("ctx", (D, T), F32)
    scr("wqf", (D, D), F32)
    scr("wkf", (D, D), F32)
    scr("wvf", (D, D), F32)
    scr("wof", (D, D), F32)
    scr("w1q", (D, 4 * D), BF16)
    scr("w2q", (4 * D, D), BF16)
    scr("wf8", (D, D), BF16)
    scr("mods", (2 * B, 6 * D), F32)

    k = K()
    lds = [k.sem(f"ld{i}") for i in range(8)]
    sts = [k.sem(f"st{i}") for i in range(4)]
    idx = {"l": 0, "s": 0}
    sT, sV, sS, BAR = k.sem("sT"), k.sem("sV"), k.sem("sS"), k.sem("bar")

    def next_ld():
        idx["l"] = (idx["l"] + 1) % 8
        return lds[idx["l"]]

    def next_st():
        idx["s"] = (idx["s"] + 1) % 4
        return sts[idx["s"]]

    from contextlib import ExitStack
    es = ExitStack()

    def sb(name, shape, dt):
        return es.enter_context(nc.sbuf_tensor(name, list(shape), dt))

    FP = sb("FP", [128, 13824], F32)
    W1Q = sb("W1Q", [128, 32768], BF16)
    W2Q = sb("W2Q", [128, 4096], BF16)
    XQb = sb("XQb", [128, 10240], BF16)
    BCP = sb("BCP", [128, 12288], BF16)
    CB = sb("CBc", [128, 2304], BF16)
    CF = sb("CFc", [128, 160], F32)
    SV = sb("SVV", [128, 640], F32)
    OO = sb("OOr", [1, 1664], F32)
    OB = sb("OBr", [1, 5248], BF16)
    U8B = sb("U8B", [128, 1024], U8)
    IT32 = sb("IT32", [128, 1024], I32)
    ONB = sb("ONB", [1, 128], BF16)
    CST = sb("CST", [128, 16], F32)
    LNV = sb("LNV", [128, 512], F32)
    ONES128 = sb("ONES128", [128, 128], F32)
    PS = [es.enter_context(nc.psum_tensor(f"PS{i}", [128, 512], F32)) for i in range(7)]
    PSB = es.enter_context(nc.psum_tensor("PSB", [128, 1024], BF16))
    for s_ in k.sems:
        s_.h = es.enter_context(nc.semaphore(s_.name))

    cvals = [-7.0 * LN2C, LN2C, -3.0 * LN2C, 1e-5, 1.0 / 1024.0, 1.0 / 4096.0,
             0.125, -1.0, 1.0, 0.0, 0.5, 60.0]
    for ci_, val in enumerate(cvals):
        nc.const_aps.register(F32, val, CST[:, ci_:ci_ + 1])

    identf = CF[:, 0:128]
    rampc = CF[:, 128:144]
    identb = CB[:, 0:128]
    maskb = CB[:, 128:2176]
    onesf = OO[0:1, 0:128]
    onescol = SV[:, 15:16]
    ALB0 = 16

    def bcast_f32(dst, row_ap, width, bias=0.0, scale=1.0):
        for i in range((width + 511) // 512):
            w = min(512, width - 512 * i)
            k.t(lambda t, i=i, w=w, row_ap=row_ap: t.matmul(PS[6][:, 0:w], onesf, row_ap[0:1, 512 * i:512 * i + w], start=True, stop=True), inc=(sT, 1))
            k.wait("s", sT, sT.n)
            k.s(lambda s, i=i, w=w, dst=dst: s.activation(out=dst[:, 512 * i:512 * i + w], in_=PS[6][:, 0:w], func=AF.Copy, bias=bias, scale=scale), inc=(sS, 1))
            k.wait("t", sS, sS.n)
            k.wait("y", sT, sT.n)

    def bcast_bf16(dst, row_bf, width):
        for i in range((width + 511) // 512):
            w = min(512, width - 512 * i)
            k.t(lambda t, i=i, w=w, row_bf=row_bf: t.matmul(PS[6][:, 0:w], ONB[0:1, :], row_bf[0:1, 512 * i:512 * i + w], start=True, stop=True), inc=(sT, 1))
            k.wait("s", sT, sT.n)
            k.s(lambda s, i=i, w=w, dst=dst: s.activation(out=dst[:, 512 * i:512 * i + w], in_=PS[6][:, 0:w], func=AF.Copy), inc=(sS, 1))
            k.wait("t", sS, sS.n)
            k.wait("y", sT, sT.n)

    def vfloor(dst, src, width, gbuf):
        k.v(lambda v: v.tensor_copy(out=IT32[:, 0:width], in_=src))
        k.v(lambda v: v.tensor_copy(out=gbuf[:, 0:width], in_=IT32[:, 0:width]))
        k.v(lambda v: v.tensor_tensor(out=dst, in0=gbuf[:, 0:width], in1=src, op=ALU.is_gt))
        k.v(lambda v: v.tensor_tensor(out=dst, in0=gbuf[:, 0:width], in1=dst, op=ALU.subtract))

    L = next_ld()
    k.load(lambda: identf, lambda: ap["identf"], L)
    k.load(lambda: identb, lambda: ap["identb"], L)
    k.load(lambda: maskb, lambda: ap["maskc"], L)
    k.load(lambda: rampc, lambda: ap["ramp"], L)
    k.load(lambda: OO[0:1, 128:144], lambda: ap["slopes"], L)
    k.load(lambda: OO[0:1, 144:145], lambda: ap["wsc"], L)
    tA = L.n
    for eng in ("t", "v", "s"):
        k.wait(eng, L, tA)
    k.v(lambda v: v.memset(onesf, 1.0))
    k.v(lambda v: v.memset(ONES128[:], 1.0))
    for ci_, val in enumerate(cvals):
        k.v(lambda v, ci_=ci_, val=val: v.memset(CST[:, ci_:ci_ + 1], val))
    k.v(lambda v: v.memset(ONB[:], 1.0))
    k.v(lambda v: v.memset(onescol, 1.0))
    k.v(lambda v: v.memset(SV[:, 13:15], 0.0))
    k.v(lambda v: v.memset(SV[0:64, 13:14], 1.0))
    k.v(lambda v: v.memset(SV[64:128, 14:15], 1.0), inc=(sV, 1))
    k.wait("t", sV, sV.n)
    k.barrier(BAR)

    cw = FP[:, 0:1024]
    cts = FP[:, 1024:1024 + B * KC]
    wt0 = FP[:, 2048:2560]
    wt1 = FP[:, 2560:3072]
    mst = FP[:, 3072:3584]
    mst2 = FP[:, 3584:4096]
    for lay in range(2):
        L = next_ld()
        k.wait("y", sT, sT.n)
        tw = k.load(lambda: cw[0:B, :], lambda lay=lay: ap["c0" if lay == 0 else "c1"], L)
        k.wait("s", L, tw)
        k.s(lambda s: s.activation(out=cw[0:B, :], in_=cw[0:B, :], func=AF.Silu), inc=(sS, 1))
        k.wait("t", sS, sS.n)
        for c in range(KC):
            k.t(lambda t, c=c: t.transpose(PS[6][:, 0:B], cw[0:B, 128 * c:128 * (c + 1)], identf[0:B, 0:B]), inc=(sT, 1))
            k.wait("v", sT, sT.n)
            k.v(lambda v, c=c: v.tensor_copy(out=cts[:, B * c:B * (c + 1)], in_=PS[6][:, 0:B]), inc=(sV, 1))
            k.wait("t", sV, sV.n)
        WMB = FP[:, 4224:8320]
        for blk in range(12):
            wm = "wmod0" if lay == 0 else "wmod1"
            Lw = next_ld()
            k.wait("y", sT, sT.n)
            tw = k.load(lambda: WMB.rearrange("p (c n) -> p c n", n=512),
                        lambda blk=blk, wm=wm: ap[wm].rearrange("(c p) n -> p c n", p=128)[:, :, 512 * blk:512 * (blk + 1)], Lw)
            k.wait("t", Lw, tw)
            for c in range(KC):
                k.t(lambda t, c=c: t.matmul(PS[6][0:B, 0:512], cts[:, B * c:B * (c + 1)], WMB[:, 512 * c:512 * (c + 1)], start=(c == 0), stop=(c == KC - 1)), inc=(sT, 1))
            k.wait("v", sT, sT.n)
            j = blk // 2
            if j in (2, 5):
                k.v(lambda v: v.tensor_scalar(out=mst[0:B, :], in0=PS[6][0:B, 0:512], scalar1=0.0, scalar2=None, op0=ALU.is_ge))
                k.v(lambda v: v.tensor_scalar(out=mst[0:B, :], in0=mst[0:B, :], scalar1=2.0, scalar2=-1.0, op0=ALU.mult, op1=ALU.add))
                k.v(lambda v: v.tensor_scalar(out=mst2[0:B, :], in0=PS[6][0:B, 0:512], scalar1=-1.0, scalar2=None, op0=ALU.mult))
                k.v(lambda v: v.tensor_tensor(out=mst2[0:B, :], in0=mst2[0:B, :], in1=PS[6][0:B, 0:512], op=ALU.max))
                k.v(lambda v: v.tensor_scalar(out=mst2[0:B, :], in0=mst2[0:B, :], scalar1=0.25, scalar2=None, op0=ALU.max))
                k.v(lambda v: v.tensor_tensor(out=mst[0:B, :], in0=mst[0:B, :], in1=mst2[0:B, :], op=ALU.mult), inc=(sV, 1))
            else:
                k.v(lambda v: v.tensor_copy(out=mst[0:B, :], in_=PS[6][0:B, 0:512]), inc=(sV, 1))
            k.wait("t", sV, sV.n)
            k.wait("g", sV, sV.n)
            S = next_st()
            ts = k.store(lambda blk=blk, lay=lay: ap["mods"][B * lay:B * (lay + 1), 512 * blk:512 * (blk + 1)], lambda: mst[0:B, :], S)
            k.wait("v", S, ts)
    k.barrier(BAR)

    wload = FP[:, 0:1024]
    wy = FP[:, 1024:2048]
    wg = FP[:, 2048:3072]
    wc = FP[:, 3072:4096]
    qb = XQb[:, 0:1024]
    tstage = XQb[:, 1024:1152]
    fstage = FP[:, 4096:4224]

    def wquant(wname, dst, nchunk, width, n_elem):
        k.v(lambda v: v.memset(SV[:, 3:4], 0.0), inc=(sV, 1))
        nsub = width // 1024
        for ch in range(nchunk):
            for sbk in range(nsub):
                L = next_ld()
                tw = k.load(lambda: wload, lambda ch=ch, sbk=sbk, wname=wname: ap[wname][128 * ch:128 * (ch + 1), 1024 * sbk:1024 * (sbk + 1)], L)
                k.wait("v", L, tw)
                k.v(lambda v: v.tensor_reduce(out=SV[:, 4:5], in_=wload, axis=AXX, op=ALU.add, apply_absolute_value=True))
                k.v(lambda v: v.tensor_tensor(out=SV[:, 3:4], in0=SV[:, 3:4], in1=SV[:, 4:5], op=ALU.add), inc=(sV, 1))
                k.wait("y", sV, sV.n)
        k.wait("t", sV, sV.n)
        k.t(lambda t: t.matmul(PS[6][0:1, 0:1], onescol, SV[:, 3:4], start=True, stop=True), inc=(sT, 1))
        k.wait("v", sT, sT.n)
        k.v(lambda v: v.tensor_scalar(out=OO[0:1, 150:151], in0=PS[6][0:1, 0:1], scalar1=1.0 / n_elem, scalar2=1e-5, op0=ALU.mult, op1=ALU.max))
        k.v(lambda v: v.reciprocal(out=OO[0:1, 151:152], in_=OO[0:1, 150:151]), inc=(sV, 1))
        k.wait("t", sV, sV.n)
        k.t(lambda t: t.matmul(PS[6][:, 0:1], onesf, OO[0:1, 151:152], start=True, stop=True), inc=(sT, 1))
        k.wait("v", sT, sT.n)
        k.v(lambda v: v.tensor_copy(out=SV[:, 8:9], in_=PS[6][:, 0:1]), inc=(sV, 1))
        k.wait("t", sV, sV.n)
        for ch in range(nchunk):
            for sbk in range(nsub):
                L = next_ld()
                tw = k.load(lambda: wload, lambda ch=ch, sbk=sbk, wname=wname: ap[wname][128 * ch:128 * (ch + 1), 1024 * sbk:1024 * (sbk + 1)], L)
                k.wait("v", L, tw)
                k.wait("v", sT, sT.n)
                k.v(lambda v: v.tensor_scalar(out=wy, in0=wload, scalar1=SV[:, 8:9], scalar2=0.5, op0=ALU.mult, op1=ALU.add))
                vfloor(wy, wy, 1024, wg)
                k.v(lambda v: v.tensor_scalar(out=qb, in0=wy, scalar1=-1.0, scalar2=1.0, op0=ALU.max, op1=ALU.min), inc=(sV, 1))
                k.wait("t", sV, sV.n)
                k.wait("y", sV, sV.n)
                for c in range(8):
                    k.t(lambda t, c=c: t.transpose(PSB[:, 0:128], qb[:, 128 * c:128 * (c + 1)], identb), inc=(sT, 1))
                    k.wait("s", sT, sT.n)
                    k.s(lambda s: s.activation(out=tstage, in_=PSB[:, 0:128], func=AF.Copy), inc=(sS, 1))
                    k.wait("t", sS, sS.n)
                    k.wait("g", sS, sS.n)
                    S = next_st()
                    ts = k.store(lambda ch=ch, sbk=sbk, c=c, dst=dst: dst[1024 * sbk + 128 * c:1024 * sbk + 128 * (c + 1), 128 * ch:128 * (ch + 1)], lambda: tstage, S)
                    k.wait("s", S, ts)

    wquant("w1", ap["w1q"], 32, 1024, 4096.0 * 1024.0)
    wquant("w2", ap["w2q"], 8, 4096, 1024.0 * 4096.0)

    for wname, dname in (("Wq", "wqf"), ("Wk", "wkf"), ("Wv", "wvf"), ("Wo", "wof")):
        for ch in range(KC):
            L = next_ld()
            tw = k.load(lambda: wload, lambda ch=ch, wname=wname: ap[wname][128 * ch:128 * (ch + 1), 0:1024], L)
            k.wait("t", L, tw)
            for c in range(KC):
                k.t(lambda t, c=c: t.transpose(PS[6][:, 0:128], wload[:, 128 * c:128 * (c + 1)], identf), inc=(sT, 1))
                k.wait("v", sT, sT.n)
                k.v(lambda v: v.tensor_copy(out=fstage, in_=PS[6][:, 0:128]), inc=(sV, 1))
                k.wait("t", sV, sV.n)
                k.wait("y", sV, sV.n)
                k.wait("g", sV, sV.n)
                S = next_st()
                ts = k.store(lambda ch=ch, c=c, dname=dname: ap[dname][128 * c:128 * (c + 1), 128 * ch:128 * (ch + 1)], lambda: fstage, S)
                k.wait("v", S, ts)

    for ch in range(KC):
        L = next_ld()
        tw = k.load(lambda: U8B[:, 0:1024], lambda ch=ch: ap["w8"][128 * ch:128 * (ch + 1), :], L)
        k.wait("v", L, tw)
        k.wait("v", sT, sT.n)
        u, sg, e_, m_ = wload, wy, wg, wc
        k.v(lambda v: v.tensor_copy(out=u, in_=U8B[:, 0:1024]))
        k.v(lambda v: v.tensor_scalar(out=sg, in0=u, scalar1=128.0, scalar2=None, op0=ALU.is_ge))
        k.v(lambda v: v.scalar_tensor_tensor(out=u, in0=sg, scalar=-128.0, in1=u, op0=ALU.mult, op1=ALU.add))
        k.v(lambda v: v.tensor_scalar(out=e_, in0=u, scalar1=0.125, scalar2=None, op0=ALU.mult))
        vfloor(e_, e_, 1024, m_)
        k.v(lambda v: v.scalar_tensor_tensor(out=m_, in0=e_, scalar=-8.0, in1=u, op0=ALU.mult, op1=ALU.add))
        k.v(lambda v: v.tensor_scalar(out=u, in0=e_, scalar1=1.0, scalar2=None, op0=ALU.is_ge))
        k.v(lambda v: v.scalar_tensor_tensor(out=m_, in0=m_, scalar=0.125, in1=u, op0=ALU.mult, op1=ALU.add))
        k.v(lambda v: v.tensor_scalar(out=e_, in0=e_, scalar1=1.0, scalar2=None, op0=ALU.max), inc=(sV, 1))
        k.wait("s", sV, sV.n)
        k.s(lambda s: s.activation(out=e_, in_=e_, func=AF.Exp, bias=-7.0 * LN2C, scale=LN2C), inc=(sS, 1))
        k.wait("v", sS, sS.n)
        k.v(lambda v: v.tensor_scalar(out=sg, in0=sg, scalar1=-2.0, scalar2=1.0, op0=ALU.mult, op1=ALU.add))
        k.v(lambda v: v.tensor_tensor(out=m_, in0=m_, in1=e_, op=ALU.mult))
        k.v(lambda v: v.tensor_tensor(out=qb, in0=m_, in1=sg, op=ALU.mult), inc=(sV, 1))
        k.wait("t", sV, sV.n)
        k.wait("y", sV, sV.n)
        for c in range(KC):
            k.t(lambda t, c=c: t.transpose(PSB[:, 0:128], qb[:, 128 * c:128 * (c + 1)], identb), inc=(sT, 1))
            k.wait("s", sT, sT.n)
            k.s(lambda s: s.activation(out=tstage, in_=PSB[:, 0:128], func=AF.Copy), inc=(sS, 1))
            k.wait("t", sS, sS.n)
            k.wait("g", sS, sS.n)
            S = next_st()
            ts = k.store(lambda ch=ch, c=c: ap["wf8"][128 * c:128 * (c + 1), 128 * ch:128 * (ch + 1)], lambda: tstage, S)
            k.wait("s", S, ts)

    for h in range(16):
        k.t(lambda t, h=h: t.matmul(PS[6][:, 0:1], onesf, OO[0:1, 128 + h:129 + h], start=True, stop=True), inc=(sT, 1))
        k.wait("v", sT, sT.n)
        k.v(lambda v: v.tensor_copy(out=SV[:, 9:10], in_=PS[6][:, 0:1]), inc=(sV, 1))
        k.wait("t", sV, sV.n)
        k.v(lambda v, h=h: v.tensor_scalar(out=SV[:, ALB0 + 16 * h:ALB0 + 16 * (h + 1)], in0=rampc, scalar1=SV[:, 9:10], scalar2=None, op0=ALU.mult), inc=(sV, 1))
    k.barrier(BAR)

    def ln_mod_phase(lay, b, half):
        xt = FP[:, 0:1024]
        ht = FP[:, 1024:2048]
        sq = FP[:, 2048:3072]
        bc1 = FP[:, 3072:4096]
        bc2 = FP[:, 4096:5120]
        hs2 = FP[:, 5120:5632]
        j0 = 1 if half == 0 else 4
        Lr = next_ld()
        t2 = k.load(lambda: OO[0:1, 256:256 + 1024], lambda: ap["mods"][B * lay + b:B * lay + b + 1, D * j0:D * (j0 + 1)], Lr)
        k.wait("t", Lr, t2)
        bcast_f32(bc1, OO[0:1, 256:256 + 1024], 1024, bias=1.0)
        Lr = next_ld()
        t2 = k.load(lambda: OO[0:1, 256:256 + 1024], lambda: ap["mods"][B * lay + b:B * lay + b + 1, D * (j0 - 1):D * j0], Lr)
        k.wait("t", Lr, t2)
        bcast_f32(bc2, OO[0:1, 256:256 + 1024], 1024)
        src = ap["x_in"] if (lay == 0 and half == 0) else ap["xs"]
        for tt in range(16):
            row = T * b + 128 * tt
            L = next_ld()
            k.wait("y", sV, sV.n)
            tw = k.load(lambda: xt, lambda row=row, src=src: src[row:row + 128, :], L)
            k.wait("v", L, tw)
            k.v(lambda v: v.tensor_reduce(out=SV[:, 0:1], in_=xt, axis=AXX, op=ALU.add))
            k.v(lambda v: v.tensor_scalar(out=SV[:, 0:1], in0=SV[:, 0:1], scalar1=1.0 / D, scalar2=None, op0=ALU.mult))
            k.v(lambda v: v.tensor_scalar(out=ht, in0=xt, scalar1=SV[:, 0:1], scalar2=None, op0=ALU.subtract), inc=(sV, 1))
            k.wait("s", sV, sV.n)
            k.s(lambda s: s.activation(out=sq, in_=ht, func=AF.Square), inc=(sS, 1))
            k.wait("v", sS, sS.n)
            k.v(lambda v: v.tensor_reduce(out=SV[:, 1:2], in_=sq, axis=AXX, op=ALU.add), inc=(sV, 1))
            k.wait("s", sV, sV.n)
            k.s(lambda s: s.activation(out=SV[:, 1:2], in_=SV[:, 1:2], func=AF.Sqrt, bias=1e-5, scale=1.0 / D), inc=(sS, 1))
            k.wait("v", sS, sS.n)
            k.v(lambda v: v.reciprocal(out=SV[:, 2:3], in_=SV[:, 1:2]))
            k.v(lambda v: v.tensor_scalar(out=ht, in0=ht, scalar1=SV[:, 2:3], scalar2=None, op0=ALU.mult))
            k.v(lambda v: v.tensor_tensor(out=ht, in0=ht, in1=bc1, op=ALU.mult))
            k.v(lambda v: v.tensor_tensor(out=ht, in0=ht, in1=bc2, op=ALU.add), inc=(sV, 1))
            k.wait("t", sV, sV.n)
            for g4 in range(2):
                for cc in range(4):
                    c = 4 * g4 + cc
                    k.t(lambda t, c=c, cc=cc: t.transpose(PS[6][:, 128 * cc:128 * (cc + 1)], ht[:, 128 * c:128 * (c + 1)], identf, ),
                        inc=(sT, 1) if cc == 3 else None)
                k.wait("v", sT, sT.n)
                k.v(lambda v: v.tensor_copy(out=hs2, in_=PS[6][:, 0:512]), inc=(sV, 1))
                k.wait("t", sV, sV.n)
                k.wait("g", sV, sV.n)
                S = next_st()
                ts = k.store(lambda g4=g4, tt=tt: ap["hfm"].rearrange("(c p) n -> p c n", p=128)[:, 4 * g4:4 * g4 + 4, 128 * tt:128 * (tt + 1)],
                             lambda: hs2.rearrange("p (c n) -> p c n", n=128), S)
                k.wait("v", S, ts)
        k.barrier(BAR)

    def attn_phase(lay, b):
        QP = FP[:, 0:2048]
        KP = FP[:, 2048:4096]
        HST = FP[:, 4096:8192]
        VP_A = FP[:, 8192:9232]
        VP_B = FP[:, 9232:10272]
        VS = FP[:, 10272:10784]
        PPa = FP[:, 10784:11296]
        PPb = FP[:, 11296:11808]
        WSL = FP[:, 11808:12192]
        for pair in range(NP):
            if lay == 1:
                Lb = next_ld()
                k.wait("y", sV, sV.n)
                k.load(lambda: SV[:, 10:11], lambda pair=pair: ap["bq"][128 * pair:128 * (pair + 1), :], Lb)
                k.load(lambda: SV[:, 11:12], lambda pair=pair: ap["bk"][128 * pair:128 * (pair + 1), :], Lb)
                tb = k.load(lambda: SV[:, 12:13], lambda pair=pair: ap["bv"][128 * pair:128 * (pair + 1), :], Lb)
                k.wait("v", Lb, tb)
                k.wait("s", Lb, tb)
            WQS = FP[:, 12192:13216]
            KQS = FP[:, 10784:11808]
            for n in range(QB):
                Lh = next_ld()
                tws = k.load(lambda: HST.rearrange("p (c n) -> p c n", n=512),
                             lambda n=n: ap["hfm"].rearrange("(c p) n -> p c n", p=128)[:, :, 512 * n:512 * (n + 1)], Lh)
                Lw = next_ld()
                k.wait("y", sS, sS.n)
                k.wait("y", sT, sT.n)
                wq_src = "w_qkv" if lay == 0 else "wqf"
                wk_src = "w_qkv" if lay == 0 else "wkf"
                wv_src = "w_qkv" if lay == 0 else "wvf"
                qo = 128 * pair
                ko = D + 128 * pair if lay == 0 else 128 * pair
                vo = 2 * D + 128 * pair if lay == 0 else 128 * pair
                k.load(lambda: WQS.rearrange("p (c n) -> p c n", n=128),
                       lambda qo=qo, wq_src=wq_src: ap[wq_src].rearrange("(c p) n -> p c n", p=128)[:, :, qo:qo + 128], Lw)
                tww = k.load(lambda: KQS.rearrange("p (c n) -> p c n", n=128),
                             lambda ko=ko, wk_src=wk_src: ap[wk_src].rearrange("(c p) n -> p c n", p=128)[:, :, ko:ko + 128], Lw)
                k.wait("t", Lw, tww)
                k.wait("t", Lh, tws)
                for c in range(KC):
                    h_ap = HST[:, 512 * c:512 * (c + 1)]
                    k.t(lambda t, h_ap=h_ap, c=c: t.matmul(PS[0][:, 0:512], WQS[:, 128 * c:128 * (c + 1)], h_ap, start=(c == 0), stop=(c == KC - 1)))
                    k.t(lambda t, h_ap=h_ap, c=c: t.matmul(PS[1][:, 0:512], KQS[:, 128 * c:128 * (c + 1)], h_ap, start=(c == 0), stop=(c == KC - 1)), inc=(sT, 1))
                k.wait("y", sT, sT.n)
                Lv = next_ld()
                tww = k.load(lambda: WQS.rearrange("p (c n) -> p c n", n=128),
                             lambda vo=vo, wv_src=wv_src: ap[wv_src].rearrange("(c p) n -> p c n", p=128)[:, :, vo:vo + 128], Lv)
                k.wait("t", Lv, tww)
                for c in range(KC):
                    h_ap = HST[:, 512 * c:512 * (c + 1)]
                    k.t(lambda t, h_ap=h_ap, c=c: t.matmul(PS[2][:, 0:512], WQS[:, 128 * c:128 * (c + 1)], h_ap, start=(c == 0), stop=(c == KC - 1)), inc=(sT, 1))
                k.wait("y", sT, sT.n)
                k.wait("v", sT, sT.n)
                k.wait("v", sS, sS.n)
                k.wait("s", sT, sT.n)
                if lay == 0:
                    k.v(lambda v, n=n: v.tensor_copy(out=QP[:, 512 * n:512 * (n + 1)], in_=PS[0][:, 0:512]), inc=(sV, 1))
                    k.s(lambda s, n=n: s.activation(out=KP[:, 512 * n:512 * (n + 1)], in_=PS[1][:, 0:512], func=AF.Copy, scale=0.125), inc=(sS, 1))
                    k.v(lambda v: v.tensor_copy(out=VS, in_=PS[2][:, 0:512]), inc=(sV, 1))
                else:
                    k.v(lambda v, n=n: v.tensor_scalar(out=QP[:, 512 * n:512 * (n + 1)], in0=PS[0][:, 0:512], scalar1=SV[:, 10:11], scalar2=None, op0=ALU.add), inc=(sV, 1))
                    k.v(lambda v, n=n: v.tensor_scalar(out=KP[:, 512 * n:512 * (n + 1)], in0=PS[1][:, 0:512], scalar1=SV[:, 11:12], scalar2=0.125, op0=ALU.add, op1=ALU.mult))
                    k.inc("s", sS)
                    k.v(lambda v: v.tensor_scalar(out=VS, in0=PS[2][:, 0:512], scalar1=SV[:, 12:13], scalar2=None, op0=ALU.add), inc=(sV, 1))
                k.wait("t", sV, sV.n)
                k.wait("t", sS, sS.n)
                for st4 in range(4):
                    k.t(lambda t, st4=st4: t.transpose(PS[3][:, 0:128], VS[:, 128 * st4:128 * (st4 + 1)], identf), inc=(sT, 1))
                    k.wait("v", sT, sT.n)
                    ci = 4 * n + st4
                    k.v(lambda v, ci=ci: v.tensor_copy(out=VP_A[:, 65 * ci:65 * ci + 64], in_=PS[3][:, 0:64]))
                    k.v(lambda v, ci=ci: v.tensor_copy(out=VP_B[:, 65 * ci:65 * ci + 64], in_=PS[3][:, 64:128]), inc=(sV, 1))
                    k.wait("t", sV, sV.n)
            for ci_ in range(16):
                k.v(lambda v, ci_=ci_: v.memset(VP_A[:, 65 * ci_ + 64:65 * ci_ + 65], 1.0))
                k.v(lambda v, ci_=ci_: v.memset(VP_B[:, 65 * ci_ + 64:65 * ci_ + 65], 1.0))
            k.inc("v", sV)
            k.wait("t", sV, sV.n)
            k.wait("s", sV, sV.n)
            for src, oi in ((QP, 0), (KP, 2)):
                for blk in range(4):
                    k.s(lambda s, blk=blk, src=src: s.activation(out=VS, in_=src[:, 512 * blk:512 * (blk + 1)], func=AF.Square), inc=(sS, 1))
                    k.wait("t", sS, sS.n)
                    for hh in range(2):
                        k.t(lambda t, hh=hh: t.matmul(PS[3][0:1, 0:512], SV[:, 13 + hh:14 + hh], VS, start=True, stop=True), inc=(sT, 1))
                        k.wait("v", sT, sT.n)
                        if blk == 0:
                            k.v(lambda v, oi=oi, hh=hh: v.tensor_reduce(out=OO[0:1, 160 + oi + hh:161 + oi + hh], in_=PS[3][0:1, 0:512], axis=AXX, op=ALU.max), inc=(sV, 1))
                        else:
                            k.v(lambda v: v.tensor_reduce(out=OO[0:1, 166:167], in_=PS[3][0:1, 0:512], axis=AXX, op=ALU.max))
                            k.v(lambda v, oi=oi, hh=hh: v.tensor_tensor(out=OO[0:1, 160 + oi + hh:161 + oi + hh], in0=OO[0:1, 160 + oi + hh:161 + oi + hh], in1=OO[0:1, 166:167], op=ALU.max), inc=(sV, 1))
                        k.wait("t", sV, sV.n)
                    k.wait("s", sV, sV.n)
            k.wait("s", sV, sV.n)
            k.s(lambda s: s.activation(out=OO[0:1, 160:164], in_=OO[0:1, 160:164], func=AF.Sqrt), inc=(sS, 1))
            k.wait("v", sS, sS.n)
            k.v(lambda v: v.tensor_tensor(out=OO[0:1, 164:165], in0=OO[0:1, 160:161], in1=OO[0:1, 162:163], op=ALU.mult))
            k.v(lambda v: v.tensor_tensor(out=OO[0:1, 165:166], in0=OO[0:1, 161:162], in1=OO[0:1, 163:164], op=ALU.mult), inc=(sV, 1))
            k.wait("t", sV, sV.n)
            for hh in range(2):
                k.t(lambda t, hh=hh: t.matmul(PS[3][:, 0:1], onesf, OO[0:1, 164 + hh:165 + hh], start=True, stop=True), inc=(sT, 1))
                k.wait("s", sT, sT.n)
                k.s(lambda s, hh=hh: s.activation(out=SV[:, 6 + hh:7 + hh], in_=PS[3][:, 0:1], func=AF.Copy, bias=60.0, scale=-1.0), inc=(sS, 1))
                k.wait("t", sS, sS.n)
                k.wait("v", sS, sS.n)
            if lay == 1:
                for hh in range(2):
                    hid = 2 * pair + hh
                    k.v(lambda v, hh=hh, hid=hid: v.tensor_scalar(out=SV[:, 272 + 16 * hh:288 + 16 * hh], in0=SV[:, ALB0 + 16 * hid:ALB0 + 16 * (hid + 1)], scalar1=SV[:, 6 + hh:7 + hh], scalar2=None, op0=ALU.add), inc=(sV, 1))
                k.wait("s", sV, sV.n)
            for n in range(QB):
                clist = list(range(16)) if lay == 1 else list(range(4 * n + 4))
                k.wait("t", sV, sV.n)
                pend = None
                for ci, c in enumerate(clist):
                    par = ci % 2
                    psA = PS[0] if par == 0 else PS[1]
                    psB = PS[4] if par == 0 else PS[5]
                    ppa = FP[:, 10784:11296] if par == 0 else FP[:, 12192:12704]
                    ppb = FP[:, 11296:11808] if par == 0 else FP[:, 12704:13216]
                    crossing = (lay == 0) and (c >= 4 * n)
                    k.t(lambda t, c=c, n=n, psA=psA, crossing=crossing: t.matmul(psA[:, 0:512], KP[0:64, 128 * c:128 * (c + 1)], QP[0:64, 512 * n:512 * (n + 1)], start=True, stop=not crossing))
                    k.t(lambda t, c=c, n=n, psB=psB, crossing=crossing: t.matmul(psB[:, 0:512], KP[64:128, 128 * c:128 * (c + 1)], QP[64:128, 512 * n:512 * (n + 1)], start=True, stop=not crossing),
                        inc=None if crossing else (sT, 1))
                    if crossing:
                        m = c - 4 * n
                        k.t(lambda t, m=m, psA=psA: t.matmul(psA[:, 0:512], identb, maskb[:, 512 * m:512 * (m + 1)], start=False, stop=True))
                        k.t(lambda t, m=m, psB=psB: t.matmul(psB[:, 0:512], identb, maskb[:, 512 * m:512 * (m + 1)], start=False, stop=True), inc=(sT, 1))
                    if lay == 0:
                        biasA, biasB = SV[:, 6:7], SV[:, 7:8]
                    else:
                        biasA, biasB = SV[:, 272 + c:273 + c], SV[:, 288 + c:289 + c]
                    k.wait("s", sT, sT.n)
                    k.s(lambda s, psA=psA, biasA=biasA, ppa=ppa: s.activation(out=ppa, in_=psA[:, 0:512], func=AF.Exp, bias=biasA, scale=1.0))
                    k.s(lambda s, psB=psB, biasB=biasB, ppb=ppb: s.activation(out=ppb, in_=psB[:, 0:512], func=AF.Exp, bias=biasB, scale=1.0), inc=(sS, 1))
                    if pend is not None:
                        pc, pci, pppa, pppb = pend
                        k.wait("t", sS, sS.n - 1)
                        k.t(lambda t, pc=pc, pci=pci, pppa=pppa: t.matmul(PS[2][0:65, 0:512], VP_A[:, 65 * pc:65 * (pc + 1)], pppa, start=(pci == 0), stop=False))
                        k.t(lambda t, pc=pc, pci=pci, pppb=pppb: t.matmul(PS[3][0:65, 0:512], VP_B[:, 65 * pc:65 * (pc + 1)], pppb, start=(pci == 0), stop=False), inc=(sT, 1))
                    pend = (c, ci, ppa, ppb)
                pc, pci, pppa, pppb = pend
                k.wait("t", sS, sS.n)
                k.t(lambda t, pc=pc, pci=pci, pppa=pppa: t.matmul(PS[2][0:65, 0:512], VP_A[:, 65 * pc:65 * (pc + 1)], pppa, start=(pci == 0), stop=True))
                k.t(lambda t, pc=pc, pci=pci, pppb=pppb: t.matmul(PS[3][0:65, 0:512], VP_B[:, 65 * pc:65 * (pc + 1)], pppb, start=(pci == 0), stop=True), inc=(sT, 1))
                k.wait("v", sT, sT.n)
                for hh, psx in ((0, PS[2]), (1, PS[3])):
                    k.v(lambda v, psx=psx: v.tensor_copy(out=LNV[64:65, 0:512], in_=psx[64:65, 0:512]), inc=(sV, 1))
                    k.v(lambda v: v.reciprocal(out=LNV[64:65, 0:512], in_=LNV[64:65, 0:512]), inc=(sV, 1))
                    k.wait("t", sV, sV.n)
                    k.t(lambda t: t.matmul(PS[6][:, 0:512], ONES128[64:65, 0:128], LNV[64:65, 0:512], start=True, stop=True), inc=(sT, 1))
                    k.wait("v", sT, sT.n)
                    k.v(lambda v: v.tensor_copy(out=LNV[0:64, 0:512], in_=PS[6][0:64, 0:512]), inc=(sV, 1))
                    k.v(lambda v, psx=psx: v.tensor_tensor(out=VS[0:64, 0:512], in0=psx[0:64, 0:512], in1=LNV[0:64, 0:512], op=ALU.mult), inc=(sV, 1))
                    k.wait("t", sV, sV.n)
                    k.wait("g", sV, sV.n)
                    S = next_st()
                    row = 128 * pair + 64 * hh
                    ts = k.store(lambda row=row, n=n: ap["ctx"][row:row + 64, 512 * n:512 * (n + 1)], lambda: VS[0:64, 0:512], S)
                    k.wait("v", S, ts)
        k.barrier(BAR)

    def outproj_phase(lay, b):
        OSb = FP[:, 0:4096]
        CXT = FP[:, 4096:8192]
        WS2 = FP[:, 8192:8448]
        XT2 = FP[:, 8448:9472]
        TT2 = FP[:, 9472:10496]
        BOC = FP[:, 10496:11520]
        GAC = FP[:, 11520:12544]
        Lb = next_ld()
        bname = "b_out" if lay == 0 else "bo"
        tb = k.load(lambda: OO[0:1, 256:256 + 1024], lambda bname=bname: ap[bname], Lb)
        k.wait("t", Lb, tb)
        bcast_f32(BOC, OO[0:1, 256:256 + 1024], 1024)
        Lr = next_ld()
        t2 = k.load(lambda: OO[0:1, 256:256 + 1024], lambda: ap["mods"][B * lay + b:B * lay + b + 1, 2 * D:3 * D], Lr)
        k.wait("t", Lr, t2)
        bcast_f32(GAC, OO[0:1, 256:256 + 1024], 1024)
        src_x = ap["x_in"] if lay == 0 else ap["xs"]
        wsrc = ap["w_out"] if lay == 0 else ap["wof"]
        WOS = FP[:, 12544:13568]
        for n in range(QB):
            Lc = next_ld()
            tcs = k.load(lambda: CXT.rearrange("p (c n) -> p c n", n=512),
                         lambda n=n: ap["ctx"].rearrange("(c p) n -> p c n", p=128)[:, :, 512 * n:512 * (n + 1)], Lc)
            k.wait("t", Lc, tcs)
            for m in range(KC):
                psm = PS[0] if m % 2 == 0 else PS[1]
                Lw = next_ld()
                k.wait("y", sT, sT.n)
                tww = k.load(lambda: WOS.rearrange("p (c n) -> p c n", n=128),
                             lambda m=m, wsrc=wsrc: wsrc.rearrange("(c p) n -> p c n", p=128)[:, :, 128 * m:128 * (m + 1)], Lw)
                k.wait("t", Lw, tww)
                for c in range(KC):
                    k.t(lambda t, c=c, psm=psm, m=m: t.matmul(psm[:, 0:512], WOS[:, 128 * c:128 * (c + 1)], CXT[:, 512 * c:512 * (c + 1)], start=(c == 0), stop=(c == KC - 1)), inc=(sT, 1))
                k.wait("v", sT, sT.n)
                k.v(lambda v, m=m, psm=psm: v.tensor_copy(out=OSb[:, 512 * m:512 * (m + 1)], in_=psm[:, 0:512]), inc=(sV, 1))
                k.wait("t", sV, sV.n)
            k.wait("y", sT, sT.n)
            for st4 in range(4):
                tt = 4 * n + st4
                row = T * b + 128 * tt
                Lx = next_ld()
                k.wait("y", sV, sV.n)
                tx = k.load(lambda: XT2, lambda row=row, src_x=src_x: src_x[row:row + 128, :], Lx)
                for m2 in range(2):
                    for cc in range(4):
                        mm = 4 * m2 + cc
                        k.t(lambda t, mm=mm, st4=st4, m2=m2, cc=cc: t.transpose(PS[2 + m2][:, 128 * cc:128 * (cc + 1)], OSb[:, 512 * mm + 128 * st4:512 * mm + 128 * (st4 + 1)], identf),
                            inc=(sT, 1) if cc == 3 else None)
                k.wait("v", sT, sT.n)
                k.wait("v", Lx, tx)
                k.v(lambda v: v.tensor_tensor(out=TT2[:, 0:512], in0=PS[2][:, 0:512], in1=BOC[:, 0:512], op=ALU.add))
                k.v(lambda v: v.tensor_tensor(out=TT2[:, 512:1024], in0=PS[3][:, 0:512], in1=BOC[:, 512:1024], op=ALU.add), inc=(sV, 1))
                k.wait("t", sV, sV.n)
                k.v(lambda v: v.tensor_tensor(out=TT2, in0=TT2, in1=GAC, op=ALU.mult))
                k.v(lambda v: v.tensor_tensor(out=XT2, in0=XT2, in1=TT2, op=ALU.add), inc=(sV, 1))
                k.wait("g", sV, sV.n)
                S = next_st()
                ts = k.store(lambda row=row: ap["xs"][row:row + 128, :], lambda: XT2, S)
                k.wait("v", S, ts)
                k.wait("y", S, ts)
        k.barrier(BAR)

    def ffn0_phase():
        xt = FP[:, 0:1024]
        zt = FP[:, 1024:2048]
        H1 = FP[:, 2048:6144]
        SQF = FP[:, 6144:10240]
        bc1 = FP[:, 10240:11264]
        bc2 = FP[:, 11264:12288]
        bcg = FP[:, 12288:13312]
        OS1 = FP[:, 13312:13824]
        g1b = BCP[:, 0:1024]
        b1b = BCP[:, 1024:2048]
        bh1b = BCP[:, 2048:6144]
        g2b = BCP[:, 6144:10240]
        b2b = BCP[:, 10240:11264]
        xq = XQb[:, 0:1024]
        xqf = XQb[:, 1024:2048]
        xq2 = XQb[:, 2048:6144]
        xq2f = XQb[:, 6144:10240]
        Lw = next_ld()
        tws = []
        for c in range(KC):
            tws.append(k.load(lambda c=c: W1Q[:, 4096 * c:4096 * (c + 1)], lambda c=c: ap["w1q"][128 * c:128 * (c + 1), :], Lw))
        for nmsrc, dst, wid in (("g1r", g1b, 1024), ("b1r", b1b, 1024)):
            Lb = next_ld()
            k.wait("y", sV, sV.n)
            tb = k.load(lambda: OO[0:1, 256:256 + 1024], lambda nmsrc=nmsrc: ap[nmsrc], Lb)
            k.wait("v", Lb, tb)
            k.v(lambda v: v.tensor_copy(out=OB[0:1, 0:1024], in_=OO[0:1, 256:256 + 1024]), inc=(sV, 1))
            k.wait("t", sV, sV.n)
            bcast_bf16(dst, OB[0:1, 0:1024], wid)
        for part in range(4):
            Lb = next_ld()
            k.wait("y", sV, sV.n)
            tb = k.load(lambda: OO[0:1, 256:256 + 1024], lambda part=part: ap["bh1"][0:1, 1024 * part:1024 * (part + 1)], Lb)
            k.wait("v", Lb, tb)
            k.v(lambda v: v.tensor_copy(out=OB[0:1, 0:1024], in_=OO[0:1, 256:256 + 1024]), inc=(sV, 1))
            k.wait("t", sV, sV.n)
            bcast_bf16(bh1b[:, 1024 * part:1024 * (part + 1)], OB[0:1, 0:1024], 1024)
        for part in range(4):
            Lb = next_ld()
            k.wait("y", sV, sV.n)
            tb = k.load(lambda: OO[0:1, 256:256 + 1024], lambda part=part: ap["g2r"][0:1, 1024 * part:1024 * (part + 1)], Lb)
            k.wait("v", Lb, tb)
            k.v(lambda v: v.tensor_copy(out=OB[0:1, 0:1024], in_=OO[0:1, 256:256 + 1024]), inc=(sV, 1))
            k.wait("t", sV, sV.n)
            bcast_bf16(g2b[:, 1024 * part:1024 * (part + 1)], OB[0:1, 0:1024], 1024)
        b2parts = []
        for part in range(4):
            Lb = next_ld()
            k.wait("y", sV, sV.n)
            tb = k.load(lambda: OO[0:1, 256:256 + 1024], lambda part=part: ap["b2r"][0:1, 1024 * part:1024 * (part + 1)], Lb)
            k.wait("v", Lb, tb)
            k.v(lambda v, part=part: v.tensor_copy(out=OB[0:1, 1024 * (part + 1):1024 * (part + 2)], in_=OO[0:1, 256:256 + 1024]), inc=(sV, 1))
            k.wait("t", sV, sV.n)
            k.v(lambda v, part=part: v.tensor_copy(out=OB[0:1, 1024 + 128 * 0:1024 + 128 * 0] if False else OB[0:1, 0:1024], in_=OO[0:1, 256:256 + 1024]), inc=(sV, 1))
        wm1 = SV[:, 304:305]
        wm2 = SV[:, 305:306]
        for wname, dstv, n_elem, nch, wid in (("w1", wm1, 4096.0 * 1024.0, 32, 1024), ("w2", wm2, 1024.0 * 4096.0, 8, 4096)):
            k.v(lambda v: v.memset(SV[:, 3:4], 0.0), inc=(sV, 1))
            for ch in range(nch):
                for sbk in range(wid // 1024):
                    L = next_ld()
                    tw = k.load(lambda: SQF[:, 0:1024], lambda ch=ch, sbk=sbk, wname=wname: ap[wname][128 * ch:128 * (ch + 1), 1024 * sbk:1024 * (sbk + 1)], L)
                    k.wait("v", L, tw)
                    k.v(lambda v: v.tensor_reduce(out=SV[:, 4:5], in_=SQF[:, 0:1024], axis=AXX, op=ALU.add, apply_absolute_value=True))
                    k.v(lambda v: v.tensor_tensor(out=SV[:, 3:4], in0=SV[:, 3:4], in1=SV[:, 4:5], op=ALU.add), inc=(sV, 1))
                    k.wait("y", sV, sV.n)
            k.wait("t", sV, sV.n)
            k.t(lambda t: t.matmul(PS[6][0:1, 0:1], onescol, SV[:, 3:4], start=True, stop=True), inc=(sT, 1))
            k.wait("v", sT, sT.n)
            k.v(lambda v, n_elem=n_elem: v.tensor_scalar(out=OO[0:1, 150:151], in0=PS[6][0:1, 0:1], scalar1=1.0 / n_elem, scalar2=1e-5, op0=ALU.mult, op1=ALU.max), inc=(sV, 1))
            k.wait("t", sV, sV.n)
            k.t(lambda t: t.matmul(PS[6][:, 0:1], onesf, OO[0:1, 150:151], start=True, stop=True), inc=(sT, 1))
            k.wait("v", sT, sT.n)
            k.v(lambda v, dstv=dstv: v.tensor_copy(out=dstv, in_=PS[6][:, 0:1]), inc=(sV, 1))
            k.wait("t", sV, sV.n)
        for c in range(KC):
            k.wait("t", Lw, tws[c])
        for b in range(B):
            Lr = next_ld()
            t2 = k.load(lambda: OO[0:1, 256:1280], lambda b=b: ap["mods"][b:b + 1, 4 * D:5 * D], Lr)
            k.wait("t", Lr, t2)
            bcast_f32(bc1, OO[0:1, 256:1280], 1024, bias=1.0)
            Lr = next_ld()
            t2 = k.load(lambda: OO[0:1, 256:1280], lambda b=b: ap["mods"][b:b + 1, 3 * D:4 * D], Lr)
            k.wait("t", Lr, t2)
            bcast_f32(bc2, OO[0:1, 256:1280], 1024)
            Lr = next_ld()
            t2 = k.load(lambda: OO[0:1, 256:1280], lambda b=b: ap["mods"][b:b + 1, 5 * D:6 * D], Lr)
            k.wait("t", Lr, t2)
            bcast_f32(bcg, OO[0:1, 256:1280], 1024)
            for tt in range(16):
                row = T * b + 128 * tt
                L = next_ld()
                k.wait("y", sV, sV.n)
                tw = k.load(lambda: xt, lambda row=row: ap["xs"][row:row + 128, :], L)
                k.wait("v", L, tw)
                k.v(lambda v: v.tensor_reduce(out=SV[:, 0:1], in_=xt, axis=AXX, op=ALU.add))
                k.v(lambda v: v.tensor_scalar(out=SV[:, 0:1], in0=SV[:, 0:1], scalar1=1.0 / D, scalar2=None, op0=ALU.mult))
                k.v(lambda v: v.tensor_scalar(out=zt, in0=xt, scalar1=SV[:, 0:1], scalar2=None, op0=ALU.subtract), inc=(sV, 1))
                k.wait("s", sV, sV.n)
                k.s(lambda s: s.activation(out=SQF[:, 0:1024], in_=zt, func=AF.Square), inc=(sS, 1))
                k.wait("v", sS, sS.n)
                k.v(lambda v: v.tensor_reduce(out=SV[:, 1:2], in_=SQF[:, 0:1024], axis=AXX, op=ALU.add), inc=(sV, 1))
                k.wait("s", sV, sV.n)
                k.s(lambda s: s.activation(out=SV[:, 1:2], in_=SV[:, 1:2], func=AF.Sqrt, bias=1e-5, scale=1.0 / D), inc=(sS, 1))
                k.wait("v", sS, sS.n)
                k.v(lambda v: v.reciprocal(out=SV[:, 2:3], in_=SV[:, 1:2]))
                k.v(lambda v: v.tensor_scalar(out=zt, in0=zt, scalar1=SV[:, 2:3], scalar2=None, op0=ALU.mult))
                k.v(lambda v: v.tensor_tensor(out=zt, in0=zt, in1=bc1, op=ALU.mult))
                k.v(lambda v: v.tensor_tensor(out=zt, in0=zt, in1=bc2, op=ALU.add))
                k.v(lambda v: v.tensor_reduce(out=SV[:, 0:1], in_=zt, axis=AXX, op=ALU.add))
                k.v(lambda v: v.tensor_scalar(out=SV[:, 0:1], in0=SV[:, 0:1], scalar1=1.0 / D, scalar2=None, op0=ALU.mult))
                k.v(lambda v: v.tensor_scalar(out=zt, in0=zt, scalar1=SV[:, 0:1], scalar2=None, op0=ALU.subtract), inc=(sV, 1))
                k.wait("s", sV, sV.n)
                k.s(lambda s: s.activation(out=SQF[:, 0:1024], in_=zt, func=AF.Square), inc=(sS, 1))
                k.wait("v", sS, sS.n)
                k.v(lambda v: v.tensor_reduce(out=SV[:, 1:2], in_=SQF[:, 0:1024], axis=AXX, op=ALU.add), inc=(sV, 1))
                k.wait("s", sV, sV.n)
                k.s(lambda s: s.activation(out=SV[:, 1:2], in_=SV[:, 1:2], func=AF.Sqrt, bias=1e-5, scale=1.0 / D), inc=(sS, 1))
                k.wait("v", sS, sS.n)
                k.v(lambda v: v.reciprocal(out=SV[:, 2:3], in_=SV[:, 1:2]))
                k.v(lambda v: v.tensor_scalar(out=zt, in0=zt, scalar1=SV[:, 2:3], scalar2=None, op0=ALU.mult))
                k.v(lambda v: v.tensor_tensor(out=zt, in0=zt, in1=g1b, op=ALU.mult))
                k.v(lambda v: v.tensor_tensor(out=zt, in0=zt, in1=b1b, op=ALU.add))
                k.v(lambda v: v.tensor_reduce(out=SV[:, 3:4], in_=zt, axis=AXX, op=ALU.max, apply_absolute_value=True))
                k.v(lambda v: v.tensor_scalar(out=SV[:, 3:4], in0=SV[:, 3:4], scalar1=1e-5, scalar2=None, op0=ALU.max))
                k.v(lambda v: v.reciprocal(out=SV[:, 4:5], in_=SV[:, 3:4]))
                k.v(lambda v: v.tensor_scalar(out=SV[:, 4:5], in0=SV[:, 4:5], scalar1=127.0, scalar2=None, op0=ALU.mult))
                k.v(lambda v: v.reciprocal(out=SV[:, 5:6], in_=SV[:, 4:5]))
                k.v(lambda v: v.tensor_scalar(out=zt, in0=zt, scalar1=SV[:, 4:5], scalar2=0.5, op0=ALU.mult, op1=ALU.add))
                vfloor(zt, zt, 1024, SQF[:, 0:1024])
                k.v(lambda v: v.tensor_scalar(out=xq, in0=zt, scalar1=-128.0, scalar2=127.0, op0=ALU.max, op1=ALU.min), inc=(sV, 1))
                k.v(lambda v: v.tensor_tensor(out=SV[:, 9:10], in0=SV[:, 5:6], in1=wm1, op=ALU.mult), inc=(sV, 1))
                k.wait("t", sV, sV.n)
                for c in range(KC):
                    k.t(lambda t, c=c: t.transpose(PSB[:, 0:128], xq[:, 128 * c:128 * (c + 1)], identb), inc=(sT, 1))
                    k.wait("v", sT, sT.n)
                    k.v(lambda v, c=c: v.tensor_copy(out=xqf[:, 128 * c:128 * (c + 1)], in_=PSB[:, 0:128]), inc=(sV, 1))
                    k.wait("t", sV, sV.n)
                for mg in range(8):
                    for cc in range(4):
                        m = 4 * mg + cc
                        for c in range(KC):
                            k.t(lambda t, m=m, cc=cc, c=c: t.matmul(PS[0][:, 128 * cc:128 * (cc + 1)], W1Q[:, 4096 * c + 128 * m:4096 * c + 128 * (m + 1)], xqf[:, 128 * c:128 * (c + 1)], start=(c == 0 and cc == 0), stop=(c == KC - 1 and cc == 3)),
                                inc=(sT, 1) if (c == KC - 1 and cc == 3) else None)
                    k.wait("v", sT, sT.n)
                    k.v(lambda v: v.tensor_copy(out=OS1, in_=PS[0][:, 0:512]), inc=(sV, 1))
                    k.wait("t", sV, sV.n)
                    for cc in range(4):
                        k.t(lambda t, cc=cc: t.transpose(PS[1][:, 128 * cc:128 * (cc + 1)], OS1[:, 128 * cc:128 * (cc + 1)], identf), inc=(sT, 1) if cc == 3 else None)
                    k.wait("v", sT, sT.n)
                    k.v(lambda v, mg=mg: v.scalar_tensor_tensor(out=H1[:, 512 * mg:512 * (mg + 1)], in0=PS[1][:, 0:512], scalar=SV[:, 9:10], in1=bh1b[:, 512 * mg:512 * (mg + 1)], op0=ALU.mult, op1=ALU.add), inc=(sV, 1))
                    k.wait("t", sV, sV.n)
                k.wait("s", sV, sV.n)
                k.s(lambda s: s.activation(out=H1, in_=H1, func=AF.Gelu), inc=(sS, 1))
                k.wait("v", sS, sS.n)
                k.v(lambda v: v.tensor_reduce(out=SV[:, 0:1], in_=H1, axis=AXX, op=ALU.add))
                k.v(lambda v: v.tensor_scalar(out=SV[:, 0:1], in0=SV[:, 0:1], scalar1=1.0 / 4096.0, scalar2=None, op0=ALU.mult))
                k.v(lambda v: v.tensor_scalar(out=H1, in0=H1, scalar1=SV[:, 0:1], scalar2=None, op0=ALU.subtract), inc=(sV, 1))
                k.wait("s", sV, sV.n)
                k.s(lambda s: s.activation(out=SQF, in_=H1, func=AF.Square), inc=(sS, 1))
                k.wait("v", sS, sS.n)
                k.v(lambda v: v.tensor_reduce(out=SV[:, 1:2], in_=SQF, axis=AXX, op=ALU.add), inc=(sV, 1))
                k.wait("s", sV, sV.n)
                k.s(lambda s: s.activation(out=SV[:, 1:2], in_=SV[:, 1:2], func=AF.Sqrt, bias=1e-5, scale=1.0 / 4096.0), inc=(sS, 1))
                k.wait("v", sS, sS.n)
                k.v(lambda v: v.reciprocal(out=SV[:, 2:3], in_=SV[:, 1:2]))
                k.v(lambda v: v.tensor_scalar(out=H1, in0=H1, scalar1=SV[:, 2:3], scalar2=None, op0=ALU.mult))
                k.v(lambda v: v.tensor_tensor(out=H1, in0=H1, in1=g2b, op=ALU.mult), inc=(sV, 1))
                for part in range(4):
                    bcast_bf16(b2b, OB[0:1, 1024 * (part + 1):1024 * (part + 2)], 1024)
                    k.wait("v", sS, sS.n)
                    k.v(lambda v, part=part: v.tensor_tensor(out=H1[:, 1024 * part:1024 * (part + 1)], in0=H1[:, 1024 * part:1024 * (part + 1)], in1=b2b, op=ALU.add), inc=(sV, 1))
                    k.wait("s", sV, sV.n)
                k.v(lambda v: v.tensor_reduce(out=SV[:, 3:4], in_=H1, axis=AXX, op=ALU.max, apply_absolute_value=True))
                k.v(lambda v: v.tensor_scalar(out=SV[:, 3:4], in0=SV[:, 3:4], scalar1=1e-5, scalar2=None, op0=ALU.max))
                k.v(lambda v: v.reciprocal(out=SV[:, 4:5], in_=SV[:, 3:4]))
                k.v(lambda v: v.tensor_scalar(out=SV[:, 4:5], in0=SV[:, 4:5], scalar1=127.0, scalar2=None, op0=ALU.mult))
                k.v(lambda v: v.reciprocal(out=SV[:, 5:6], in_=SV[:, 4:5]))
                k.v(lambda v: v.tensor_scalar(out=H1, in0=H1, scalar1=SV[:, 4:5], scalar2=0.5, op0=ALU.mult, op1=ALU.add))
                for part in range(4):
                    sl = slice(1024 * part, 1024 * (part + 1))
                    vfloor(H1[:, sl], H1[:, sl], 1024, SQF[:, 0:1024])
                k.v(lambda v: v.tensor_scalar(out=xq2, in0=H1, scalar1=-128.0, scalar2=127.0, op0=ALU.max, op1=ALU.min), inc=(sV, 1))
                k.v(lambda v: v.tensor_tensor(out=SV[:, 9:10], in0=SV[:, 5:6], in1=wm2, op=ALU.mult), inc=(sV, 1))
                k.wait("t", sV, sV.n)
                for c in range(32):
                    k.t(lambda t, c=c: t.transpose(PSB[:, 0:128], xq2[:, 128 * c:128 * (c + 1)], identb), inc=(sT, 1))
                    k.wait("v", sT, sT.n)
                    k.v(lambda v, c=c: v.tensor_copy(out=xq2f[:, 128 * c:128 * (c + 1)], in_=PSB[:, 0:128]), inc=(sV, 1))
                    k.wait("t", sV, sV.n)
                for mg in range(2):
                    for cc in range(4):
                        m = 4 * mg + cc
                        Lw2 = next_ld()
                        wslot = W2Q[:, 0:4096]
                        tws2 = k.load(lambda wslot=wslot: wslot.rearrange("p (c n) -> p c n", n=128),
                                      lambda m=m: ap["w2q"].rearrange("(c p) n -> p c n", p=128)[:, :, 128 * m:128 * (m + 1)], Lw2)
                        k.wait("t", Lw2, tws2)
                        for c in range(32):
                            k.t(lambda t, c=c, cc=cc, wslot=wslot: t.matmul(PS[0][:, 128 * cc:128 * (cc + 1)], wslot[:, 128 * c:128 * (c + 1)], xq2f[:, 128 * c:128 * (c + 1)], start=(c == 0 and cc == 0), stop=(c == 31 and cc == 3)),
                                inc=(sT, 1) if (c == 31) else None)
                        k.wait("y", sT, sT.n)
                    k.wait("v", sT, sT.n)
                    k.v(lambda v: v.tensor_copy(out=OS1, in_=PS[0][:, 0:512]), inc=(sV, 1))
                    k.wait("t", sV, sV.n)
                    for cc in range(4):
                        k.t(lambda t, cc=cc: t.transpose(PS[1][:, 128 * cc:128 * (cc + 1)], OS1[:, 128 * cc:128 * (cc + 1)], identf), inc=(sT, 1) if cc == 3 else None)
                    k.wait("v", sT, sT.n)
                    k.v(lambda v, mg=mg: v.scalar_tensor_tensor(out=zt[:, 512 * mg:512 * (mg + 1)], in0=PS[1][:, 0:512], scalar=SV[:, 9:10], in1=BCP[:, 11264 + 512 * mg:11264 + 512 * (mg + 1)], op0=ALU.mult, op1=ALU.add), inc=(sV, 1))
                    k.wait("t", sV, sV.n)
                k.v(lambda v: v.tensor_tensor(out=zt, in0=zt, in1=bcg, op=ALU.mult))
                k.v(lambda v: v.tensor_tensor(out=xt, in0=xt, in1=zt, op=ALU.add), inc=(sV, 1))
                k.wait("g", sV, sV.n)
                S = next_st()
                ts = k.store(lambda row=row: ap["xs"][row:row + 128, :], lambda: xt, S)
                k.wait("v", S, ts)
                k.wait("y", S, ts)
        k.barrier(BAR)

    def bh2_bcast():
        Lb = next_ld()
        k.wait("y", sV, sV.n)
        tb = k.load(lambda: OO[0:1, 256:256 + 1024], lambda: ap["bh2"], Lb)
        k.wait("v", Lb, tb)
        k.v(lambda v: v.tensor_copy(out=OB[0:1, 0:1024], in_=OO[0:1, 256:256 + 1024]), inc=(sV, 1))
        k.wait("t", sV, sV.n)
        bcast_bf16(BCP[:, 11264:12288], OB[0:1, 0:1024], 1024)

    def ffn1_phase():
        xt = FP[:, 0:1024]
        yt = FP[:, 1024:2048]
        at = FP[:, 2048:3072]
        et = FP[:, 3072:4096]
        st_ = FP[:, 4096:5120]
        ut = FP[:, 5120:6144]
        cm = FP[:, 6144:7168]
        zt = FP[:, 7168:8192]
        OS1 = FP[:, 8192:8704]
        bc1 = FP[:, 10240:11264]
        bc2 = FP[:, 11264:12288]
        bcg = FP[:, 12288:13312]
        bfb = FP[:, 8704:9728]
        xq = XQb[:, 0:1024]
        xqf = XQb[:, 1024:2048]
        Lw = next_ld()
        tws = []
        for c in range(KC):
            tws.append(k.load(lambda c=c: W2Q[:, 1024 * c:1024 * (c + 1)] if False else W1Q[:, 1024 * c:1024 * (c + 1)], lambda c=c: ap["wf8"][128 * c:128 * (c + 1), :], Lw))
        Lb = next_ld()
        tb = k.load(lambda: OO[0:1, 256:256 + 1024], lambda: ap["bf8"], Lb)
        k.wait("t", Lb, tb)
        bcast_f32(bfb, OO[0:1, 256:256 + 1024], 1024)
        k.t(lambda t: t.matmul(PS[6][:, 0:1], onesf, OO[0:1, 144:145], start=True, stop=True), inc=(sT, 1))
        k.wait("v", sT, sT.n)
        k.v(lambda v: v.tensor_copy(out=SV[:, 12:13], in_=PS[6][:, 0:1]), inc=(sV, 1))
        k.wait("t", sV, sV.n)
        for c in range(KC):
            k.wait("t", Lw, tws[c])
        for b in range(B):
            Lr = next_ld()
            t2 = k.load(lambda: OO[0:1, 256:1280], lambda b=b: ap["mods"][B + b:B + b + 1, 4 * D:5 * D], Lr)
            k.wait("t", Lr, t2)
            bcast_f32(bc1, OO[0:1, 256:1280], 1024, bias=1.0)
            Lr = next_ld()
            t2 = k.load(lambda: OO[0:1, 256:1280], lambda b=b: ap["mods"][B + b:B + b + 1, 3 * D:4 * D], Lr)
            k.wait("t", Lr, t2)
            bcast_f32(bc2, OO[0:1, 256:1280], 1024)
            Lr = next_ld()
            t2 = k.load(lambda: OO[0:1, 256:1280], lambda b=b: ap["mods"][B + b:B + b + 1, 5 * D:6 * D], Lr)
            k.wait("t", Lr, t2)
            bcast_f32(bcg, OO[0:1, 256:1280], 1024)
            for tt in range(16):
                row = T * b + 128 * tt
                L = next_ld()
                k.wait("y", sV, sV.n)
                tw = k.load(lambda: xt, lambda row=row: ap["xs"][row:row + 128, :], L)
                k.wait("v", L, tw)
                k.v(lambda v: v.tensor_reduce(out=SV[:, 0:1], in_=xt, axis=AXX, op=ALU.add))
                k.v(lambda v: v.tensor_scalar(out=SV[:, 0:1], in0=SV[:, 0:1], scalar1=1.0 / D, scalar2=None, op0=ALU.mult))
                k.v(lambda v: v.tensor_scalar(out=zt, in0=xt, scalar1=SV[:, 0:1], scalar2=None, op0=ALU.subtract), inc=(sV, 1))
                k.wait("s", sV, sV.n)
                k.s(lambda s: s.activation(out=cm, in_=zt, func=AF.Square), inc=(sS, 1))
                k.wait("v", sS, sS.n)
                k.v(lambda v: v.tensor_reduce(out=SV[:, 1:2], in_=cm, axis=AXX, op=ALU.add), inc=(sV, 1))
                k.wait("s", sV, sV.n)
                k.s(lambda s: s.activation(out=SV[:, 1:2], in_=SV[:, 1:2], func=AF.Sqrt, bias=1e-5, scale=1.0 / D), inc=(sS, 1))
                k.wait("v", sS, sS.n)
                k.v(lambda v: v.reciprocal(out=SV[:, 2:3], in_=SV[:, 1:2]))
                k.v(lambda v: v.tensor_scalar(out=zt, in0=zt, scalar1=SV[:, 2:3], scalar2=None, op0=ALU.mult))
                k.v(lambda v: v.tensor_tensor(out=zt, in0=zt, in1=bc1, op=ALU.mult))
                k.v(lambda v: v.tensor_tensor(out=zt, in0=zt, in1=bc2, op=ALU.add))
                k.v(lambda v: v.tensor_reduce(out=SV[:, 3:4], in_=zt, axis=AXX, op=ALU.max, apply_absolute_value=True))
                k.v(lambda v: v.tensor_scalar(out=SV[:, 3:4], in0=SV[:, 3:4], scalar1=1e-12, scalar2=1.0 / 448.0, op0=ALU.max, op1=ALU.mult))
                k.v(lambda v: v.reciprocal(out=SV[:, 4:5], in_=SV[:, 3:4]))
                k.v(lambda v: v.tensor_scalar(out=yt, in0=zt, scalar1=SV[:, 4:5], scalar2=None, op0=ALU.mult))
                k.v(lambda v: v.tensor_scalar(out=at, in0=yt, scalar1=-1.0, scalar2=None, op0=ALU.mult))
                k.v(lambda v: v.tensor_tensor(out=at, in0=at, in1=yt, op=ALU.max))
                k.v(lambda v: v.tensor_scalar(out=at, in0=at, scalar1=1e-30, scalar2=None, op0=ALU.max), inc=(sV, 1))
                k.wait("s", sV, sV.n)
                k.s(lambda s: s.activation(out=at, in_=at, func=AF.Ln), inc=(sS, 1))
                k.wait("v", sS, sS.n)
                k.v(lambda v: v.tensor_scalar(out=et, in0=at, scalar1=1.0 / LN2C, scalar2=None, op0=ALU.mult))
                vfloor(et, et, 1024, cm)
                k.v(lambda v: v.tensor_scalar(out=et, in0=et, scalar1=-6.0, scalar2=8.0, op0=ALU.max, op1=ALU.min), inc=(sV, 1))
                k.wait("s", sV, sV.n)
                k.s(lambda s: s.activation(out=st_, in_=et, func=AF.Exp, bias=-3.0 * LN2C, scale=LN2C), inc=(sS, 1))
                k.wait("v", sS, sS.n)
                k.v(lambda v: v.reciprocal(out=ut, in_=st_))
                k.v(lambda v: v.tensor_tensor(out=ut, in0=yt, in1=ut, op=ALU.mult))
                k.v(lambda v: v.tensor_scalar(out=ut, in0=ut, scalar1=0.5, scalar2=None, op0=ALU.add))
                vfloor(ut, ut, 1024, cm)
                k.v(lambda v: v.tensor_tensor(out=xq, in0=ut, in1=st_, op=ALU.mult), inc=(sV, 1))
                k.v(lambda v: v.tensor_tensor(out=SV[:, 9:10], in0=SV[:, 3:4], in1=SV[:, 12:13], op=ALU.mult), inc=(sV, 1))
                k.wait("t", sV, sV.n)
                for c in range(KC):
                    k.t(lambda t, c=c: t.transpose(PSB[:, 0:128], xq[:, 128 * c:128 * (c + 1)], identb), inc=(sT, 1))
                    k.wait("v", sT, sT.n)
                    k.v(lambda v, c=c: v.tensor_copy(out=xqf[:, 128 * c:128 * (c + 1)], in_=PSB[:, 0:128]), inc=(sV, 1))
                    k.wait("t", sV, sV.n)
                for mg in range(2):
                    for cc in range(4):
                        m = 4 * mg + cc
                        for c in range(KC):
                            k.t(lambda t, m=m, cc=cc, c=c: t.matmul(PS[0][:, 128 * cc:128 * (cc + 1)], W1Q[:, 1024 * c + 128 * m:1024 * c + 128 * (m + 1)], xqf[:, 128 * c:128 * (c + 1)], start=(c == 0 and cc == 0), stop=(c == KC - 1 and cc == 3)),
                                inc=(sT, 1) if (c == KC - 1 and cc == 3) else None)
                    k.wait("v", sT, sT.n)
                    k.v(lambda v: v.tensor_copy(out=OS1, in_=PS[0][:, 0:512]), inc=(sV, 1))
                    k.wait("t", sV, sV.n)
                    for cc in range(4):
                        k.t(lambda t, cc=cc: t.transpose(PS[1][:, 128 * cc:128 * (cc + 1)], OS1[:, 128 * cc:128 * (cc + 1)], identf), inc=(sT, 1) if cc == 3 else None)
                    k.wait("v", sT, sT.n)
                    k.v(lambda v, mg=mg: v.scalar_tensor_tensor(out=zt[:, 512 * mg:512 * (mg + 1)], in0=PS[1][:, 0:512], scalar=SV[:, 9:10], in1=bfb[:, 512 * mg:512 * (mg + 1)], op0=ALU.mult, op1=ALU.add), inc=(sV, 1))
                    k.wait("t", sV, sV.n)
                k.v(lambda v: v.tensor_tensor(out=zt, in0=zt, in1=bcg, op=ALU.mult))
                k.v(lambda v: v.tensor_tensor(out=xt, in0=xt, in1=zt, op=ALU.add), inc=(sV, 1))
                k.wait("g", sV, sV.n)
                S = next_st()
                ts = k.store(lambda row=row: y_out[row:row + 128, :], lambda: xt, S)
                k.wait("v", S, ts)
                k.wait("y", S, ts)
        k.barrier(BAR)

    import os
    KPH = int(os.environ.get("KPH", "99"))
    bh2_bcast()
    for lay in range(2):
        if KPH >= (1 if lay == 0 else 3):
            for b in range(B):
                ln_mod_phase(lay, b, 0)
                attn_phase(lay, b)
                outproj_phase(lay, b)
        if lay == 0:
            if KPH >= 2:
                ffn0_phase()
        else:
            if KPH >= 4:
                ffn1_phase()

    for s_ in sts:
        k.wait("g", s_, s_.n)

    with nc.Block() as block:
        @block.tensor
        def _(t):
            for f in k.q["t"]:
                f(t)

        @block.vector
        def _(v):
            for f in k.q["v"]:
                f(v)

        @block.scalar
        def _(s):
            for f in k.q["s"]:
                f(s)

        @block.gpsimd
        def _(g):
            for f in k.q["g"]:
                f(g)

        @block.sync
        def _(y):
            for f in k.q["y"]:
                f(y)

    es.close()
    return nc


_NC_CACHE = [None]


def kernel(**inputs):
    n = 8
    xfull = np.ascontiguousarray(np.asarray(inputs["L0_attn_x"], dtype=np.float32))
    maskc = np.zeros((128, 4 * 512), dtype=np.float64)
    for m in range(4):
        i = np.arange(128)[:, None]
        j = np.arange(512)[None, :]
        maskc[:, 512 * m:512 * (m + 1)] = np.where(i + 128 * m > j, -1e10, 0.0)
    maskc = maskc.astype(ml_dtypes.bfloat16)
    ramp = (np.arange(128)[:, None] + 128 * np.arange(16)[None, :] - 2047).astype(np.float32)
    common = {
        "w_qkv": np.asarray(inputs["L0_attn_w_qkv"], np.float32),
        "w_out": np.asarray(inputs["L0_attn_w_out"], np.float32),
        "b_out": np.asarray(inputs["L0_attn_b_out"], np.float32).reshape(1, D),
        "g1r": np.asarray(inputs["L0_ffn_ln1_g"], np.float32).reshape(1, D),
        "b1r": np.asarray(inputs["L0_ffn_ln1_b"], np.float32).reshape(1, D),
        "w1": np.asarray(inputs["L0_ffn_w1"], np.float32),
        "bh1": np.asarray(inputs["L0_ffn_b1"], np.float32).reshape(1, 4 * D),
        "g2r": np.asarray(inputs["L0_ffn_ln2_g"], np.float32).reshape(1, 4 * D),
        "b2r": np.asarray(inputs["L0_ffn_ln2_b"], np.float32).reshape(1, 4 * D),
        "w2": np.asarray(inputs["L0_ffn_w2"], np.float32),
        "bh2": np.asarray(inputs["L0_ffn_b2"], np.float32).reshape(1, D),
        "wmod0": np.asarray(inputs["L0_wmod"], np.float32),
        "Wq": np.asarray(inputs["L1_attn_Wq"], np.float32),
        "bq": np.asarray(inputs["L1_attn_bq"], np.float32).reshape(D, 1),
        "Wk": np.asarray(inputs["L1_attn_Wk"], np.float32),
        "bk": np.asarray(inputs["L1_attn_bk"], np.float32).reshape(D, 1),
        "Wv": np.asarray(inputs["L1_attn_Wv"], np.float32),
        "bv": np.asarray(inputs["L1_attn_bv"], np.float32).reshape(D, 1),
        "Wo": np.asarray(inputs["L1_attn_Wo"], np.float32),
        "bo": np.asarray(inputs["L1_attn_bo"], np.float32).reshape(1, D),
        "slopes": np.asarray(inputs["L1_attn_slopes"], np.float32).reshape(1, 16),
        "w8": np.asarray(inputs["L1_ffn_weight"]).view(np.uint8),
        "wsc": np.asarray(inputs["L1_ffn_weight_scale"], np.float32).reshape(1, 1),
        "bf8": np.asarray(inputs["L1_ffn_bias"], np.float32).reshape(1, D),
        "wmod1": np.asarray(inputs["L1_wmod"], np.float32),
        "identf": np.eye(128, dtype=np.float32),
        "identb": np.eye(128, dtype=ml_dtypes.bfloat16),
        "maskc": maskc,
        "ramp": ramp,
    }
    c0 = np.asarray(inputs["L0_c"], np.float32)
    c1 = np.asarray(inputs["L1_c"], np.float32)
    in_maps = []
    for i in range(n):
        m = dict(common)
        m["x_in"] = np.ascontiguousarray(xfull[2 * i:2 * i + 2].reshape(B * T, D))
        m["c0"] = np.ascontiguousarray(c0[2 * i:2 * i + 2])
        m["c1"] = np.ascontiguousarray(c1[2 * i:2 * i + 2])
        in_maps.append(m)
    if _NC_CACHE[0] is None:
        _NC_CACHE[0] = build_program()
    nc = _NC_CACHE[0]
    res = run_bass_kernel_spmd(nc, in_maps, core_ids=list(range(n)))
    out = np.concatenate([r["y"].reshape(1, B, T, D) for r in res.results], axis=0)
    return np.ascontiguousarray(out.reshape(16, T, D)).astype(np.float32)


# version 09
# speedup vs baseline: 1.8728x; 1.0277x over previous
import numpy as np
import ml_dtypes
import concourse.bass as bass
import concourse.mybir as mybir
from concourse.bass_utils import run_bass_kernel_spmd

F32 = mybir.dt.float32
BF16 = mybir.dt.bfloat16
I32 = mybir.dt.int32
U8 = mybir.dt.uint8
AF = mybir.ActivationFunctionType
ALU = mybir.AluOpType
AXX = mybir.AxisListType.X

D = 1024
T = 2048
B = 2
NP = 8
KC = 8
QB = 4
LN2C = float(np.log(2.0))


class Sem:
    def __init__(self, name):
        self.name, self.n, self.h = name, 0, None


class K:
    def __init__(self):
        self.q = {"t": [], "v": [], "s": [], "g": [], "y": []}
        self.sems = []
        self.nbar = 0

    def sem(self, name):
        s = Sem(name)
        self.sems.append(s)
        return s

    def op(self, eng, fn, inc=None, drain=False):
        def run(e):
            ins = fn(e)
            if inc is not None:
                ins.then_inc(inc[0].h, inc[1])
            if drain:
                e.drain()
        self.q[eng].append(run)
        if inc is not None:
            inc[0].n += inc[1]

    def wait(self, eng, s, val):
        self.q[eng].append(lambda e, s=s, val=val: e.wait_ge(s.h, val))

    def inc(self, eng, s, kk=1):
        s.n += kk
        self.q[eng].append(lambda e, s=s, kk=kk: e.sem_inc(s.h, kk))

    def barrier(self, bar):
        self.nbar += 1
        tgt = 5 * self.nbar
        for eng in ("t", "v", "s", "g", "y"):
            self.inc(eng, bar)
            self.wait(eng, bar, tgt)

    def t(self, fn, inc=None):
        self.op("t", fn, inc)

    def v(self, fn, inc=None):
        self.op("v", fn, inc, drain=True)

    def s(self, fn, inc=None):
        self.op("s", fn, inc, drain=True)

    def load(self, dst_fn, src_fn, sem):
        prev = sem.n
        def run(e, d=dst_fn, s=src_fn, m=sem, prev=prev):
            e.wait_ge(m.h, prev)
            e.dma_start(out=d(), in_=s()).then_inc(m.h, 16)
        self.q["y"].append(run)
        sem.n += 16
        return sem.n

    def store(self, dst_fn, src_fn, sem):
        prev = sem.n
        def run(e, d=dst_fn, s=src_fn, m=sem, prev=prev):
            e.wait_ge(m.h, prev)
            e.dma_start(out=d(), in_=s()).then_inc(m.h, 16)
        self.q["g"].append(run)
        sem.n += 16
        return sem.n


def build_program():
    nc = bass.Bass("TRN2", target_bir_lowering=False)
    ap = {}

    def din(name, shape, dt):
        ap[name] = nc.dram_tensor(name, list(shape), dt, kind="ExternalInput").ap()

    din("x_in", (B * T, D), F32)
    din("w_qkv", (D, 3 * D), F32)
    din("w_out", (D, D), F32)
    din("b_out", (1, D), F32)
    din("g1r", (1, D), F32)
    din("b1r", (1, D), F32)
    din("w1", (4 * D, D), F32)
    din("bh1", (1, 4 * D), F32)
    din("g2r", (1, 4 * D), F32)
    din("b2r", (1, 4 * D), F32)
    din("w2", (D, 4 * D), F32)
    din("bh2", (1, D), F32)
    din("c0", (B, D), F32)
    din("wmod0", (D, 6 * D), F32)
    din("Wq", (D, D), F32)
    din("bq", (D, 1), F32)
    din("Wk", (D, D), F32)
    din("bk", (D, 1), F32)
    din("Wv", (D, D), F32)
    din("bv", (D, 1), F32)
    din("Wo", (D, D), F32)
    din("bo", (1, D), F32)
    din("slopes", (1, 16), F32)
    din("w8", (D, D), U8)
    din("wsc", (1, 1), F32)
    din("bf8", (1, D), F32)
    din("c1", (B, D), F32)
    din("wmod1", (D, 6 * D), F32)
    din("identf", (128, 128), F32)
    din("identb", (128, 128), BF16)
    din("maskc", (128, 4 * 512), BF16)
    din("ramp", (128, 16), F32)

    y_out = nc.dram_tensor("y", [B * T, D], F32, kind="ExternalOutput").ap()

    def scr(name, shape, dt):
        ap[name] = nc.dram_tensor(name, list(shape), dt, kind="Internal").ap()

    scr("xs", (B * T, D), F32)
    scr("hfm", (D, T), F32)
    scr("ctx", (D, T), F32)
    scr("wqf", (D, D), F32)
    scr("wkf", (D, D), F32)
    scr("wvf", (D, D), F32)
    scr("wof", (D, D), F32)
    scr("w1q", (D, 4 * D), BF16)
    scr("w2q", (D, 4 * D), BF16)
    scr("wf8", (D, D), BF16)
    scr("mods", (2 * B, 6 * D), F32)

    k = K()
    lds = [k.sem(f"ld{i}") for i in range(8)]
    sts = [k.sem(f"st{i}") for i in range(4)]
    idx = {"l": 0, "s": 0}
    sT, sV, sS, BAR = k.sem("sT"), k.sem("sV"), k.sem("sS"), k.sem("bar")

    def next_ld():
        idx["l"] = (idx["l"] + 1) % 8
        return lds[idx["l"]]

    def next_st():
        idx["s"] = (idx["s"] + 1) % 4
        return sts[idx["s"]]

    from contextlib import ExitStack
    es = ExitStack()

    def sb(name, shape, dt):
        return es.enter_context(nc.sbuf_tensor(name, list(shape), dt))

    FP = sb("FP", [128, 13824], F32)
    W1Q = sb("W1Q", [128, 32768], BF16)
    W2Q = sb("W2Q", [128, 4096], BF16)
    XQb = sb("XQb", [128, 10240], BF16)
    BCP = sb("BCP", [128, 12288], BF16)
    CB = sb("CBc", [128, 2304], BF16)
    CF = sb("CFc", [128, 160], F32)
    SV = sb("SVV", [128, 640], F32)
    OO = sb("OOr", [1, 1664], F32)
    OB = sb("OBr", [1, 5248], BF16)
    U8B = sb("U8B", [128, 1024], U8)
    IT32 = sb("IT32", [128, 1024], I32)
    ONB = sb("ONB", [1, 128], BF16)
    CST = sb("CST", [128, 16], F32)
    LNV = sb("LNV", [128, 512], F32)
    ONES128 = sb("ONES128", [128, 128], F32)
    PS = [es.enter_context(nc.psum_tensor(f"PS{i}", [128, 512], F32)) for i in range(7)]
    PSB = es.enter_context(nc.psum_tensor("PSB", [128, 1024], BF16))
    for s_ in k.sems:
        s_.h = es.enter_context(nc.semaphore(s_.name))

    cvals = [-7.0 * LN2C, LN2C, -3.0 * LN2C, 1e-5, 1.0 / 1024.0, 1.0 / 4096.0,
             0.125, -1.0, 1.0, 0.0, 0.5, 60.0]
    for ci_, val in enumerate(cvals):
        nc.const_aps.register(F32, val, CST[:, ci_:ci_ + 1])

    identf = CF[:, 0:128]
    rampc = CF[:, 128:144]
    identb = CB[:, 0:128]
    maskb = CB[:, 128:2176]
    onesf = OO[0:1, 0:128]
    onescol = SV[:, 15:16]
    ALB0 = 16

    def bcast_f32(dst, row_ap, width, bias=0.0, scale=1.0):
        for i in range((width + 511) // 512):
            w = min(512, width - 512 * i)
            k.t(lambda t, i=i, w=w, row_ap=row_ap: t.matmul(PS[6][:, 0:w], onesf, row_ap[0:1, 512 * i:512 * i + w], start=True, stop=True), inc=(sT, 1))
            k.wait("s", sT, sT.n)
            k.s(lambda s, i=i, w=w, dst=dst: s.activation(out=dst[:, 512 * i:512 * i + w], in_=PS[6][:, 0:w], func=AF.Copy, bias=bias, scale=scale), inc=(sS, 1))
            k.wait("t", sS, sS.n)
            k.wait("y", sT, sT.n)

    def bcast_bf16(dst, row_bf, width):
        for i in range((width + 511) // 512):
            w = min(512, width - 512 * i)
            k.t(lambda t, i=i, w=w, row_bf=row_bf: t.matmul(PS[6][:, 0:w], ONB[0:1, :], row_bf[0:1, 512 * i:512 * i + w], start=True, stop=True), inc=(sT, 1))
            k.wait("s", sT, sT.n)
            k.s(lambda s, i=i, w=w, dst=dst: s.activation(out=dst[:, 512 * i:512 * i + w], in_=PS[6][:, 0:w], func=AF.Copy), inc=(sS, 1))
            k.wait("t", sS, sS.n)
            k.wait("y", sT, sT.n)

    def vfloor(dst, src, width, gbuf):
        k.v(lambda v: v.tensor_copy(out=IT32[:, 0:width], in_=src))
        k.v(lambda v: v.tensor_copy(out=gbuf[:, 0:width], in_=IT32[:, 0:width]))
        k.v(lambda v: v.tensor_tensor(out=dst, in0=gbuf[:, 0:width], in1=src, op=ALU.is_gt))
        k.v(lambda v: v.tensor_tensor(out=dst, in0=gbuf[:, 0:width], in1=dst, op=ALU.subtract))

    L = next_ld()
    k.load(lambda: identf, lambda: ap["identf"], L)
    k.load(lambda: identb, lambda: ap["identb"], L)
    k.load(lambda: maskb, lambda: ap["maskc"], L)
    k.load(lambda: rampc, lambda: ap["ramp"], L)
    k.load(lambda: OO[0:1, 128:144], lambda: ap["slopes"], L)
    k.load(lambda: OO[0:1, 144:145], lambda: ap["wsc"], L)
    tA = L.n
    for eng in ("t", "v", "s"):
        k.wait(eng, L, tA)
    k.v(lambda v: v.memset(onesf, 1.0))
    k.v(lambda v: v.memset(ONES128[:], 1.0))
    for ci_, val in enumerate(cvals):
        k.v(lambda v, ci_=ci_, val=val: v.memset(CST[:, ci_:ci_ + 1], val))
    k.v(lambda v: v.memset(ONB[:], 1.0))
    k.v(lambda v: v.memset(onescol, 1.0))
    k.v(lambda v: v.memset(SV[:, 13:15], 0.0))
    k.v(lambda v: v.memset(SV[0:64, 13:14], 1.0))
    k.v(lambda v: v.memset(SV[64:128, 14:15], 1.0), inc=(sV, 1))
    k.wait("t", sV, sV.n)
    k.barrier(BAR)

    cw = FP[:, 0:1024]
    cts = FP[:, 1024:1024 + B * KC]
    wt0 = FP[:, 2048:2560]
    wt1 = FP[:, 2560:3072]
    mst = FP[:, 3072:3584]
    mst2 = FP[:, 3584:4096]
    for lay in range(2):
        L = next_ld()
        k.wait("y", sT, sT.n)
        tw = k.load(lambda: cw[0:B, :], lambda lay=lay: ap["c0" if lay == 0 else "c1"], L)
        k.wait("s", L, tw)
        k.s(lambda s: s.activation(out=cw[0:B, :], in_=cw[0:B, :], func=AF.Silu), inc=(sS, 1))
        k.wait("t", sS, sS.n)
        for c in range(KC):
            k.t(lambda t, c=c: t.transpose(PS[6][:, 0:B], cw[0:B, 128 * c:128 * (c + 1)], identf[0:B, 0:B]), inc=(sT, 1))
            k.wait("v", sT, sT.n)
            k.v(lambda v, c=c: v.tensor_copy(out=cts[:, B * c:B * (c + 1)], in_=PS[6][:, 0:B]), inc=(sV, 1))
            k.wait("t", sV, sV.n)
        WMB = FP[:, 4224:8320]
        for blk in range(12):
            wm = "wmod0" if lay == 0 else "wmod1"
            Lw = next_ld()
            k.wait("y", sT, sT.n)
            tw = k.load(lambda: WMB.rearrange("p (c n) -> p c n", n=512),
                        lambda blk=blk, wm=wm: ap[wm].rearrange("(c p) n -> p c n", p=128)[:, :, 512 * blk:512 * (blk + 1)], Lw)
            k.wait("t", Lw, tw)
            for c in range(KC):
                k.t(lambda t, c=c: t.matmul(PS[6][0:B, 0:512], cts[:, B * c:B * (c + 1)], WMB[:, 512 * c:512 * (c + 1)], start=(c == 0), stop=(c == KC - 1)), inc=(sT, 1))
            k.wait("v", sT, sT.n)
            j = blk // 2
            if j in (2, 5):
                k.v(lambda v: v.tensor_scalar(out=mst[0:B, :], in0=PS[6][0:B, 0:512], scalar1=0.0, scalar2=None, op0=ALU.is_ge))
                k.v(lambda v: v.tensor_scalar(out=mst[0:B, :], in0=mst[0:B, :], scalar1=2.0, scalar2=-1.0, op0=ALU.mult, op1=ALU.add))
                k.v(lambda v: v.tensor_scalar(out=mst2[0:B, :], in0=PS[6][0:B, 0:512], scalar1=-1.0, scalar2=None, op0=ALU.mult))
                k.v(lambda v: v.tensor_tensor(out=mst2[0:B, :], in0=mst2[0:B, :], in1=PS[6][0:B, 0:512], op=ALU.max))
                k.v(lambda v: v.tensor_scalar(out=mst2[0:B, :], in0=mst2[0:B, :], scalar1=0.25, scalar2=None, op0=ALU.max))
                k.v(lambda v: v.tensor_tensor(out=mst[0:B, :], in0=mst[0:B, :], in1=mst2[0:B, :], op=ALU.mult), inc=(sV, 1))
            else:
                k.v(lambda v: v.tensor_copy(out=mst[0:B, :], in_=PS[6][0:B, 0:512]), inc=(sV, 1))
            k.wait("t", sV, sV.n)
            k.wait("g", sV, sV.n)
            S = next_st()
            ts = k.store(lambda blk=blk, lay=lay: ap["mods"][B * lay:B * (lay + 1), 512 * blk:512 * (blk + 1)], lambda: mst[0:B, :], S)
            k.wait("v", S, ts)
    k.barrier(BAR)

    wload = FP[:, 0:1024]
    wy = FP[:, 1024:2048]
    wg = FP[:, 2048:3072]
    wc = FP[:, 3072:4096]
    qb = XQb[:, 0:1024]
    tstage = XQb[:, 1024:1152]
    fstage = FP[:, 4096:4224]

    def wquant(wname, dst, nchunk, width, n_elem, slab=False):
        k.v(lambda v: v.memset(SV[:, 3:4], 0.0), inc=(sV, 1))
        nsub = width // 1024
        for ch in range(nchunk):
            for sbk in range(nsub):
                L = next_ld()
                tw = k.load(lambda: wload, lambda ch=ch, sbk=sbk, wname=wname: ap[wname][128 * ch:128 * (ch + 1), 1024 * sbk:1024 * (sbk + 1)], L)
                k.wait("v", L, tw)
                k.v(lambda v: v.tensor_reduce(out=SV[:, 4:5], in_=wload, axis=AXX, op=ALU.add, apply_absolute_value=True))
                k.v(lambda v: v.tensor_tensor(out=SV[:, 3:4], in0=SV[:, 3:4], in1=SV[:, 4:5], op=ALU.add), inc=(sV, 1))
                k.wait("y", sV, sV.n)
        k.wait("t", sV, sV.n)
        k.t(lambda t: t.matmul(PS[6][0:1, 0:1], onescol, SV[:, 3:4], start=True, stop=True), inc=(sT, 1))
        k.wait("v", sT, sT.n)
        k.v(lambda v: v.tensor_scalar(out=OO[0:1, 150:151], in0=PS[6][0:1, 0:1], scalar1=1.0 / n_elem, scalar2=1e-5, op0=ALU.mult, op1=ALU.max))
        k.v(lambda v: v.reciprocal(out=OO[0:1, 151:152], in_=OO[0:1, 150:151]), inc=(sV, 1))
        k.wait("t", sV, sV.n)
        k.t(lambda t: t.matmul(PS[6][:, 0:1], onesf, OO[0:1, 151:152], start=True, stop=True), inc=(sT, 1))
        k.wait("v", sT, sT.n)
        k.v(lambda v: v.tensor_copy(out=SV[:, 8:9], in_=PS[6][:, 0:1]), inc=(sV, 1))
        k.wait("t", sV, sV.n)
        for ch in range(nchunk):
            for sbk in range(nsub):
                L = next_ld()
                tw = k.load(lambda: wload, lambda ch=ch, sbk=sbk, wname=wname: ap[wname][128 * ch:128 * (ch + 1), 1024 * sbk:1024 * (sbk + 1)], L)
                k.wait("v", L, tw)
                k.wait("v", sT, sT.n)
                k.v(lambda v: v.tensor_scalar(out=wy, in0=wload, scalar1=SV[:, 8:9], scalar2=0.5, op0=ALU.mult, op1=ALU.add))
                vfloor(wy, wy, 1024, wg)
                k.v(lambda v: v.tensor_scalar(out=qb, in0=wy, scalar1=-1.0, scalar2=1.0, op0=ALU.max, op1=ALU.min), inc=(sV, 1))
                k.wait("t", sV, sV.n)
                k.wait("y", sV, sV.n)
                for c in range(8):
                    k.t(lambda t, c=c: t.transpose(PSB[:, 0:128], qb[:, 128 * c:128 * (c + 1)], identb), inc=(sT, 1))
                    k.wait("s", sT, sT.n)
                    k.s(lambda s: s.activation(out=tstage, in_=PSB[:, 0:128], func=AF.Copy), inc=(sS, 1))
                    k.wait("t", sS, sS.n)
                    k.wait("g", sS, sS.n)
                    S = next_st()
                    if slab:
                        g = 8 * sbk + c
                        ts = k.store(lambda ch=ch, g=g, dst=dst: dst[128 * ch:128 * (ch + 1), 128 * g:128 * (g + 1)], lambda: tstage, S)
                    else:
                        ts = k.store(lambda ch=ch, sbk=sbk, c=c, dst=dst: dst[1024 * sbk + 128 * c:1024 * sbk + 128 * (c + 1), 128 * ch:128 * (ch + 1)], lambda: tstage, S)
                    k.wait("s", S, ts)

    wquant("w1", ap["w1q"], 32, 1024, 4096.0 * 1024.0)
    wquant("w2", ap["w2q"], 8, 4096, 1024.0 * 4096.0, slab=True)

    for wname, dname in (("Wq", "wqf"), ("Wk", "wkf"), ("Wv", "wvf"), ("Wo", "wof")):
        for ch in range(KC):
            L = next_ld()
            tw = k.load(lambda: wload, lambda ch=ch, wname=wname: ap[wname][128 * ch:128 * (ch + 1), 0:1024], L)
            k.wait("t", L, tw)
            for c in range(KC):
                k.t(lambda t, c=c: t.transpose(PS[6][:, 0:128], wload[:, 128 * c:128 * (c + 1)], identf), inc=(sT, 1))
                k.wait("v", sT, sT.n)
                k.v(lambda v: v.tensor_copy(out=fstage, in_=PS[6][:, 0:128]), inc=(sV, 1))
                k.wait("t", sV, sV.n)
                k.wait("y", sV, sV.n)
                k.wait("g", sV, sV.n)
                S = next_st()
                ts = k.store(lambda ch=ch, c=c, dname=dname: ap[dname][128 * c:128 * (c + 1), 128 * ch:128 * (ch + 1)], lambda: fstage, S)
                k.wait("v", S, ts)

    for ch in range(KC):
        L = next_ld()
        tw = k.load(lambda: U8B[:, 0:1024], lambda ch=ch: ap["w8"][128 * ch:128 * (ch + 1), :], L)
        k.wait("v", L, tw)
        k.wait("v", sT, sT.n)
        u, sg, e_, m_ = wload, wy, wg, wc
        k.v(lambda v: v.tensor_copy(out=u, in_=U8B[:, 0:1024]))
        k.v(lambda v: v.tensor_scalar(out=sg, in0=u, scalar1=128.0, scalar2=None, op0=ALU.is_ge))
        k.v(lambda v: v.scalar_tensor_tensor(out=u, in0=sg, scalar=-128.0, in1=u, op0=ALU.mult, op1=ALU.add))
        k.v(lambda v: v.tensor_scalar(out=e_, in0=u, scalar1=0.125, scalar2=None, op0=ALU.mult))
        vfloor(e_, e_, 1024, m_)
        k.v(lambda v: v.scalar_tensor_tensor(out=m_, in0=e_, scalar=-8.0, in1=u, op0=ALU.mult, op1=ALU.add))
        k.v(lambda v: v.tensor_scalar(out=u, in0=e_, scalar1=1.0, scalar2=None, op0=ALU.is_ge))
        k.v(lambda v: v.scalar_tensor_tensor(out=m_, in0=m_, scalar=0.125, in1=u, op0=ALU.mult, op1=ALU.add))
        k.v(lambda v: v.tensor_scalar(out=e_, in0=e_, scalar1=1.0, scalar2=None, op0=ALU.max), inc=(sV, 1))
        k.wait("s", sV, sV.n)
        k.s(lambda s: s.activation(out=e_, in_=e_, func=AF.Exp, bias=-7.0 * LN2C, scale=LN2C), inc=(sS, 1))
        k.wait("v", sS, sS.n)
        k.v(lambda v: v.tensor_scalar(out=sg, in0=sg, scalar1=-2.0, scalar2=1.0, op0=ALU.mult, op1=ALU.add))
        k.v(lambda v: v.tensor_tensor(out=m_, in0=m_, in1=e_, op=ALU.mult))
        k.v(lambda v: v.tensor_tensor(out=qb, in0=m_, in1=sg, op=ALU.mult), inc=(sV, 1))
        k.wait("t", sV, sV.n)
        k.wait("y", sV, sV.n)
        for c in range(KC):
            k.t(lambda t, c=c: t.transpose(PSB[:, 0:128], qb[:, 128 * c:128 * (c + 1)], identb), inc=(sT, 1))
            k.wait("s", sT, sT.n)
            k.s(lambda s: s.activation(out=tstage, in_=PSB[:, 0:128], func=AF.Copy), inc=(sS, 1))
            k.wait("t", sS, sS.n)
            k.wait("g", sS, sS.n)
            S = next_st()
            ts = k.store(lambda ch=ch, c=c: ap["wf8"][128 * c:128 * (c + 1), 128 * ch:128 * (ch + 1)], lambda: tstage, S)
            k.wait("s", S, ts)

    for h in range(16):
        k.t(lambda t, h=h: t.matmul(PS[6][:, 0:1], onesf, OO[0:1, 128 + h:129 + h], start=True, stop=True), inc=(sT, 1))
        k.wait("v", sT, sT.n)
        k.v(lambda v: v.tensor_copy(out=SV[:, 9:10], in_=PS[6][:, 0:1]), inc=(sV, 1))
        k.wait("t", sV, sV.n)
        k.v(lambda v, h=h: v.tensor_scalar(out=SV[:, ALB0 + 16 * h:ALB0 + 16 * (h + 1)], in0=rampc, scalar1=SV[:, 9:10], scalar2=None, op0=ALU.mult), inc=(sV, 1))
    k.barrier(BAR)

    def ln_mod_phase(lay, b, half):
        xt = FP[:, 0:1024]
        ht = FP[:, 1024:2048]
        sq = FP[:, 2048:3072]
        bc1 = FP[:, 3072:4096]
        bc2 = FP[:, 4096:5120]
        hs2 = FP[:, 5120:5632]
        j0 = 1 if half == 0 else 4
        Lr = next_ld()
        t2 = k.load(lambda: OO[0:1, 256:256 + 1024], lambda: ap["mods"][B * lay + b:B * lay + b + 1, D * j0:D * (j0 + 1)], Lr)
        k.wait("t", Lr, t2)
        bcast_f32(bc1, OO[0:1, 256:256 + 1024], 1024, bias=1.0)
        Lr = next_ld()
        t2 = k.load(lambda: OO[0:1, 256:256 + 1024], lambda: ap["mods"][B * lay + b:B * lay + b + 1, D * (j0 - 1):D * j0], Lr)
        k.wait("t", Lr, t2)
        bcast_f32(bc2, OO[0:1, 256:256 + 1024], 1024)
        src = ap["x_in"] if (lay == 0 and half == 0) else ap["xs"]
        for tt in range(16):
            row = T * b + 128 * tt
            L = next_ld()
            k.wait("y", sV, sV.n)
            tw = k.load(lambda: xt, lambda row=row, src=src: src[row:row + 128, :], L)
            k.wait("v", L, tw)
            k.v(lambda v: v.tensor_reduce(out=SV[:, 0:1], in_=xt, axis=AXX, op=ALU.add))
            k.v(lambda v: v.tensor_scalar(out=SV[:, 0:1], in0=SV[:, 0:1], scalar1=1.0 / D, scalar2=None, op0=ALU.mult))
            k.v(lambda v: v.tensor_scalar(out=ht, in0=xt, scalar1=SV[:, 0:1], scalar2=None, op0=ALU.subtract), inc=(sV, 1))
            k.wait("s", sV, sV.n)
            k.s(lambda s: s.activation(out=sq, in_=ht, func=AF.Square), inc=(sS, 1))
            k.wait("v", sS, sS.n)
            k.v(lambda v: v.tensor_reduce(out=SV[:, 1:2], in_=sq, axis=AXX, op=ALU.add), inc=(sV, 1))
            k.wait("s", sV, sV.n)
            k.s(lambda s: s.activation(out=SV[:, 1:2], in_=SV[:, 1:2], func=AF.Sqrt, bias=1e-5, scale=1.0 / D), inc=(sS, 1))
            k.wait("v", sS, sS.n)
            k.v(lambda v: v.reciprocal(out=SV[:, 2:3], in_=SV[:, 1:2]))
            k.v(lambda v: v.tensor_scalar(out=ht, in0=ht, scalar1=SV[:, 2:3], scalar2=None, op0=ALU.mult))
            k.v(lambda v: v.tensor_tensor(out=ht, in0=ht, in1=bc1, op=ALU.mult))
            k.v(lambda v: v.tensor_tensor(out=ht, in0=ht, in1=bc2, op=ALU.add), inc=(sV, 1))
            k.wait("t", sV, sV.n)
            for g4 in range(2):
                for cc in range(4):
                    c = 4 * g4 + cc
                    k.t(lambda t, c=c, cc=cc: t.transpose(PS[6][:, 128 * cc:128 * (cc + 1)], ht[:, 128 * c:128 * (c + 1)], identf, ),
                        inc=(sT, 1) if cc == 3 else None)
                k.wait("v", sT, sT.n)
                k.v(lambda v: v.tensor_copy(out=hs2, in_=PS[6][:, 0:512]), inc=(sV, 1))
                k.wait("t", sV, sV.n)
                k.wait("g", sV, sV.n)
                S = next_st()
                ts = k.store(lambda g4=g4, tt=tt: ap["hfm"].rearrange("(c p) n -> p c n", p=128)[:, 4 * g4:4 * g4 + 4, 128 * tt:128 * (tt + 1)],
                             lambda: hs2.rearrange("p (c n) -> p c n", n=128), S)
                k.wait("v", S, ts)
        k.barrier(BAR)

    def attn_phase(lay, b):
        QP = FP[:, 0:2048]
        KP = FP[:, 2048:4096]
        HST = FP[:, 4096:8192]
        VP_A = FP[:, 8192:9232]
        VP_B = FP[:, 9232:10272]
        VS = FP[:, 10272:10784]
        PPa = FP[:, 10784:11296]
        PPb = FP[:, 11296:11808]
        WSL = FP[:, 11808:12192]
        for pair in range(NP):
            if lay == 1:
                Lb = next_ld()
                k.wait("y", sV, sV.n)
                k.load(lambda: SV[:, 10:11], lambda pair=pair: ap["bq"][128 * pair:128 * (pair + 1), :], Lb)
                k.load(lambda: SV[:, 11:12], lambda pair=pair: ap["bk"][128 * pair:128 * (pair + 1), :], Lb)
                tb = k.load(lambda: SV[:, 12:13], lambda pair=pair: ap["bv"][128 * pair:128 * (pair + 1), :], Lb)
                k.wait("v", Lb, tb)
                k.wait("s", Lb, tb)
            WQS = FP[:, 12192:13216]
            KQS = FP[:, 10784:11808]
            for n in range(QB):
                Lh = next_ld()
                tws = k.load(lambda: HST.rearrange("p (c n) -> p c n", n=512),
                             lambda n=n: ap["hfm"].rearrange("(c p) n -> p c n", p=128)[:, :, 512 * n:512 * (n + 1)], Lh)
                Lw = next_ld()
                k.wait("y", sS, sS.n)
                k.wait("y", sT, sT.n)
                wq_src = "w_qkv" if lay == 0 else "wqf"
                wk_src = "w_qkv" if lay == 0 else "wkf"
                wv_src = "w_qkv" if lay == 0 else "wvf"
                qo = 128 * pair
                ko = D + 128 * pair if lay == 0 else 128 * pair
                vo = 2 * D + 128 * pair if lay == 0 else 128 * pair
                k.load(lambda: WQS.rearrange("p (c n) -> p c n", n=128),
                       lambda qo=qo, wq_src=wq_src: ap[wq_src].rearrange("(c p) n -> p c n", p=128)[:, :, qo:qo + 128], Lw)
                tww = k.load(lambda: KQS.rearrange("p (c n) -> p c n", n=128),
                             lambda ko=ko, wk_src=wk_src: ap[wk_src].rearrange("(c p) n -> p c n", p=128)[:, :, ko:ko + 128], Lw)
                k.wait("t", Lw, tww)
                k.wait("t", Lh, tws)
                for c in range(KC):
                    h_ap = HST[:, 512 * c:512 * (c + 1)]
                    k.t(lambda t, h_ap=h_ap, c=c: t.matmul(PS[0][:, 0:512], WQS[:, 128 * c:128 * (c + 1)], h_ap, start=(c == 0), stop=(c == KC - 1)))
                    k.t(lambda t, h_ap=h_ap, c=c: t.matmul(PS[1][:, 0:512], KQS[:, 128 * c:128 * (c + 1)], h_ap, start=(c == 0), stop=(c == KC - 1)), inc=(sT, 1))
                k.wait("y", sT, sT.n)
                Lv = next_ld()
                tww = k.load(lambda: WQS.rearrange("p (c n) -> p c n", n=128),
                             lambda vo=vo, wv_src=wv_src: ap[wv_src].rearrange("(c p) n -> p c n", p=128)[:, :, vo:vo + 128], Lv)
                k.wait("t", Lv, tww)
                for c in range(KC):
                    h_ap = HST[:, 512 * c:512 * (c + 1)]
                    k.t(lambda t, h_ap=h_ap, c=c: t.matmul(PS[2][:, 0:512], WQS[:, 128 * c:128 * (c + 1)], h_ap, start=(c == 0), stop=(c == KC - 1)), inc=(sT, 1))
                k.wait("y", sT, sT.n)
                k.wait("v", sT, sT.n)
                k.wait("v", sS, sS.n)
                k.wait("s", sT, sT.n)
                if lay == 0:
                    k.v(lambda v, n=n: v.tensor_copy(out=QP[:, 512 * n:512 * (n + 1)], in_=PS[0][:, 0:512]), inc=(sV, 1))
                    k.s(lambda s, n=n: s.activation(out=KP[:, 512 * n:512 * (n + 1)], in_=PS[1][:, 0:512], func=AF.Copy, scale=0.125), inc=(sS, 1))
                    k.v(lambda v: v.tensor_copy(out=VS, in_=PS[2][:, 0:512]), inc=(sV, 1))
                else:
                    k.v(lambda v, n=n: v.tensor_scalar(out=QP[:, 512 * n:512 * (n + 1)], in0=PS[0][:, 0:512], scalar1=SV[:, 10:11], scalar2=None, op0=ALU.add), inc=(sV, 1))
                    k.v(lambda v, n=n: v.tensor_scalar(out=KP[:, 512 * n:512 * (n + 1)], in0=PS[1][:, 0:512], scalar1=SV[:, 11:12], scalar2=0.125, op0=ALU.add, op1=ALU.mult))
                    k.inc("s", sS)
                    k.v(lambda v: v.tensor_scalar(out=VS, in0=PS[2][:, 0:512], scalar1=SV[:, 12:13], scalar2=None, op0=ALU.add), inc=(sV, 1))
                k.wait("t", sV, sV.n)
                k.wait("t", sS, sS.n)
                for st4 in range(4):
                    k.t(lambda t, st4=st4: t.transpose(PS[3][:, 0:128], VS[:, 128 * st4:128 * (st4 + 1)], identf), inc=(sT, 1))
                    k.wait("v", sT, sT.n)
                    ci = 4 * n + st4
                    k.v(lambda v, ci=ci: v.tensor_copy(out=VP_A[:, 65 * ci:65 * ci + 64], in_=PS[3][:, 0:64]))
                    k.v(lambda v, ci=ci: v.tensor_copy(out=VP_B[:, 65 * ci:65 * ci + 64], in_=PS[3][:, 64:128]), inc=(sV, 1))
                    k.wait("t", sV, sV.n)
            for ci_ in range(16):
                k.v(lambda v, ci_=ci_: v.memset(VP_A[:, 65 * ci_ + 64:65 * ci_ + 65], 1.0))
                k.v(lambda v, ci_=ci_: v.memset(VP_B[:, 65 * ci_ + 64:65 * ci_ + 65], 1.0))
            k.inc("v", sV)
            k.wait("t", sV, sV.n)
            k.wait("s", sV, sV.n)
            for src, oi in ((QP, 0), (KP, 2)):
                for blk in range(4):
                    k.s(lambda s, blk=blk, src=src: s.activation(out=VS, in_=src[:, 512 * blk:512 * (blk + 1)], func=AF.Square), inc=(sS, 1))
                    k.wait("t", sS, sS.n)
                    for hh in range(2):
                        k.t(lambda t, hh=hh: t.matmul(PS[3][0:1, 0:512], SV[:, 13 + hh:14 + hh], VS, start=True, stop=True), inc=(sT, 1))
                        k.wait("v", sT, sT.n)
                        if blk == 0:
                            k.v(lambda v, oi=oi, hh=hh: v.tensor_reduce(out=OO[0:1, 160 + oi + hh:161 + oi + hh], in_=PS[3][0:1, 0:512], axis=AXX, op=ALU.max), inc=(sV, 1))
                        else:
                            k.v(lambda v: v.tensor_reduce(out=OO[0:1, 166:167], in_=PS[3][0:1, 0:512], axis=AXX, op=ALU.max))
                            k.v(lambda v, oi=oi, hh=hh: v.tensor_tensor(out=OO[0:1, 160 + oi + hh:161 + oi + hh], in0=OO[0:1, 160 + oi + hh:161 + oi + hh], in1=OO[0:1, 166:167], op=ALU.max), inc=(sV, 1))
                        k.wait("t", sV, sV.n)
                    k.wait("s", sV, sV.n)
            k.wait("s", sV, sV.n)
            k.s(lambda s: s.activation(out=OO[0:1, 160:164], in_=OO[0:1, 160:164], func=AF.Sqrt), inc=(sS, 1))
            k.wait("v", sS, sS.n)
            k.v(lambda v: v.tensor_tensor(out=OO[0:1, 164:165], in0=OO[0:1, 160:161], in1=OO[0:1, 162:163], op=ALU.mult))
            k.v(lambda v: v.tensor_tensor(out=OO[0:1, 165:166], in0=OO[0:1, 161:162], in1=OO[0:1, 163:164], op=ALU.mult), inc=(sV, 1))
            k.wait("t", sV, sV.n)
            for hh in range(2):
                k.t(lambda t, hh=hh: t.matmul(PS[3][:, 0:1], onesf, OO[0:1, 164 + hh:165 + hh], start=True, stop=True), inc=(sT, 1))
                k.wait("s", sT, sT.n)
                k.s(lambda s, hh=hh: s.activation(out=SV[:, 6 + hh:7 + hh], in_=PS[3][:, 0:1], func=AF.Copy, bias=60.0, scale=-1.0), inc=(sS, 1))
                k.wait("t", sS, sS.n)
                k.wait("v", sS, sS.n)
            if lay == 1:
                for hh in range(2):
                    hid = 2 * pair + hh
                    k.v(lambda v, hh=hh, hid=hid: v.tensor_scalar(out=SV[:, 272 + 16 * hh:288 + 16 * hh], in0=SV[:, ALB0 + 16 * hid:ALB0 + 16 * (hid + 1)], scalar1=SV[:, 6 + hh:7 + hh], scalar2=None, op0=ALU.add), inc=(sV, 1))
                k.wait("s", sV, sV.n)
            for n in range(QB):
                clist = list(range(16)) if lay == 1 else list(range(4 * n + 4))
                k.wait("t", sV, sV.n)
                pend = None
                for ci, c in enumerate(clist):
                    par = ci % 2
                    psA = PS[0] if par == 0 else PS[1]
                    psB = PS[4] if par == 0 else PS[5]
                    ppa = FP[:, 10784:11296] if par == 0 else FP[:, 12192:12704]
                    ppb = FP[:, 11296:11808] if par == 0 else FP[:, 12704:13216]
                    crossing = (lay == 0) and (c >= 4 * n)
                    k.t(lambda t, c=c, n=n, psA=psA, crossing=crossing: t.matmul(psA[:, 0:512], KP[0:64, 128 * c:128 * (c + 1)], QP[0:64, 512 * n:512 * (n + 1)], start=True, stop=not crossing))
                    k.t(lambda t, c=c, n=n, psB=psB, crossing=crossing: t.matmul(psB[:, 0:512], KP[64:128, 128 * c:128 * (c + 1)], QP[64:128, 512 * n:512 * (n + 1)], start=True, stop=not crossing),
                        inc=None if crossing else (sT, 1))
                    if crossing:
                        m = c - 4 * n
                        k.t(lambda t, m=m, psA=psA: t.matmul(psA[:, 0:512], identb, maskb[:, 512 * m:512 * (m + 1)], start=False, stop=True))
                        k.t(lambda t, m=m, psB=psB: t.matmul(psB[:, 0:512], identb, maskb[:, 512 * m:512 * (m + 1)], start=False, stop=True), inc=(sT, 1))
                    if lay == 0:
                        biasA, biasB = SV[:, 6:7], SV[:, 7:8]
                    else:
                        biasA, biasB = SV[:, 272 + c:273 + c], SV[:, 288 + c:289 + c]
                    k.wait("s", sT, sT.n)
                    k.s(lambda s, psA=psA, biasA=biasA, ppa=ppa: s.activation(out=ppa, in_=psA[:, 0:512], func=AF.Exp, bias=biasA, scale=1.0))
                    k.s(lambda s, psB=psB, biasB=biasB, ppb=ppb: s.activation(out=ppb, in_=psB[:, 0:512], func=AF.Exp, bias=biasB, scale=1.0), inc=(sS, 1))
                    if pend is not None:
                        pc, pci, pppa, pppb = pend
                        k.wait("t", sS, sS.n - 1)
                        k.t(lambda t, pc=pc, pci=pci, pppa=pppa: t.matmul(PS[2][0:65, 0:512], VP_A[:, 65 * pc:65 * (pc + 1)], pppa, start=(pci == 0), stop=False))
                        k.t(lambda t, pc=pc, pci=pci, pppb=pppb: t.matmul(PS[3][0:65, 0:512], VP_B[:, 65 * pc:65 * (pc + 1)], pppb, start=(pci == 0), stop=False), inc=(sT, 1))
                    pend = (c, ci, ppa, ppb)
                pc, pci, pppa, pppb = pend
                k.wait("t", sS, sS.n)
                k.t(lambda t, pc=pc, pci=pci, pppa=pppa: t.matmul(PS[2][0:65, 0:512], VP_A[:, 65 * pc:65 * (pc + 1)], pppa, start=(pci == 0), stop=True))
                k.t(lambda t, pc=pc, pci=pci, pppb=pppb: t.matmul(PS[3][0:65, 0:512], VP_B[:, 65 * pc:65 * (pc + 1)], pppb, start=(pci == 0), stop=True), inc=(sT, 1))
                k.wait("v", sT, sT.n)
                for hh, psx in ((0, PS[2]), (1, PS[3])):
                    k.v(lambda v, psx=psx: v.tensor_copy(out=LNV[64:65, 0:512], in_=psx[64:65, 0:512]), inc=(sV, 1))
                    k.v(lambda v: v.reciprocal(out=LNV[64:65, 0:512], in_=LNV[64:65, 0:512]), inc=(sV, 1))
                    k.wait("t", sV, sV.n)
                    k.t(lambda t: t.matmul(PS[6][:, 0:512], ONES128[64:65, 0:128], LNV[64:65, 0:512], start=True, stop=True), inc=(sT, 1))
                    k.wait("v", sT, sT.n)
                    k.v(lambda v: v.tensor_copy(out=LNV[0:64, 0:512], in_=PS[6][0:64, 0:512]), inc=(sV, 1))
                    k.v(lambda v, psx=psx: v.tensor_tensor(out=VS[0:64, 0:512], in0=psx[0:64, 0:512], in1=LNV[0:64, 0:512], op=ALU.mult), inc=(sV, 1))
                    k.wait("t", sV, sV.n)
                    k.wait("g", sV, sV.n)
                    S = next_st()
                    row = 128 * pair + 64 * hh
                    ts = k.store(lambda row=row, n=n: ap["ctx"][row:row + 64, 512 * n:512 * (n + 1)], lambda: VS[0:64, 0:512], S)
                    k.wait("v", S, ts)
        k.barrier(BAR)

    def outproj_phase(lay, b):
        OSb = FP[:, 0:4096]
        CXT = FP[:, 4096:8192]
        WS2 = FP[:, 8192:8448]
        XT2 = FP[:, 8448:9472]
        TT2 = FP[:, 9472:10496]
        BOC = FP[:, 10496:11520]
        GAC = FP[:, 11520:12544]
        Lb = next_ld()
        bname = "b_out" if lay == 0 else "bo"
        tb = k.load(lambda: OO[0:1, 256:256 + 1024], lambda bname=bname: ap[bname], Lb)
        k.wait("t", Lb, tb)
        bcast_f32(BOC, OO[0:1, 256:256 + 1024], 1024)
        Lr = next_ld()
        t2 = k.load(lambda: OO[0:1, 256:256 + 1024], lambda: ap["mods"][B * lay + b:B * lay + b + 1, 2 * D:3 * D], Lr)
        k.wait("t", Lr, t2)
        bcast_f32(GAC, OO[0:1, 256:256 + 1024], 1024)
        src_x = ap["x_in"] if lay == 0 else ap["xs"]
        wsrc = ap["w_out"] if lay == 0 else ap["wof"]
        WOS = FP[:, 12544:13568]
        for n in range(QB):
            Lc = next_ld()
            tcs = k.load(lambda: CXT.rearrange("p (c n) -> p c n", n=512),
                         lambda n=n: ap["ctx"].rearrange("(c p) n -> p c n", p=128)[:, :, 512 * n:512 * (n + 1)], Lc)
            k.wait("t", Lc, tcs)
            for m in range(KC):
                psm = PS[0] if m % 2 == 0 else PS[1]
                Lw = next_ld()
                k.wait("y", sT, sT.n)
                tww = k.load(lambda: WOS.rearrange("p (c n) -> p c n", n=128),
                             lambda m=m, wsrc=wsrc: wsrc.rearrange("(c p) n -> p c n", p=128)[:, :, 128 * m:128 * (m + 1)], Lw)
                k.wait("t", Lw, tww)
                for c in range(KC):
                    k.t(lambda t, c=c, psm=psm, m=m: t.matmul(psm[:, 0:512], WOS[:, 128 * c:128 * (c + 1)], CXT[:, 512 * c:512 * (c + 1)], start=(c == 0), stop=(c == KC - 1)), inc=(sT, 1))
                k.wait("v", sT, sT.n)
                k.v(lambda v, m=m, psm=psm: v.tensor_copy(out=OSb[:, 512 * m:512 * (m + 1)], in_=psm[:, 0:512]), inc=(sV, 1))
                k.wait("t", sV, sV.n)
            k.wait("y", sT, sT.n)
            for st4 in range(4):
                tt = 4 * n + st4
                row = T * b + 128 * tt
                Lx = next_ld()
                k.wait("y", sV, sV.n)
                tx = k.load(lambda: XT2, lambda row=row, src_x=src_x: src_x[row:row + 128, :], Lx)
                for m2 in range(2):
                    for cc in range(4):
                        mm = 4 * m2 + cc
                        k.t(lambda t, mm=mm, st4=st4, m2=m2, cc=cc: t.transpose(PS[2 + m2][:, 128 * cc:128 * (cc + 1)], OSb[:, 512 * mm + 128 * st4:512 * mm + 128 * (st4 + 1)], identf),
                            inc=(sT, 1) if cc == 3 else None)
                k.wait("v", sT, sT.n)
                k.wait("v", Lx, tx)
                k.v(lambda v: v.tensor_tensor(out=TT2[:, 0:512], in0=PS[2][:, 0:512], in1=BOC[:, 0:512], op=ALU.add))
                k.v(lambda v: v.tensor_tensor(out=TT2[:, 512:1024], in0=PS[3][:, 0:512], in1=BOC[:, 512:1024], op=ALU.add), inc=(sV, 1))
                k.wait("t", sV, sV.n)
                k.v(lambda v: v.tensor_tensor(out=TT2, in0=TT2, in1=GAC, op=ALU.mult))
                k.v(lambda v: v.tensor_tensor(out=XT2, in0=XT2, in1=TT2, op=ALU.add), inc=(sV, 1))
                k.wait("g", sV, sV.n)
                S = next_st()
                ts = k.store(lambda row=row: ap["xs"][row:row + 128, :], lambda: XT2, S)
                k.wait("v", S, ts)
                k.wait("y", S, ts)
        k.barrier(BAR)

    def ffn0_phase():
        xt = FP[:, 0:1024]
        zt = FP[:, 1024:2048]
        H1 = FP[:, 2048:6144]
        SQF = FP[:, 6144:10240]
        bc1 = FP[:, 10240:11264]
        bc2 = FP[:, 11264:12288]
        bcg = FP[:, 12288:13312]
        OS1 = FP[:, 13312:13824]
        g1b = BCP[:, 0:1024]
        b1b = BCP[:, 1024:2048]
        bh1b = BCP[:, 2048:6144]
        g2b = BCP[:, 6144:10240]
        b2b = BCP[:, 10240:11264]
        xq = XQb[:, 0:1024]
        xqf = XQb[:, 1024:2048]
        xq2 = XQb[:, 2048:6144]
        xq2f = XQb[:, 6144:10240]
        Lw = next_ld()
        tws = []
        for c in range(KC):
            tws.append(k.load(lambda c=c: W1Q[:, 4096 * c:4096 * (c + 1)], lambda c=c: ap["w1q"][128 * c:128 * (c + 1), :], Lw))
        for nmsrc, dst, wid in (("g1r", g1b, 1024), ("b1r", b1b, 1024)):
            Lb = next_ld()
            k.wait("y", sV, sV.n)
            tb = k.load(lambda: OO[0:1, 256:256 + 1024], lambda nmsrc=nmsrc: ap[nmsrc], Lb)
            k.wait("v", Lb, tb)
            k.v(lambda v: v.tensor_copy(out=OB[0:1, 0:1024], in_=OO[0:1, 256:256 + 1024]), inc=(sV, 1))
            k.wait("t", sV, sV.n)
            bcast_bf16(dst, OB[0:1, 0:1024], wid)
        for part in range(4):
            Lb = next_ld()
            k.wait("y", sV, sV.n)
            tb = k.load(lambda: OO[0:1, 256:256 + 1024], lambda part=part: ap["bh1"][0:1, 1024 * part:1024 * (part + 1)], Lb)
            k.wait("v", Lb, tb)
            k.v(lambda v: v.tensor_copy(out=OB[0:1, 0:1024], in_=OO[0:1, 256:256 + 1024]), inc=(sV, 1))
            k.wait("t", sV, sV.n)
            bcast_bf16(bh1b[:, 1024 * part:1024 * (part + 1)], OB[0:1, 0:1024], 1024)
        for part in range(4):
            Lb = next_ld()
            k.wait("y", sV, sV.n)
            tb = k.load(lambda: OO[0:1, 256:256 + 1024], lambda part=part: ap["g2r"][0:1, 1024 * part:1024 * (part + 1)], Lb)
            k.wait("v", Lb, tb)
            k.v(lambda v: v.tensor_copy(out=OB[0:1, 0:1024], in_=OO[0:1, 256:256 + 1024]), inc=(sV, 1))
            k.wait("t", sV, sV.n)
            bcast_bf16(g2b[:, 1024 * part:1024 * (part + 1)], OB[0:1, 0:1024], 1024)
        b2parts = []
        for part in range(4):
            Lb = next_ld()
            k.wait("y", sV, sV.n)
            tb = k.load(lambda: OO[0:1, 256:256 + 1024], lambda part=part: ap["b2r"][0:1, 1024 * part:1024 * (part + 1)], Lb)
            k.wait("v", Lb, tb)
            k.v(lambda v, part=part: v.tensor_copy(out=OB[0:1, 1024 * (part + 1):1024 * (part + 2)], in_=OO[0:1, 256:256 + 1024]), inc=(sV, 1))
            k.wait("t", sV, sV.n)
            k.v(lambda v, part=part: v.tensor_copy(out=OB[0:1, 1024 + 128 * 0:1024 + 128 * 0] if False else OB[0:1, 0:1024], in_=OO[0:1, 256:256 + 1024]), inc=(sV, 1))
        wm1 = SV[:, 304:305]
        wm2 = SV[:, 305:306]
        for wname, dstv, n_elem, nch, wid in (("w1", wm1, 4096.0 * 1024.0, 32, 1024), ("w2", wm2, 1024.0 * 4096.0, 8, 4096)):
            k.v(lambda v: v.memset(SV[:, 3:4], 0.0), inc=(sV, 1))
            for ch in range(nch):
                for sbk in range(wid // 1024):
                    L = next_ld()
                    tw = k.load(lambda: SQF[:, 0:1024], lambda ch=ch, sbk=sbk, wname=wname: ap[wname][128 * ch:128 * (ch + 1), 1024 * sbk:1024 * (sbk + 1)], L)
                    k.wait("v", L, tw)
                    k.v(lambda v: v.tensor_reduce(out=SV[:, 4:5], in_=SQF[:, 0:1024], axis=AXX, op=ALU.add, apply_absolute_value=True))
                    k.v(lambda v: v.tensor_tensor(out=SV[:, 3:4], in0=SV[:, 3:4], in1=SV[:, 4:5], op=ALU.add), inc=(sV, 1))
                    k.wait("y", sV, sV.n)
            k.wait("t", sV, sV.n)
            k.t(lambda t: t.matmul(PS[6][0:1, 0:1], onescol, SV[:, 3:4], start=True, stop=True), inc=(sT, 1))
            k.wait("v", sT, sT.n)
            k.v(lambda v, n_elem=n_elem: v.tensor_scalar(out=OO[0:1, 150:151], in0=PS[6][0:1, 0:1], scalar1=1.0 / n_elem, scalar2=1e-5, op0=ALU.mult, op1=ALU.max), inc=(sV, 1))
            k.wait("t", sV, sV.n)
            k.t(lambda t: t.matmul(PS[6][:, 0:1], onesf, OO[0:1, 150:151], start=True, stop=True), inc=(sT, 1))
            k.wait("v", sT, sT.n)
            k.v(lambda v, dstv=dstv: v.tensor_copy(out=dstv, in_=PS[6][:, 0:1]), inc=(sV, 1))
            k.wait("t", sV, sV.n)
        for c in range(KC):
            k.wait("t", Lw, tws[c])
        for b in range(B):
            Lr = next_ld()
            t2 = k.load(lambda: OO[0:1, 256:1280], lambda b=b: ap["mods"][b:b + 1, 4 * D:5 * D], Lr)
            k.wait("t", Lr, t2)
            bcast_f32(bc1, OO[0:1, 256:1280], 1024, bias=1.0)
            Lr = next_ld()
            t2 = k.load(lambda: OO[0:1, 256:1280], lambda b=b: ap["mods"][b:b + 1, 3 * D:4 * D], Lr)
            k.wait("t", Lr, t2)
            bcast_f32(bc2, OO[0:1, 256:1280], 1024)
            Lr = next_ld()
            t2 = k.load(lambda: OO[0:1, 256:1280], lambda b=b: ap["mods"][b:b + 1, 5 * D:6 * D], Lr)
            k.wait("t", Lr, t2)
            bcast_f32(bcg, OO[0:1, 256:1280], 1024)
            for tt in range(16):
                row = T * b + 128 * tt
                L = next_ld()
                k.wait("y", sV, sV.n)
                tw = k.load(lambda: xt, lambda row=row: ap["xs"][row:row + 128, :], L)
                k.wait("v", L, tw)
                k.v(lambda v: v.tensor_reduce(out=SV[:, 0:1], in_=xt, axis=AXX, op=ALU.add))
                k.v(lambda v: v.tensor_scalar(out=SV[:, 0:1], in0=SV[:, 0:1], scalar1=1.0 / D, scalar2=None, op0=ALU.mult))
                k.v(lambda v: v.tensor_scalar(out=zt, in0=xt, scalar1=SV[:, 0:1], scalar2=None, op0=ALU.subtract), inc=(sV, 1))
                k.wait("s", sV, sV.n)
                k.s(lambda s: s.activation(out=SQF[:, 0:1024], in_=zt, func=AF.Square), inc=(sS, 1))
                k.wait("v", sS, sS.n)
                k.v(lambda v: v.tensor_reduce(out=SV[:, 1:2], in_=SQF[:, 0:1024], axis=AXX, op=ALU.add), inc=(sV, 1))
                k.wait("s", sV, sV.n)
                k.s(lambda s: s.activation(out=SV[:, 1:2], in_=SV[:, 1:2], func=AF.Sqrt, bias=1e-5, scale=1.0 / D), inc=(sS, 1))
                k.wait("v", sS, sS.n)
                k.v(lambda v: v.reciprocal(out=SV[:, 2:3], in_=SV[:, 1:2]))
                k.v(lambda v: v.tensor_scalar(out=zt, in0=zt, scalar1=SV[:, 2:3], scalar2=None, op0=ALU.mult))
                k.v(lambda v: v.tensor_tensor(out=zt, in0=zt, in1=bc1, op=ALU.mult))
                k.v(lambda v: v.tensor_tensor(out=zt, in0=zt, in1=bc2, op=ALU.add))
                k.v(lambda v: v.tensor_reduce(out=SV[:, 0:1], in_=zt, axis=AXX, op=ALU.add))
                k.v(lambda v: v.tensor_scalar(out=SV[:, 0:1], in0=SV[:, 0:1], scalar1=1.0 / D, scalar2=None, op0=ALU.mult))
                k.v(lambda v: v.tensor_scalar(out=zt, in0=zt, scalar1=SV[:, 0:1], scalar2=None, op0=ALU.subtract), inc=(sV, 1))
                k.wait("s", sV, sV.n)
                k.s(lambda s: s.activation(out=SQF[:, 0:1024], in_=zt, func=AF.Square), inc=(sS, 1))
                k.wait("v", sS, sS.n)
                k.v(lambda v: v.tensor_reduce(out=SV[:, 1:2], in_=SQF[:, 0:1024], axis=AXX, op=ALU.add), inc=(sV, 1))
                k.wait("s", sV, sV.n)
                k.s(lambda s: s.activation(out=SV[:, 1:2], in_=SV[:, 1:2], func=AF.Sqrt, bias=1e-5, scale=1.0 / D), inc=(sS, 1))
                k.wait("v", sS, sS.n)
                k.v(lambda v: v.reciprocal(out=SV[:, 2:3], in_=SV[:, 1:2]))
                k.v(lambda v: v.tensor_scalar(out=zt, in0=zt, scalar1=SV[:, 2:3], scalar2=None, op0=ALU.mult))
                k.v(lambda v: v.tensor_tensor(out=zt, in0=zt, in1=g1b, op=ALU.mult))
                k.v(lambda v: v.tensor_tensor(out=zt, in0=zt, in1=b1b, op=ALU.add))
                k.v(lambda v: v.tensor_reduce(out=SV[:, 3:4], in_=zt, axis=AXX, op=ALU.max, apply_absolute_value=True))
                k.v(lambda v: v.tensor_scalar(out=SV[:, 3:4], in0=SV[:, 3:4], scalar1=1e-5, scalar2=None, op0=ALU.max))
                k.v(lambda v: v.reciprocal(out=SV[:, 4:5], in_=SV[:, 3:4]))
                k.v(lambda v: v.tensor_scalar(out=SV[:, 4:5], in0=SV[:, 4:5], scalar1=127.0, scalar2=None, op0=ALU.mult))
                k.v(lambda v: v.reciprocal(out=SV[:, 5:6], in_=SV[:, 4:5]))
                k.v(lambda v: v.tensor_scalar(out=zt, in0=zt, scalar1=SV[:, 4:5], scalar2=0.5, op0=ALU.mult, op1=ALU.add))
                vfloor(zt, zt, 1024, SQF[:, 0:1024])
                k.v(lambda v: v.tensor_scalar(out=xq, in0=zt, scalar1=-128.0, scalar2=127.0, op0=ALU.max, op1=ALU.min), inc=(sV, 1))
                k.v(lambda v: v.tensor_tensor(out=SV[:, 9:10], in0=SV[:, 5:6], in1=wm1, op=ALU.mult), inc=(sV, 1))
                k.wait("t", sV, sV.n)
                for g4 in range(2):
                    for cc in range(4):
                        c = 4 * g4 + cc
                        k.t(lambda t, c=c, cc=cc: t.transpose(PSB[:, 128 * cc:128 * (cc + 1)], xq[:, 128 * c:128 * (c + 1)], identb), inc=(sT, 1) if cc == 3 else None)
                    k.wait("v", sT, sT.n)
                    k.v(lambda v, g4=g4: v.tensor_copy(out=xqf[:, 512 * g4:512 * (g4 + 1)], in_=PSB[:, 0:512]), inc=(sV, 1))
                    k.wait("t", sV, sV.n)
                for mg in range(8):
                    for cc in range(4):
                        m = 4 * mg + cc
                        for c in range(KC):
                            k.t(lambda t, m=m, cc=cc, c=c: t.matmul(PS[0][:, 128 * cc:128 * (cc + 1)], W1Q[:, 4096 * c + 128 * m:4096 * c + 128 * (m + 1)], xqf[:, 128 * c:128 * (c + 1)], start=(c == 0 and cc == 0), stop=(c == KC - 1 and cc == 3)),
                                inc=(sT, 1) if (c == KC - 1 and cc == 3) else None)
                    k.wait("v", sT, sT.n)
                    k.v(lambda v: v.tensor_copy(out=OS1, in_=PS[0][:, 0:512]), inc=(sV, 1))
                    k.wait("t", sV, sV.n)
                    for cc in range(4):
                        k.t(lambda t, cc=cc: t.transpose(PS[1][:, 128 * cc:128 * (cc + 1)], OS1[:, 128 * cc:128 * (cc + 1)], identf), inc=(sT, 1) if cc == 3 else None)
                    k.wait("v", sT, sT.n)
                    k.v(lambda v, mg=mg: v.scalar_tensor_tensor(out=H1[:, 512 * mg:512 * (mg + 1)], in0=PS[1][:, 0:512], scalar=SV[:, 9:10], in1=bh1b[:, 512 * mg:512 * (mg + 1)], op0=ALU.mult, op1=ALU.add), inc=(sV, 1))
                    k.wait("t", sV, sV.n)
                k.wait("s", sV, sV.n)
                k.s(lambda s: s.activation(out=H1, in_=H1, func=AF.Gelu), inc=(sS, 1))
                k.wait("v", sS, sS.n)
                k.v(lambda v: v.tensor_reduce(out=SV[:, 0:1], in_=H1, axis=AXX, op=ALU.add))
                k.v(lambda v: v.tensor_scalar(out=SV[:, 0:1], in0=SV[:, 0:1], scalar1=1.0 / 4096.0, scalar2=None, op0=ALU.mult))
                k.v(lambda v: v.tensor_scalar(out=H1, in0=H1, scalar1=SV[:, 0:1], scalar2=None, op0=ALU.subtract), inc=(sV, 1))
                k.wait("s", sV, sV.n)
                k.s(lambda s: s.activation(out=SQF, in_=H1, func=AF.Square), inc=(sS, 1))
                k.wait("v", sS, sS.n)
                k.v(lambda v: v.tensor_reduce(out=SV[:, 1:2], in_=SQF, axis=AXX, op=ALU.add), inc=(sV, 1))
                k.wait("s", sV, sV.n)
                k.s(lambda s: s.activation(out=SV[:, 1:2], in_=SV[:, 1:2], func=AF.Sqrt, bias=1e-5, scale=1.0 / 4096.0), inc=(sS, 1))
                k.wait("v", sS, sS.n)
                k.v(lambda v: v.reciprocal(out=SV[:, 2:3], in_=SV[:, 1:2]))
                k.v(lambda v: v.tensor_scalar(out=H1, in0=H1, scalar1=SV[:, 2:3], scalar2=None, op0=ALU.mult))
                k.v(lambda v: v.tensor_tensor(out=H1, in0=H1, in1=g2b, op=ALU.mult), inc=(sV, 1))
                for part in range(4):
                    bcast_bf16(b2b, OB[0:1, 1024 * (part + 1):1024 * (part + 2)], 1024)
                    k.wait("v", sS, sS.n)
                    k.v(lambda v, part=part: v.tensor_tensor(out=H1[:, 1024 * part:1024 * (part + 1)], in0=H1[:, 1024 * part:1024 * (part + 1)], in1=b2b, op=ALU.add), inc=(sV, 1))
                    k.wait("s", sV, sV.n)
                k.v(lambda v: v.tensor_reduce(out=SV[:, 3:4], in_=H1, axis=AXX, op=ALU.max, apply_absolute_value=True))
                k.v(lambda v: v.tensor_scalar(out=SV[:, 3:4], in0=SV[:, 3:4], scalar1=1e-5, scalar2=None, op0=ALU.max))
                k.v(lambda v: v.reciprocal(out=SV[:, 4:5], in_=SV[:, 3:4]))
                k.v(lambda v: v.tensor_scalar(out=SV[:, 4:5], in0=SV[:, 4:5], scalar1=127.0, scalar2=None, op0=ALU.mult))
                k.v(lambda v: v.reciprocal(out=SV[:, 5:6], in_=SV[:, 4:5]))
                k.v(lambda v: v.tensor_scalar(out=H1, in0=H1, scalar1=SV[:, 4:5], scalar2=0.5, op0=ALU.mult, op1=ALU.add))
                for part in range(4):
                    sl = slice(1024 * part, 1024 * (part + 1))
                    vfloor(H1[:, sl], H1[:, sl], 1024, SQF[:, 0:1024])
                k.v(lambda v: v.tensor_scalar(out=xq2, in0=H1, scalar1=-128.0, scalar2=127.0, op0=ALU.max, op1=ALU.min), inc=(sV, 1))
                k.v(lambda v: v.tensor_tensor(out=SV[:, 9:10], in0=SV[:, 5:6], in1=wm2, op=ALU.mult), inc=(sV, 1))
                k.wait("t", sV, sV.n)
                for g4 in range(8):
                    for cc in range(4):
                        c = 4 * g4 + cc
                        k.t(lambda t, c=c, cc=cc: t.transpose(PSB[:, 128 * cc:128 * (cc + 1)], xq2[:, 128 * c:128 * (c + 1)], identb), inc=(sT, 1) if cc == 3 else None)
                    k.wait("v", sT, sT.n)
                    k.v(lambda v, g4=g4: v.tensor_copy(out=xq2f[:, 512 * g4:512 * (g4 + 1)], in_=PSB[:, 0:512]), inc=(sV, 1))
                    k.wait("t", sV, sV.n)
                for mg in range(2):
                    for cc in range(4):
                        m = 4 * mg + cc
                        Lw2 = next_ld()
                        wslot = W2Q[:, 0:4096]
                        tws2 = k.load(lambda wslot=wslot: wslot,
                                      lambda m=m: ap["w2q"][128 * m:128 * (m + 1), :], Lw2)
                        k.wait("t", Lw2, tws2)
                        for c in range(32):
                            k.t(lambda t, c=c, cc=cc, wslot=wslot: t.matmul(PS[0][:, 128 * cc:128 * (cc + 1)], wslot[:, 128 * c:128 * (c + 1)], xq2f[:, 128 * c:128 * (c + 1)], start=(c == 0 and cc == 0), stop=(c == 31 and cc == 3)),
                                inc=(sT, 1) if (c == 31) else None)
                        k.wait("y", sT, sT.n)
                    k.wait("v", sT, sT.n)
                    k.v(lambda v: v.tensor_copy(out=OS1, in_=PS[0][:, 0:512]), inc=(sV, 1))
                    k.wait("t", sV, sV.n)
                    for cc in range(4):
                        k.t(lambda t, cc=cc: t.transpose(PS[1][:, 128 * cc:128 * (cc + 1)], OS1[:, 128 * cc:128 * (cc + 1)], identf), inc=(sT, 1) if cc == 3 else None)
                    k.wait("v", sT, sT.n)
                    k.v(lambda v, mg=mg: v.scalar_tensor_tensor(out=zt[:, 512 * mg:512 * (mg + 1)], in0=PS[1][:, 0:512], scalar=SV[:, 9:10], in1=BCP[:, 11264 + 512 * mg:11264 + 512 * (mg + 1)], op0=ALU.mult, op1=ALU.add), inc=(sV, 1))
                    k.wait("t", sV, sV.n)
                k.v(lambda v: v.tensor_tensor(out=zt, in0=zt, in1=bcg, op=ALU.mult))
                k.v(lambda v: v.tensor_tensor(out=xt, in0=xt, in1=zt, op=ALU.add), inc=(sV, 1))
                k.wait("g", sV, sV.n)
                S = next_st()
                ts = k.store(lambda row=row: ap["xs"][row:row + 128, :], lambda: xt, S)
                k.wait("v", S, ts)
                k.wait("y", S, ts)
        k.barrier(BAR)

    def bh2_bcast():
        Lb = next_ld()
        k.wait("y", sV, sV.n)
        tb = k.load(lambda: OO[0:1, 256:256 + 1024], lambda: ap["bh2"], Lb)
        k.wait("v", Lb, tb)
        k.v(lambda v: v.tensor_copy(out=OB[0:1, 0:1024], in_=OO[0:1, 256:256 + 1024]), inc=(sV, 1))
        k.wait("t", sV, sV.n)
        bcast_bf16(BCP[:, 11264:12288], OB[0:1, 0:1024], 1024)

    def ffn1_phase():
        xt = FP[:, 0:1024]
        yt = FP[:, 1024:2048]
        at = FP[:, 2048:3072]
        et = FP[:, 3072:4096]
        st_ = FP[:, 4096:5120]
        ut = FP[:, 5120:6144]
        cm = FP[:, 6144:7168]
        zt = FP[:, 7168:8192]
        OS1 = FP[:, 8192:8704]
        bc1 = FP[:, 10240:11264]
        bc2 = FP[:, 11264:12288]
        bcg = FP[:, 12288:13312]
        bfb = FP[:, 8704:9728]
        xq = XQb[:, 0:1024]
        xqf = XQb[:, 1024:2048]
        Lw = next_ld()
        tws = []
        for c in range(KC):
            tws.append(k.load(lambda c=c: W2Q[:, 1024 * c:1024 * (c + 1)] if False else W1Q[:, 1024 * c:1024 * (c + 1)], lambda c=c: ap["wf8"][128 * c:128 * (c + 1), :], Lw))
        Lb = next_ld()
        tb = k.load(lambda: OO[0:1, 256:256 + 1024], lambda: ap["bf8"], Lb)
        k.wait("t", Lb, tb)
        bcast_f32(bfb, OO[0:1, 256:256 + 1024], 1024)
        k.t(lambda t: t.matmul(PS[6][:, 0:1], onesf, OO[0:1, 144:145], start=True, stop=True), inc=(sT, 1))
        k.wait("v", sT, sT.n)
        k.v(lambda v: v.tensor_copy(out=SV[:, 12:13], in_=PS[6][:, 0:1]), inc=(sV, 1))
        k.wait("t", sV, sV.n)
        for c in range(KC):
            k.wait("t", Lw, tws[c])
        for b in range(B):
            Lr = next_ld()
            t2 = k.load(lambda: OO[0:1, 256:1280], lambda b=b: ap["mods"][B + b:B + b + 1, 4 * D:5 * D], Lr)
            k.wait("t", Lr, t2)
            bcast_f32(bc1, OO[0:1, 256:1280], 1024, bias=1.0)
            Lr = next_ld()
            t2 = k.load(lambda: OO[0:1, 256:1280], lambda b=b: ap["mods"][B + b:B + b + 1, 3 * D:4 * D], Lr)
            k.wait("t", Lr, t2)
            bcast_f32(bc2, OO[0:1, 256:1280], 1024)
            Lr = next_ld()
            t2 = k.load(lambda: OO[0:1, 256:1280], lambda b=b: ap["mods"][B + b:B + b + 1, 5 * D:6 * D], Lr)
            k.wait("t", Lr, t2)
            bcast_f32(bcg, OO[0:1, 256:1280], 1024)
            for tt in range(16):
                row = T * b + 128 * tt
                L = next_ld()
                k.wait("y", sV, sV.n)
                tw = k.load(lambda: xt, lambda row=row: ap["xs"][row:row + 128, :], L)
                k.wait("v", L, tw)
                k.v(lambda v: v.tensor_reduce(out=SV[:, 0:1], in_=xt, axis=AXX, op=ALU.add))
                k.v(lambda v: v.tensor_scalar(out=SV[:, 0:1], in0=SV[:, 0:1], scalar1=1.0 / D, scalar2=None, op0=ALU.mult))
                k.v(lambda v: v.tensor_scalar(out=zt, in0=xt, scalar1=SV[:, 0:1], scalar2=None, op0=ALU.subtract), inc=(sV, 1))
                k.wait("s", sV, sV.n)
                k.s(lambda s: s.activation(out=cm, in_=zt, func=AF.Square), inc=(sS, 1))
                k.wait("v", sS, sS.n)
                k.v(lambda v: v.tensor_reduce(out=SV[:, 1:2], in_=cm, axis=AXX, op=ALU.add), inc=(sV, 1))
                k.wait("s", sV, sV.n)
                k.s(lambda s: s.activation(out=SV[:, 1:2], in_=SV[:, 1:2], func=AF.Sqrt, bias=1e-5, scale=1.0 / D), inc=(sS, 1))
                k.wait("v", sS, sS.n)
                k.v(lambda v: v.reciprocal(out=SV[:, 2:3], in_=SV[:, 1:2]))
                k.v(lambda v: v.tensor_scalar(out=zt, in0=zt, scalar1=SV[:, 2:3], scalar2=None, op0=ALU.mult))
                k.v(lambda v: v.tensor_tensor(out=zt, in0=zt, in1=bc1, op=ALU.mult))
                k.v(lambda v: v.tensor_tensor(out=zt, in0=zt, in1=bc2, op=ALU.add))
                k.v(lambda v: v.tensor_reduce(out=SV[:, 3:4], in_=zt, axis=AXX, op=ALU.max, apply_absolute_value=True))
                k.v(lambda v: v.tensor_scalar(out=SV[:, 3:4], in0=SV[:, 3:4], scalar1=1e-12, scalar2=1.0 / 448.0, op0=ALU.max, op1=ALU.mult))
                k.v(lambda v: v.reciprocal(out=SV[:, 4:5], in_=SV[:, 3:4]))
                k.v(lambda v: v.tensor_scalar(out=yt, in0=zt, scalar1=SV[:, 4:5], scalar2=None, op0=ALU.mult))
                k.v(lambda v: v.tensor_scalar(out=at, in0=yt, scalar1=-1.0, scalar2=None, op0=ALU.mult))
                k.v(lambda v: v.tensor_tensor(out=at, in0=at, in1=yt, op=ALU.max))
                k.v(lambda v: v.tensor_scalar(out=at, in0=at, scalar1=1e-30, scalar2=None, op0=ALU.max), inc=(sV, 1))
                k.wait("s", sV, sV.n)
                k.s(lambda s: s.activation(out=at, in_=at, func=AF.Ln), inc=(sS, 1))
                k.wait("v", sS, sS.n)
                k.v(lambda v: v.tensor_scalar(out=et, in0=at, scalar1=1.0 / LN2C, scalar2=None, op0=ALU.mult))
                vfloor(et, et, 1024, cm)
                k.v(lambda v: v.tensor_scalar(out=et, in0=et, scalar1=-6.0, scalar2=8.0, op0=ALU.max, op1=ALU.min), inc=(sV, 1))
                k.wait("s", sV, sV.n)
                k.s(lambda s: s.activation(out=st_, in_=et, func=AF.Exp, bias=-3.0 * LN2C, scale=LN2C), inc=(sS, 1))
                k.wait("v", sS, sS.n)
                k.v(lambda v: v.reciprocal(out=ut, in_=st_))
                k.v(lambda v: v.tensor_tensor(out=ut, in0=yt, in1=ut, op=ALU.mult))
                k.v(lambda v: v.tensor_scalar(out=ut, in0=ut, scalar1=0.5, scalar2=None, op0=ALU.add))
                vfloor(ut, ut, 1024, cm)
                k.v(lambda v: v.tensor_tensor(out=xq, in0=ut, in1=st_, op=ALU.mult), inc=(sV, 1))
                k.v(lambda v: v.tensor_tensor(out=SV[:, 9:10], in0=SV[:, 3:4], in1=SV[:, 12:13], op=ALU.mult), inc=(sV, 1))
                k.wait("t", sV, sV.n)
                for g4 in range(2):
                    for cc in range(4):
                        c = 4 * g4 + cc
                        k.t(lambda t, c=c, cc=cc: t.transpose(PSB[:, 128 * cc:128 * (cc + 1)], xq[:, 128 * c:128 * (c + 1)], identb), inc=(sT, 1) if cc == 3 else None)
                    k.wait("v", sT, sT.n)
                    k.v(lambda v, g4=g4: v.tensor_copy(out=xqf[:, 512 * g4:512 * (g4 + 1)], in_=PSB[:, 0:512]), inc=(sV, 1))
                    k.wait("t", sV, sV.n)
                for mg in range(2):
                    for cc in range(4):
                        m = 4 * mg + cc
                        for c in range(KC):
                            k.t(lambda t, m=m, cc=cc, c=c: t.matmul(PS[0][:, 128 * cc:128 * (cc + 1)], W1Q[:, 1024 * c + 128 * m:1024 * c + 128 * (m + 1)], xqf[:, 128 * c:128 * (c + 1)], start=(c == 0 and cc == 0), stop=(c == KC - 1 and cc == 3)),
                                inc=(sT, 1) if (c == KC - 1 and cc == 3) else None)
                    k.wait("v", sT, sT.n)
                    k.v(lambda v: v.tensor_copy(out=OS1, in_=PS[0][:, 0:512]), inc=(sV, 1))
                    k.wait("t", sV, sV.n)
                    for cc in range(4):
                        k.t(lambda t, cc=cc: t.transpose(PS[1][:, 128 * cc:128 * (cc + 1)], OS1[:, 128 * cc:128 * (cc + 1)], identf), inc=(sT, 1) if cc == 3 else None)
                    k.wait("v", sT, sT.n)
                    k.v(lambda v, mg=mg: v.scalar_tensor_tensor(out=zt[:, 512 * mg:512 * (mg + 1)], in0=PS[1][:, 0:512], scalar=SV[:, 9:10], in1=bfb[:, 512 * mg:512 * (mg + 1)], op0=ALU.mult, op1=ALU.add), inc=(sV, 1))
                    k.wait("t", sV, sV.n)
                k.v(lambda v: v.tensor_tensor(out=zt, in0=zt, in1=bcg, op=ALU.mult))
                k.v(lambda v: v.tensor_tensor(out=xt, in0=xt, in1=zt, op=ALU.add), inc=(sV, 1))
                k.wait("g", sV, sV.n)
                S = next_st()
                ts = k.store(lambda row=row: y_out[row:row + 128, :], lambda: xt, S)
                k.wait("v", S, ts)
                k.wait("y", S, ts)
        k.barrier(BAR)

    import os
    KPH = int(os.environ.get("KPH", "99"))
    bh2_bcast()
    for lay in range(2):
        if KPH >= (1 if lay == 0 else 3):
            for b in range(B):
                ln_mod_phase(lay, b, 0)
                attn_phase(lay, b)
                outproj_phase(lay, b)
        if lay == 0:
            if KPH >= 2:
                ffn0_phase()
        else:
            if KPH >= 4:
                ffn1_phase()

    for s_ in sts:
        k.wait("g", s_, s_.n)

    with nc.Block() as block:
        @block.tensor
        def _(t):
            for f in k.q["t"]:
                f(t)

        @block.vector
        def _(v):
            for f in k.q["v"]:
                f(v)

        @block.scalar
        def _(s):
            for f in k.q["s"]:
                f(s)

        @block.gpsimd
        def _(g):
            for f in k.q["g"]:
                f(g)

        @block.sync
        def _(y):
            for f in k.q["y"]:
                f(y)

    es.close()
    return nc


_NC_CACHE = [None]


def kernel(**inputs):
    n = 8
    xfull = np.ascontiguousarray(np.asarray(inputs["L0_attn_x"], dtype=np.float32))
    maskc = np.zeros((128, 4 * 512), dtype=np.float64)
    for m in range(4):
        i = np.arange(128)[:, None]
        j = np.arange(512)[None, :]
        maskc[:, 512 * m:512 * (m + 1)] = np.where(i + 128 * m > j, -1e10, 0.0)
    maskc = maskc.astype(ml_dtypes.bfloat16)
    ramp = (np.arange(128)[:, None] + 128 * np.arange(16)[None, :] - 2047).astype(np.float32)
    common = {
        "w_qkv": np.asarray(inputs["L0_attn_w_qkv"], np.float32),
        "w_out": np.asarray(inputs["L0_attn_w_out"], np.float32),
        "b_out": np.asarray(inputs["L0_attn_b_out"], np.float32).reshape(1, D),
        "g1r": np.asarray(inputs["L0_ffn_ln1_g"], np.float32).reshape(1, D),
        "b1r": np.asarray(inputs["L0_ffn_ln1_b"], np.float32).reshape(1, D),
        "w1": np.asarray(inputs["L0_ffn_w1"], np.float32),
        "bh1": np.asarray(inputs["L0_ffn_b1"], np.float32).reshape(1, 4 * D),
        "g2r": np.asarray(inputs["L0_ffn_ln2_g"], np.float32).reshape(1, 4 * D),
        "b2r": np.asarray(inputs["L0_ffn_ln2_b"], np.float32).reshape(1, 4 * D),
        "w2": np.asarray(inputs["L0_ffn_w2"], np.float32),
        "bh2": np.asarray(inputs["L0_ffn_b2"], np.float32).reshape(1, D),
        "wmod0": np.asarray(inputs["L0_wmod"], np.float32),
        "Wq": np.asarray(inputs["L1_attn_Wq"], np.float32),
        "bq": np.asarray(inputs["L1_attn_bq"], np.float32).reshape(D, 1),
        "Wk": np.asarray(inputs["L1_attn_Wk"], np.float32),
        "bk": np.asarray(inputs["L1_attn_bk"], np.float32).reshape(D, 1),
        "Wv": np.asarray(inputs["L1_attn_Wv"], np.float32),
        "bv": np.asarray(inputs["L1_attn_bv"], np.float32).reshape(D, 1),
        "Wo": np.asarray(inputs["L1_attn_Wo"], np.float32),
        "bo": np.asarray(inputs["L1_attn_bo"], np.float32).reshape(1, D),
        "slopes": np.asarray(inputs["L1_attn_slopes"], np.float32).reshape(1, 16),
        "w8": np.asarray(inputs["L1_ffn_weight"]).view(np.uint8),
        "wsc": np.asarray(inputs["L1_ffn_weight_scale"], np.float32).reshape(1, 1),
        "bf8": np.asarray(inputs["L1_ffn_bias"], np.float32).reshape(1, D),
        "wmod1": np.asarray(inputs["L1_wmod"], np.float32),
        "identf": np.eye(128, dtype=np.float32),
        "identb": np.eye(128, dtype=ml_dtypes.bfloat16),
        "maskc": maskc,
        "ramp": ramp,
    }
    c0 = np.asarray(inputs["L0_c"], np.float32)
    c1 = np.asarray(inputs["L1_c"], np.float32)
    in_maps = []
    for i in range(n):
        m = dict(common)
        m["x_in"] = np.ascontiguousarray(xfull[2 * i:2 * i + 2].reshape(B * T, D))
        m["c0"] = np.ascontiguousarray(c0[2 * i:2 * i + 2])
        m["c1"] = np.ascontiguousarray(c1[2 * i:2 * i + 2])
        in_maps.append(m)
    if _NC_CACHE[0] is None:
        _NC_CACHE[0] = build_program()
    nc = _NC_CACHE[0]
    res = run_bass_kernel_spmd(nc, in_maps, core_ids=list(range(n)))
    out = np.concatenate([r["y"].reshape(1, B, T, D) for r in res.results], axis=0)
    return np.ascontiguousarray(out.reshape(16, T, D)).astype(np.float32)


# version 11 (final)
# speedup vs baseline: 2.0309x; 1.0844x over previous
import numpy as np
import ml_dtypes
import concourse.bass as bass
import concourse.mybir as mybir
from concourse.bass_utils import run_bass_kernel_spmd

F32 = mybir.dt.float32
BF16 = mybir.dt.bfloat16
I32 = mybir.dt.int32
U8 = mybir.dt.uint8
AF = mybir.ActivationFunctionType
ALU = mybir.AluOpType
AXX = mybir.AxisListType.X

D = 1024
T = 2048
B = 2
NP = 8
KC = 8
QB = 4
LN2C = float(np.log(2.0))


class Sem:
    def __init__(self, name):
        self.name, self.n, self.h = name, 0, None


class K:
    def __init__(self):
        self.q = {"t": [], "v": [], "s": [], "g": [], "y": []}
        self.sems = []
        self.nbar = 0

    def sem(self, name):
        s = Sem(name)
        self.sems.append(s)
        return s

    def op(self, eng, fn, inc=None, drain=False):
        def run(e):
            ins = fn(e)
            if inc is not None:
                ins.then_inc(inc[0].h, inc[1])
            if drain:
                e.drain()
        self.q[eng].append(run)
        if inc is not None:
            inc[0].n += inc[1]

    def wait(self, eng, s, val):
        self.q[eng].append(lambda e, s=s, val=val: e.wait_ge(s.h, val))

    def inc(self, eng, s, kk=1):
        s.n += kk
        self.q[eng].append(lambda e, s=s, kk=kk: e.sem_inc(s.h, kk))

    def barrier(self, bar):
        self.nbar += 1
        tgt = 5 * self.nbar
        for eng in ("t", "v", "s", "g", "y"):
            self.inc(eng, bar)
            self.wait(eng, bar, tgt)

    def t(self, fn, inc=None):
        self.op("t", fn, inc)

    def v(self, fn, inc=None):
        self.op("v", fn, inc, drain=True)

    def s(self, fn, inc=None):
        self.op("s", fn, inc, drain=True)

    def load(self, dst_fn, src_fn, sem):
        prev = sem.n
        def run(e, d=dst_fn, s=src_fn, m=sem, prev=prev):
            e.wait_ge(m.h, prev)
            e.dma_start(out=d(), in_=s()).then_inc(m.h, 16)
        self.q["y"].append(run)
        sem.n += 16
        return sem.n

    def store(self, dst_fn, src_fn, sem):
        prev = sem.n
        def run(e, d=dst_fn, s=src_fn, m=sem, prev=prev):
            e.wait_ge(m.h, prev)
            e.dma_start(out=d(), in_=s()).then_inc(m.h, 16)
        self.q["g"].append(run)
        sem.n += 16
        return sem.n


def build_program():
    nc = bass.Bass("TRN2", target_bir_lowering=False)
    ap = {}

    def din(name, shape, dt):
        ap[name] = nc.dram_tensor(name, list(shape), dt, kind="ExternalInput").ap()

    din("x_in", (B * T, D), F32)
    din("w_qkv", (D, 3 * D), F32)
    din("w_out", (D, D), F32)
    din("b_out", (1, D), F32)
    din("g1r", (1, D), F32)
    din("b1r", (1, D), F32)
    din("w1", (4 * D, D), F32)
    din("bh1", (1, 4 * D), F32)
    din("g2r", (1, 4 * D), F32)
    din("b2r", (1, 4 * D), F32)
    din("w2", (D, 4 * D), F32)
    din("bh2", (1, D), F32)
    din("c0", (B, D), F32)
    din("wmod0", (D, 6 * D), F32)
    din("Wq", (D, D), F32)
    din("bq", (D, 1), F32)
    din("Wk", (D, D), F32)
    din("bk", (D, 1), F32)
    din("Wv", (D, D), F32)
    din("bv", (D, 1), F32)
    din("Wo", (D, D), F32)
    din("bo", (1, D), F32)
    din("slopes", (1, 16), F32)
    din("w8", (D, D), U8)
    din("wsc", (1, 1), F32)
    din("bf8", (1, D), F32)
    din("c1", (B, D), F32)
    din("wmod1", (D, 6 * D), F32)
    din("identf", (128, 128), F32)
    din("identb", (128, 128), BF16)
    din("maskc", (128, 4 * 512), BF16)
    din("ramp", (128, 16), F32)

    y_out = nc.dram_tensor("y", [B * T, D], F32, kind="ExternalOutput").ap()

    def scr(name, shape, dt):
        ap[name] = nc.dram_tensor(name, list(shape), dt, kind="Internal").ap()

    scr("xs", (B * T, D), F32)
    scr("hfm", (D, T), F32)
    scr("ctx", (D, T), F32)
    scr("wqf", (D, D), F32)
    scr("wkf", (D, D), F32)
    scr("wvf", (D, D), F32)
    scr("wof", (D, D), F32)
    scr("w1q", (D, 4 * D), BF16)
    scr("w2q", (D, 4 * D), BF16)
    scr("wf8", (D, D), BF16)
    scr("mods", (2 * B, 6 * D), F32)

    k = K()
    lds = [k.sem(f"ld{i}") for i in range(8)]
    sts = [k.sem(f"st{i}") for i in range(4)]
    idx = {"l": 0, "s": 0}
    sT, sV, sS, BAR = k.sem("sT"), k.sem("sV"), k.sem("sS"), k.sem("bar")

    def next_ld():
        idx["l"] = (idx["l"] + 1) % 8
        return lds[idx["l"]]

    def next_st():
        idx["s"] = (idx["s"] + 1) % 4
        return sts[idx["s"]]

    from contextlib import ExitStack
    es = ExitStack()

    def sb(name, shape, dt):
        return es.enter_context(nc.sbuf_tensor(name, list(shape), dt))

    FP = sb("FP", [128, 13824], F32)
    W1Q = sb("W1Q", [128, 32768], BF16)
    W2Q = sb("W2Q", [128, 8192], BF16)
    XQb = sb("XQb", [128, 10240], BF16)
    BCP = sb("BCP", [128, 12288], BF16)
    CB = sb("CBc", [128, 2176], BF16)
    CF = sb("CFc", [128, 144], F32)
    SV = sb("SVV", [128, 640], F32)
    OO = sb("OOr", [1, 1280], F32)
    OB = sb("OBr", [1, 5120], BF16)
    U8B = sb("U8B", [128, 1024], U8)
    IT32 = sb("IT32", [128, 1024], I32)
    ONB = sb("ONB", [1, 128], BF16)
    CST = sb("CST", [128, 16], F32)
    LNV = sb("LNV", [128, 512], F32)
    PS = [es.enter_context(nc.psum_tensor(f"PS{i}", [128, 512], F32)) for i in range(7)]
    PSB = es.enter_context(nc.psum_tensor("PSB", [128, 1024], BF16))
    for s_ in k.sems:
        s_.h = es.enter_context(nc.semaphore(s_.name))

    cvals = [-7.0 * LN2C, LN2C, -3.0 * LN2C, 1e-5, 1.0 / 1024.0, 1.0 / 4096.0,
             0.125, -1.0, 1.0, 0.0, 0.5, 60.0]
    for ci_, val in enumerate(cvals):
        nc.const_aps.register(F32, val, CST[:, ci_:ci_ + 1])

    identf = CF[:, 0:128]
    rampc = CF[:, 128:144]
    identb = CB[:, 0:128]
    maskb = CB[:, 128:2176]
    onesf = OO[0:1, 0:128]
    onescol = SV[:, 15:16]
    ALB0 = 16

    def bcast_f32(dst, row_ap, width, bias=0.0, scale=1.0):
        for i in range((width + 511) // 512):
            w = min(512, width - 512 * i)
            k.t(lambda t, i=i, w=w, row_ap=row_ap: t.matmul(PS[6][:, 0:w], onesf, row_ap[0:1, 512 * i:512 * i + w], start=True, stop=True), inc=(sT, 1))
            k.wait("s", sT, sT.n)
            k.s(lambda s, i=i, w=w, dst=dst: s.activation(out=dst[:, 512 * i:512 * i + w], in_=PS[6][:, 0:w], func=AF.Copy, bias=bias, scale=scale), inc=(sS, 1))
            k.wait("t", sS, sS.n)
            k.wait("y", sT, sT.n)

    def bcast_bf16(dst, row_bf, width):
        for i in range((width + 511) // 512):
            w = min(512, width - 512 * i)
            k.t(lambda t, i=i, w=w, row_bf=row_bf: t.matmul(PS[6][:, 0:w], ONB[0:1, :], row_bf[0:1, 512 * i:512 * i + w], start=True, stop=True), inc=(sT, 1))
            k.wait("s", sT, sT.n)
            k.s(lambda s, i=i, w=w, dst=dst: s.activation(out=dst[:, 512 * i:512 * i + w], in_=PS[6][:, 0:w], func=AF.Copy), inc=(sS, 1))
            k.wait("t", sS, sS.n)
            k.wait("y", sT, sT.n)

    def vfloor(dst, src, width, gbuf):
        k.v(lambda v: v.tensor_copy(out=IT32[:, 0:width], in_=src))
        k.v(lambda v: v.tensor_copy(out=gbuf[:, 0:width], in_=IT32[:, 0:width]))
        k.v(lambda v: v.tensor_tensor(out=dst, in0=gbuf[:, 0:width], in1=src, op=ALU.is_gt))
        k.v(lambda v: v.tensor_tensor(out=dst, in0=gbuf[:, 0:width], in1=dst, op=ALU.subtract))

    L = next_ld()
    k.load(lambda: identf, lambda: ap["identf"], L)
    k.load(lambda: identb, lambda: ap["identb"], L)
    k.load(lambda: maskb, lambda: ap["maskc"], L)
    k.load(lambda: rampc, lambda: ap["ramp"], L)
    k.load(lambda: OO[0:1, 128:144], lambda: ap["slopes"], L)
    k.load(lambda: OO[0:1, 144:145], lambda: ap["wsc"], L)
    tA = L.n
    for eng in ("t", "v", "s"):
        k.wait(eng, L, tA)
    k.v(lambda v: v.memset(onesf, 1.0))
    for ci_, val in enumerate(cvals):
        k.v(lambda v, ci_=ci_, val=val: v.memset(CST[:, ci_:ci_ + 1], val))
    k.v(lambda v: v.memset(ONB[:], 1.0))
    k.v(lambda v: v.memset(onescol, 1.0))
    k.v(lambda v: v.memset(SV[:, 13:15], 0.0))
    k.v(lambda v: v.memset(SV[0:64, 13:14], 1.0))
    k.v(lambda v: v.memset(SV[64:128, 14:15], 1.0), inc=(sV, 1))
    k.wait("t", sV, sV.n)
    k.barrier(BAR)

    cw = FP[:, 0:1024]
    cts = FP[:, 1024:1024 + B * KC]
    wt0 = FP[:, 2048:2560]
    wt1 = FP[:, 2560:3072]
    mst = FP[:, 3072:3584]
    mst2 = FP[:, 3584:4096]
    for lay in range(2):
        L = next_ld()
        k.wait("y", sT, sT.n)
        tw = k.load(lambda: cw[0:B, :], lambda lay=lay: ap["c0" if lay == 0 else "c1"], L)
        k.wait("s", L, tw)
        k.s(lambda s: s.activation(out=cw[0:B, :], in_=cw[0:B, :], func=AF.Silu), inc=(sS, 1))
        k.wait("t", sS, sS.n)
        for c in range(KC):
            k.t(lambda t, c=c: t.transpose(PS[6][:, 0:B], cw[0:B, 128 * c:128 * (c + 1)], identf[0:B, 0:B]), inc=(sT, 1))
            k.wait("v", sT, sT.n)
            k.v(lambda v, c=c: v.tensor_copy(out=cts[:, B * c:B * (c + 1)], in_=PS[6][:, 0:B]), inc=(sV, 1))
            k.wait("t", sV, sV.n)
        WMB = FP[:, 4224:8320]
        for blk in range(12):
            wm = "wmod0" if lay == 0 else "wmod1"
            Lw = next_ld()
            k.wait("y", sT, sT.n)
            tw = k.load(lambda: WMB.rearrange("p (c n) -> p c n", n=512),
                        lambda blk=blk, wm=wm: ap[wm].rearrange("(c p) n -> p c n", p=128)[:, :, 512 * blk:512 * (blk + 1)], Lw)
            k.wait("t", Lw, tw)
            for c in range(KC):
                k.t(lambda t, c=c: t.matmul(PS[6][0:B, 0:512], cts[:, B * c:B * (c + 1)], WMB[:, 512 * c:512 * (c + 1)], start=(c == 0), stop=(c == KC - 1)), inc=(sT, 1))
            k.wait("v", sT, sT.n)
            j = blk // 2
            if j in (2, 5):
                k.v(lambda v: v.tensor_scalar(out=mst[0:B, :], in0=PS[6][0:B, 0:512], scalar1=0.0, scalar2=None, op0=ALU.is_ge))
                k.v(lambda v: v.tensor_scalar(out=mst[0:B, :], in0=mst[0:B, :], scalar1=2.0, scalar2=-1.0, op0=ALU.mult, op1=ALU.add))
                k.v(lambda v: v.tensor_scalar(out=mst2[0:B, :], in0=PS[6][0:B, 0:512], scalar1=-1.0, scalar2=None, op0=ALU.mult))
                k.v(lambda v: v.tensor_tensor(out=mst2[0:B, :], in0=mst2[0:B, :], in1=PS[6][0:B, 0:512], op=ALU.max))
                k.v(lambda v: v.tensor_scalar(out=mst2[0:B, :], in0=mst2[0:B, :], scalar1=0.25, scalar2=None, op0=ALU.max))
                k.v(lambda v: v.tensor_tensor(out=mst[0:B, :], in0=mst[0:B, :], in1=mst2[0:B, :], op=ALU.mult), inc=(sV, 1))
            else:
                k.v(lambda v: v.tensor_copy(out=mst[0:B, :], in_=PS[6][0:B, 0:512]), inc=(sV, 1))
            k.wait("t", sV, sV.n)
            k.wait("g", sV, sV.n)
            S = next_st()
            ts = k.store(lambda blk=blk, lay=lay: ap["mods"][B * lay:B * (lay + 1), 512 * blk:512 * (blk + 1)], lambda: mst[0:B, :], S)
            k.wait("v", S, ts)
    k.barrier(BAR)

    wload = FP[:, 0:1024]
    wy = FP[:, 1024:2048]
    wg = FP[:, 2048:3072]
    wc = FP[:, 3072:4096]
    qb = XQb[:, 0:1024]
    tstage = XQb[:, 1024:1152]
    fstage = FP[:, 4096:4224]

    def wquant(wname, dst, nchunk, width, n_elem, slab=False):
        k.v(lambda v: v.memset(SV[:, 3:4], 0.0), inc=(sV, 1))
        nsub = width // 1024
        for ch in range(nchunk):
            for sbk in range(nsub):
                L = next_ld()
                tw = k.load(lambda: wload, lambda ch=ch, sbk=sbk, wname=wname: ap[wname][128 * ch:128 * (ch + 1), 1024 * sbk:1024 * (sbk + 1)], L)
                k.wait("v", L, tw)
                k.v(lambda v: v.tensor_reduce(out=SV[:, 4:5], in_=wload, axis=AXX, op=ALU.add, apply_absolute_value=True))
                k.v(lambda v: v.tensor_tensor(out=SV[:, 3:4], in0=SV[:, 3:4], in1=SV[:, 4:5], op=ALU.add), inc=(sV, 1))
                k.wait("y", sV, sV.n)
        k.wait("t", sV, sV.n)
        k.t(lambda t: t.matmul(PS[6][0:1, 0:1], onescol, SV[:, 3:4], start=True, stop=True), inc=(sT, 1))
        k.wait("v", sT, sT.n)
        k.v(lambda v: v.tensor_scalar(out=OO[0:1, 150:151], in0=PS[6][0:1, 0:1], scalar1=1.0 / n_elem, scalar2=1e-5, op0=ALU.mult, op1=ALU.max))
        k.v(lambda v: v.reciprocal(out=OO[0:1, 151:152], in_=OO[0:1, 150:151]), inc=(sV, 1))
        k.wait("t", sV, sV.n)
        k.t(lambda t: t.matmul(PS[6][:, 0:1], onesf, OO[0:1, 151:152], start=True, stop=True), inc=(sT, 1))
        k.wait("v", sT, sT.n)
        k.v(lambda v: v.tensor_copy(out=SV[:, 8:9], in_=PS[6][:, 0:1]), inc=(sV, 1))
        k.wait("t", sV, sV.n)
        for ch in range(nchunk):
            for sbk in range(nsub):
                L = next_ld()
                tw = k.load(lambda: wload, lambda ch=ch, sbk=sbk, wname=wname: ap[wname][128 * ch:128 * (ch + 1), 1024 * sbk:1024 * (sbk + 1)], L)
                k.wait("v", L, tw)
                k.wait("v", sT, sT.n)
                k.v(lambda v: v.tensor_scalar(out=wy, in0=wload, scalar1=SV[:, 8:9], scalar2=0.5, op0=ALU.mult, op1=ALU.add))
                vfloor(wy, wy, 1024, wg)
                k.v(lambda v: v.tensor_scalar(out=qb, in0=wy, scalar1=-1.0, scalar2=1.0, op0=ALU.max, op1=ALU.min), inc=(sV, 1))
                k.wait("t", sV, sV.n)
                k.wait("y", sV, sV.n)
                for g4 in range(2):
                    for cc in range(4):
                        c = 4 * g4 + cc
                        k.t(lambda t, c=c, cc=cc: t.transpose(PSB[:, 128 * cc:128 * (cc + 1)], qb[:, 128 * c:128 * (c + 1)], identb), inc=(sT, 1) if cc == 3 else None)
                    k.wait("s", sT, sT.n)
                    k.s(lambda s: s.activation(out=XQb[:, 1024:1536], in_=PSB[:, 0:512], func=AF.Copy), inc=(sS, 1))
                    k.wait("t", sS, sS.n)
                    k.wait("g", sS, sS.n)
                    S = next_st()
                    if slab:
                        g0 = 8 * sbk + 4 * g4
                        ts = k.store(lambda ch=ch, g0=g0, dst=dst: dst[128 * ch:128 * (ch + 1), 128 * g0:128 * (g0 + 4)],
                                     lambda: XQb[:, 1024:1536], S)
                    else:
                        ts = k.store(lambda ch=ch, sbk=sbk, g4=g4, dst=dst: dst.rearrange("(c p) n -> p c n", p=128)[:, 8 * sbk + 4 * g4:8 * sbk + 4 * g4 + 4, 128 * ch:128 * (ch + 1)],
                                     lambda: XQb[:, 1024:1536].rearrange("p (c n) -> p c n", n=128), S)
                    k.wait("s", S, ts)

    wquant("w1", ap["w1q"], 32, 1024, 4096.0 * 1024.0)
    wquant("w2", ap["w2q"], 8, 4096, 1024.0 * 4096.0, slab=True)

    for wname, dname in (("Wq", "wqf"), ("Wk", "wkf"), ("Wv", "wvf"), ("Wo", "wof")):
        for ch in range(KC):
            L = next_ld()
            tw = k.load(lambda: wload, lambda ch=ch, wname=wname: ap[wname][128 * ch:128 * (ch + 1), 0:1024], L)
            k.wait("t", L, tw)
            for c in range(KC):
                k.t(lambda t, c=c: t.transpose(PS[6][:, 0:128], wload[:, 128 * c:128 * (c + 1)], identf), inc=(sT, 1))
                k.wait("v", sT, sT.n)
                k.v(lambda v: v.tensor_copy(out=fstage, in_=PS[6][:, 0:128]), inc=(sV, 1))
                k.wait("t", sV, sV.n)
                k.wait("y", sV, sV.n)
                k.wait("g", sV, sV.n)
                S = next_st()
                ts = k.store(lambda ch=ch, c=c, dname=dname: ap[dname][128 * c:128 * (c + 1), 128 * ch:128 * (ch + 1)], lambda: fstage, S)
                k.wait("v", S, ts)

    for ch in range(KC):
        L = next_ld()
        tw = k.load(lambda: U8B[:, 0:1024], lambda ch=ch: ap["w8"][128 * ch:128 * (ch + 1), :], L)
        k.wait("v", L, tw)
        k.wait("v", sT, sT.n)
        u, sg, e_, m_ = wload, wy, wg, wc
        k.v(lambda v: v.tensor_copy(out=u, in_=U8B[:, 0:1024]))
        k.v(lambda v: v.tensor_scalar(out=sg, in0=u, scalar1=128.0, scalar2=None, op0=ALU.is_ge))
        k.v(lambda v: v.scalar_tensor_tensor(out=u, in0=sg, scalar=-128.0, in1=u, op0=ALU.mult, op1=ALU.add))
        k.v(lambda v: v.tensor_scalar(out=e_, in0=u, scalar1=0.125, scalar2=None, op0=ALU.mult))
        vfloor(e_, e_, 1024, m_)
        k.v(lambda v: v.scalar_tensor_tensor(out=m_, in0=e_, scalar=-8.0, in1=u, op0=ALU.mult, op1=ALU.add))
        k.v(lambda v: v.tensor_scalar(out=u, in0=e_, scalar1=1.0, scalar2=None, op0=ALU.is_ge))
        k.v(lambda v: v.scalar_tensor_tensor(out=m_, in0=m_, scalar=0.125, in1=u, op0=ALU.mult, op1=ALU.add))
        k.v(lambda v: v.tensor_scalar(out=e_, in0=e_, scalar1=1.0, scalar2=None, op0=ALU.max), inc=(sV, 1))
        k.wait("s", sV, sV.n)
        k.s(lambda s: s.activation(out=e_, in_=e_, func=AF.Exp, bias=-7.0 * LN2C, scale=LN2C), inc=(sS, 1))
        k.wait("v", sS, sS.n)
        k.v(lambda v: v.tensor_scalar(out=sg, in0=sg, scalar1=-2.0, scalar2=1.0, op0=ALU.mult, op1=ALU.add))
        k.v(lambda v: v.tensor_tensor(out=m_, in0=m_, in1=e_, op=ALU.mult))
        k.v(lambda v: v.tensor_tensor(out=qb, in0=m_, in1=sg, op=ALU.mult), inc=(sV, 1))
        k.wait("t", sV, sV.n)
        k.wait("y", sV, sV.n)
        for c in range(KC):
            k.t(lambda t, c=c: t.transpose(PSB[:, 0:128], qb[:, 128 * c:128 * (c + 1)], identb), inc=(sT, 1))
            k.wait("s", sT, sT.n)
            k.s(lambda s: s.activation(out=tstage, in_=PSB[:, 0:128], func=AF.Copy), inc=(sS, 1))
            k.wait("t", sS, sS.n)
            k.wait("g", sS, sS.n)
            S = next_st()
            ts = k.store(lambda ch=ch, c=c: ap["wf8"][128 * c:128 * (c + 1), 128 * ch:128 * (ch + 1)], lambda: tstage, S)
            k.wait("s", S, ts)

    for h in range(16):
        k.t(lambda t, h=h: t.matmul(PS[6][:, 0:1], onesf, OO[0:1, 128 + h:129 + h], start=True, stop=True), inc=(sT, 1))
        k.wait("v", sT, sT.n)
        k.v(lambda v: v.tensor_copy(out=SV[:, 9:10], in_=PS[6][:, 0:1]), inc=(sV, 1))
        k.wait("t", sV, sV.n)
        k.v(lambda v, h=h: v.tensor_scalar(out=SV[:, ALB0 + 16 * h:ALB0 + 16 * (h + 1)], in0=rampc, scalar1=SV[:, 9:10], scalar2=None, op0=ALU.mult), inc=(sV, 1))
    k.barrier(BAR)

    def ln_mod_phase(lay, b, half):
        xt = FP[:, 0:1024]
        ht = FP[:, 1024:2048]
        sq = FP[:, 2048:3072]
        bc1 = FP[:, 3072:4096]
        bc2 = FP[:, 4096:5120]
        hs2 = FP[:, 5120:5632]
        j0 = 1 if half == 0 else 4
        Lr = next_ld()
        t2 = k.load(lambda: OO[0:1, 256:256 + 1024], lambda: ap["mods"][B * lay + b:B * lay + b + 1, D * j0:D * (j0 + 1)], Lr)
        k.wait("t", Lr, t2)
        bcast_f32(bc1, OO[0:1, 256:256 + 1024], 1024, bias=1.0)
        Lr = next_ld()
        t2 = k.load(lambda: OO[0:1, 256:256 + 1024], lambda: ap["mods"][B * lay + b:B * lay + b + 1, D * (j0 - 1):D * j0], Lr)
        k.wait("t", Lr, t2)
        bcast_f32(bc2, OO[0:1, 256:256 + 1024], 1024)
        src = ap["x_in"] if (lay == 0 and half == 0) else ap["xs"]
        for tt in range(16):
            row = T * b + 128 * tt
            L = next_ld()
            k.wait("y", sV, sV.n)
            tw = k.load(lambda: xt, lambda row=row, src=src: src[row:row + 128, :], L)
            k.wait("v", L, tw)
            k.v(lambda v: v.tensor_reduce(out=SV[:, 0:1], in_=xt, axis=AXX, op=ALU.add))
            k.v(lambda v: v.tensor_scalar(out=SV[:, 0:1], in0=SV[:, 0:1], scalar1=1.0 / D, scalar2=None, op0=ALU.mult))
            k.v(lambda v: v.tensor_scalar(out=ht, in0=xt, scalar1=SV[:, 0:1], scalar2=None, op0=ALU.subtract), inc=(sV, 1))
            k.wait("s", sV, sV.n)
            k.s(lambda s: s.activation(out=sq, in_=ht, func=AF.Square), inc=(sS, 1))
            k.wait("v", sS, sS.n)
            k.v(lambda v: v.tensor_reduce(out=SV[:, 1:2], in_=sq, axis=AXX, op=ALU.add), inc=(sV, 1))
            k.wait("s", sV, sV.n)
            k.s(lambda s: s.activation(out=SV[:, 1:2], in_=SV[:, 1:2], func=AF.Sqrt, bias=1e-5, scale=1.0 / D), inc=(sS, 1))
            k.wait("v", sS, sS.n)
            k.v(lambda v: v.reciprocal(out=SV[:, 2:3], in_=SV[:, 1:2]))
            k.v(lambda v: v.tensor_scalar(out=ht, in0=ht, scalar1=SV[:, 2:3], scalar2=None, op0=ALU.mult))
            k.v(lambda v: v.tensor_tensor(out=ht, in0=ht, in1=bc1, op=ALU.mult))
            k.v(lambda v: v.tensor_tensor(out=ht, in0=ht, in1=bc2, op=ALU.add), inc=(sV, 1))
            k.wait("t", sV, sV.n)
            for g4 in range(2):
                for cc in range(4):
                    c = 4 * g4 + cc
                    k.t(lambda t, c=c, cc=cc: t.transpose(PS[6][:, 128 * cc:128 * (cc + 1)], ht[:, 128 * c:128 * (c + 1)], identf, ),
                        inc=(sT, 1) if cc == 3 else None)
                k.wait("v", sT, sT.n)
                k.v(lambda v: v.tensor_copy(out=hs2, in_=PS[6][:, 0:512]), inc=(sV, 1))
                k.wait("t", sV, sV.n)
                k.wait("g", sV, sV.n)
                S = next_st()
                ts = k.store(lambda g4=g4, tt=tt: ap["hfm"].rearrange("(c p) n -> p c n", p=128)[:, 4 * g4:4 * g4 + 4, 128 * tt:128 * (tt + 1)],
                             lambda: hs2.rearrange("p (c n) -> p c n", n=128), S)
                k.wait("v", S, ts)
        k.barrier(BAR)

    def attn_phase(lay, b):
        QP = FP[:, 0:2048]
        KP = FP[:, 2048:4096]
        HST = FP[:, 4096:8192]
        VP_A = FP[:, 8192:9232]
        VP_B = FP[:, 9232:10272]
        VS = FP[:, 10272:10784]
        PPa = FP[:, 10784:11296]
        PPb = FP[:, 11296:11808]
        WSL = FP[:, 11808:12192]
        ONESA = FP[:, 13216:13344]
        k.v(lambda v: v.memset(ONESA, 1.0), inc=(sV, 1))
        k.wait("t", sV, sV.n)
        for pair in range(NP):
            if lay == 1:
                Lb = next_ld()
                k.wait("y", sV, sV.n)
                k.load(lambda: SV[:, 10:11], lambda pair=pair: ap["bq"][128 * pair:128 * (pair + 1), :], Lb)
                k.load(lambda: SV[:, 11:12], lambda pair=pair: ap["bk"][128 * pair:128 * (pair + 1), :], Lb)
                tb = k.load(lambda: SV[:, 12:13], lambda pair=pair: ap["bv"][128 * pair:128 * (pair + 1), :], Lb)
                k.wait("v", Lb, tb)
                k.wait("s", Lb, tb)
            WQS = FP[:, 12192:13216]
            KQS = FP[:, 10784:11808]
            for n in range(QB):
                Lh = next_ld()
                tws = k.load(lambda: HST.rearrange("p (c n) -> p c n", n=512),
                             lambda n=n: ap["hfm"].rearrange("(c p) n -> p c n", p=128)[:, :, 512 * n:512 * (n + 1)], Lh)
                Lw = next_ld()
                k.wait("y", sS, sS.n)
                k.wait("y", sT, sT.n)
                wq_src = "w_qkv" if lay == 0 else "wqf"
                wk_src = "w_qkv" if lay == 0 else "wkf"
                wv_src = "w_qkv" if lay == 0 else "wvf"
                qo = 128 * pair
                ko = D + 128 * pair if lay == 0 else 128 * pair
                vo = 2 * D + 128 * pair if lay == 0 else 128 * pair
                k.load(lambda: WQS.rearrange("p (c n) -> p c n", n=128),
                       lambda qo=qo, wq_src=wq_src: ap[wq_src].rearrange("(c p) n -> p c n", p=128)[:, :, qo:qo + 128], Lw)
                tww = k.load(lambda: KQS.rearrange("p (c n) -> p c n", n=128),
                             lambda ko=ko, wk_src=wk_src: ap[wk_src].rearrange("(c p) n -> p c n", p=128)[:, :, ko:ko + 128], Lw)
                k.wait("t", Lw, tww)
                k.wait("t", Lh, tws)
                for c in range(KC):
                    h_ap = HST[:, 512 * c:512 * (c + 1)]
                    k.t(lambda t, h_ap=h_ap, c=c: t.matmul(PS[0][:, 0:512], WQS[:, 128 * c:128 * (c + 1)], h_ap, start=(c == 0), stop=(c == KC - 1)))
                    k.t(lambda t, h_ap=h_ap, c=c: t.matmul(PS[1][:, 0:512], KQS[:, 128 * c:128 * (c + 1)], h_ap, start=(c == 0), stop=(c == KC - 1)), inc=(sT, 1))
                k.wait("y", sT, sT.n)
                Lv = next_ld()
                tww = k.load(lambda: WQS.rearrange("p (c n) -> p c n", n=128),
                             lambda vo=vo, wv_src=wv_src: ap[wv_src].rearrange("(c p) n -> p c n", p=128)[:, :, vo:vo + 128], Lv)
                k.wait("t", Lv, tww)
                for c in range(KC):
                    h_ap = HST[:, 512 * c:512 * (c + 1)]
                    k.t(lambda t, h_ap=h_ap, c=c: t.matmul(PS[2][:, 0:512], WQS[:, 128 * c:128 * (c + 1)], h_ap, start=(c == 0), stop=(c == KC - 1)), inc=(sT, 1))
                k.wait("y", sT, sT.n)
                k.wait("v", sT, sT.n)
                k.wait("v", sS, sS.n)
                k.wait("s", sT, sT.n)
                if lay == 0:
                    k.v(lambda v, n=n: v.tensor_copy(out=QP[:, 512 * n:512 * (n + 1)], in_=PS[0][:, 0:512]), inc=(sV, 1))
                    k.s(lambda s, n=n: s.activation(out=KP[:, 512 * n:512 * (n + 1)], in_=PS[1][:, 0:512], func=AF.Copy, scale=0.125), inc=(sS, 1))
                    k.v(lambda v: v.tensor_copy(out=VS, in_=PS[2][:, 0:512]), inc=(sV, 1))
                else:
                    k.v(lambda v, n=n: v.tensor_scalar(out=QP[:, 512 * n:512 * (n + 1)], in0=PS[0][:, 0:512], scalar1=SV[:, 10:11], scalar2=None, op0=ALU.add), inc=(sV, 1))
                    k.v(lambda v, n=n: v.tensor_scalar(out=KP[:, 512 * n:512 * (n + 1)], in0=PS[1][:, 0:512], scalar1=SV[:, 11:12], scalar2=0.125, op0=ALU.add, op1=ALU.mult))
                    k.inc("s", sS)
                    k.v(lambda v: v.tensor_scalar(out=VS, in0=PS[2][:, 0:512], scalar1=SV[:, 12:13], scalar2=None, op0=ALU.add), inc=(sV, 1))
                k.wait("t", sV, sV.n)
                k.wait("t", sS, sS.n)
                for st4 in range(4):
                    k.t(lambda t, st4=st4: t.transpose(PS[3][:, 0:128], VS[:, 128 * st4:128 * (st4 + 1)], identf), inc=(sT, 1))
                    k.wait("v", sT, sT.n)
                    ci = 4 * n + st4
                    k.v(lambda v, ci=ci: v.tensor_copy(out=VP_A[:, 65 * ci:65 * ci + 64], in_=PS[3][:, 0:64]))
                    k.v(lambda v, ci=ci: v.tensor_copy(out=VP_B[:, 65 * ci:65 * ci + 64], in_=PS[3][:, 64:128]), inc=(sV, 1))
                    k.wait("t", sV, sV.n)
            for ci_ in range(16):
                k.v(lambda v, ci_=ci_: v.memset(VP_A[:, 65 * ci_ + 64:65 * ci_ + 65], 1.0))
                k.v(lambda v, ci_=ci_: v.memset(VP_B[:, 65 * ci_ + 64:65 * ci_ + 65], 1.0))
            k.inc("v", sV)
            k.wait("t", sV, sV.n)
            k.wait("s", sV, sV.n)
            for src, oi in ((QP, 0), (KP, 2)):
                for blk in range(4):
                    k.s(lambda s, blk=blk, src=src: s.activation(out=VS, in_=src[:, 512 * blk:512 * (blk + 1)], func=AF.Square), inc=(sS, 1))
                    k.wait("t", sS, sS.n)
                    for hh in range(2):
                        k.t(lambda t, hh=hh: t.matmul(PS[3][0:1, 0:512], SV[:, 13 + hh:14 + hh], VS, start=True, stop=True), inc=(sT, 1))
                        k.wait("v", sT, sT.n)
                        if blk == 0:
                            k.v(lambda v, oi=oi, hh=hh: v.tensor_reduce(out=OO[0:1, 160 + oi + hh:161 + oi + hh], in_=PS[3][0:1, 0:512], axis=AXX, op=ALU.max), inc=(sV, 1))
                        else:
                            k.v(lambda v: v.tensor_reduce(out=OO[0:1, 166:167], in_=PS[3][0:1, 0:512], axis=AXX, op=ALU.max))
                            k.v(lambda v, oi=oi, hh=hh: v.tensor_tensor(out=OO[0:1, 160 + oi + hh:161 + oi + hh], in0=OO[0:1, 160 + oi + hh:161 + oi + hh], in1=OO[0:1, 166:167], op=ALU.max), inc=(sV, 1))
                        k.wait("t", sV, sV.n)
                    k.wait("s", sV, sV.n)
            k.wait("s", sV, sV.n)
            k.s(lambda s: s.activation(out=OO[0:1, 160:164], in_=OO[0:1, 160:164], func=AF.Sqrt), inc=(sS, 1))
            k.wait("v", sS, sS.n)
            k.v(lambda v: v.tensor_tensor(out=OO[0:1, 164:165], in0=OO[0:1, 160:161], in1=OO[0:1, 162:163], op=ALU.mult))
            k.v(lambda v: v.tensor_tensor(out=OO[0:1, 165:166], in0=OO[0:1, 161:162], in1=OO[0:1, 163:164], op=ALU.mult), inc=(sV, 1))
            k.wait("t", sV, sV.n)
            for hh in range(2):
                k.t(lambda t, hh=hh: t.matmul(PS[3][:, 0:1], onesf, OO[0:1, 164 + hh:165 + hh], start=True, stop=True), inc=(sT, 1))
                k.wait("s", sT, sT.n)
                k.s(lambda s, hh=hh: s.activation(out=SV[:, 6 + hh:7 + hh], in_=PS[3][:, 0:1], func=AF.Copy, bias=60.0, scale=-1.0), inc=(sS, 1))
                k.wait("t", sS, sS.n)
                k.wait("v", sS, sS.n)
            if lay == 1:
                for hh in range(2):
                    hid = 2 * pair + hh
                    k.v(lambda v, hh=hh, hid=hid: v.tensor_scalar(out=SV[:, 272 + 16 * hh:288 + 16 * hh], in0=SV[:, ALB0 + 16 * hid:ALB0 + 16 * (hid + 1)], scalar1=SV[:, 6 + hh:7 + hh], scalar2=None, op0=ALU.add), inc=(sV, 1))
                k.wait("s", sV, sV.n)
            for n in range(QB):
                clist = list(range(16)) if lay == 1 else list(range(4 * n + 4))
                k.wait("t", sV, sV.n)
                pend = None
                for ci, c in enumerate(clist):
                    par = ci % 2
                    psA = PS[0] if par == 0 else PS[1]
                    psB = PS[4] if par == 0 else PS[5]
                    ppa = FP[:, 10784:11296] if par == 0 else FP[:, 12192:12704]
                    ppb = FP[:, 11296:11808] if par == 0 else FP[:, 12704:13216]
                    crossing = (lay == 0) and (c >= 4 * n)
                    k.t(lambda t, c=c, n=n, psA=psA, crossing=crossing: t.matmul(psA[:, 0:512], KP[0:64, 128 * c:128 * (c + 1)], QP[0:64, 512 * n:512 * (n + 1)], start=True, stop=not crossing))
                    k.t(lambda t, c=c, n=n, psB=psB, crossing=crossing: t.matmul(psB[:, 0:512], KP[64:128, 128 * c:128 * (c + 1)], QP[64:128, 512 * n:512 * (n + 1)], start=True, stop=not crossing),
                        inc=None if crossing else (sT, 1))
                    if crossing:
                        m = c - 4 * n
                        k.t(lambda t, m=m, psA=psA: t.matmul(psA[:, 0:512], identb, maskb[:, 512 * m:512 * (m + 1)], start=False, stop=True))
                        k.t(lambda t, m=m, psB=psB: t.matmul(psB[:, 0:512], identb, maskb[:, 512 * m:512 * (m + 1)], start=False, stop=True), inc=(sT, 1))
                    if lay == 0:
                        biasA, biasB = SV[:, 6:7], SV[:, 7:8]
                    else:
                        biasA, biasB = SV[:, 272 + c:273 + c], SV[:, 288 + c:289 + c]
                    k.wait("s", sT, sT.n)
                    k.s(lambda s, psA=psA, biasA=biasA, ppa=ppa: s.activation(out=ppa, in_=psA[:, 0:512], func=AF.Exp, bias=biasA, scale=1.0))
                    k.s(lambda s, psB=psB, biasB=biasB, ppb=ppb: s.activation(out=ppb, in_=psB[:, 0:512], func=AF.Exp, bias=biasB, scale=1.0), inc=(sS, 1))
                    if pend is not None:
                        pc, pci, pppa, pppb = pend
                        k.wait("t", sS, sS.n - 1)
                        k.t(lambda t, pc=pc, pci=pci, pppa=pppa: t.matmul(PS[2][0:65, 0:512], VP_A[:, 65 * pc:65 * (pc + 1)], pppa, start=(pci == 0), stop=False))
                        k.t(lambda t, pc=pc, pci=pci, pppb=pppb: t.matmul(PS[3][0:65, 0:512], VP_B[:, 65 * pc:65 * (pc + 1)], pppb, start=(pci == 0), stop=False), inc=(sT, 1))
                    pend = (c, ci, ppa, ppb)
                pc, pci, pppa, pppb = pend
                k.wait("t", sS, sS.n)
                k.t(lambda t, pc=pc, pci=pci, pppa=pppa: t.matmul(PS[2][0:65, 0:512], VP_A[:, 65 * pc:65 * (pc + 1)], pppa, start=(pci == 0), stop=True))
                k.t(lambda t, pc=pc, pci=pci, pppb=pppb: t.matmul(PS[3][0:65, 0:512], VP_B[:, 65 * pc:65 * (pc + 1)], pppb, start=(pci == 0), stop=True), inc=(sT, 1))
                k.wait("v", sT, sT.n)
                for hh, psx in ((0, PS[2]), (1, PS[3])):
                    k.v(lambda v, psx=psx: v.tensor_copy(out=LNV[64:65, 0:512], in_=psx[64:65, 0:512]), inc=(sV, 1))
                    k.v(lambda v: v.reciprocal(out=LNV[64:65, 0:512], in_=LNV[64:65, 0:512]), inc=(sV, 1))
                    k.wait("t", sV, sV.n)
                    k.t(lambda t: t.matmul(PS[6][:, 0:512], ONESA[64:65, 0:128], LNV[64:65, 0:512], start=True, stop=True), inc=(sT, 1))
                    k.wait("v", sT, sT.n)
                    k.v(lambda v: v.tensor_copy(out=LNV[0:64, 0:512], in_=PS[6][0:64, 0:512]), inc=(sV, 1))
                    k.v(lambda v, psx=psx: v.tensor_tensor(out=VS[0:64, 0:512], in0=psx[0:64, 0:512], in1=LNV[0:64, 0:512], op=ALU.mult), inc=(sV, 1))
                    k.wait("t", sV, sV.n)
                    k.wait("g", sV, sV.n)
                    S = next_st()
                    row = 128 * pair + 64 * hh
                    ts = k.store(lambda row=row, n=n: ap["ctx"][row:row + 64, 512 * n:512 * (n + 1)], lambda: VS[0:64, 0:512], S)
                    k.wait("v", S, ts)
        k.barrier(BAR)

    def outproj_phase(lay, b):
        OSb = FP[:, 0:4096]
        CXT = FP[:, 4096:8192]
        WS2 = FP[:, 8192:8448]
        XT2 = FP[:, 8448:9472]
        TT2 = FP[:, 9472:10496]
        BOC = FP[:, 10496:11520]
        GAC = FP[:, 11520:12544]
        Lb = next_ld()
        bname = "b_out" if lay == 0 else "bo"
        tb = k.load(lambda: OO[0:1, 256:256 + 1024], lambda bname=bname: ap[bname], Lb)
        k.wait("t", Lb, tb)
        bcast_f32(BOC, OO[0:1, 256:256 + 1024], 1024)
        Lr = next_ld()
        t2 = k.load(lambda: OO[0:1, 256:256 + 1024], lambda: ap["mods"][B * lay + b:B * lay + b + 1, 2 * D:3 * D], Lr)
        k.wait("t", Lr, t2)
        bcast_f32(GAC, OO[0:1, 256:256 + 1024], 1024)
        src_x = ap["x_in"] if lay == 0 else ap["xs"]
        wsrc = ap["w_out"] if lay == 0 else ap["wof"]
        WOS = FP[:, 12544:13568]
        for n in range(QB):
            Lc = next_ld()
            tcs = k.load(lambda: CXT.rearrange("p (c n) -> p c n", n=512),
                         lambda n=n: ap["ctx"].rearrange("(c p) n -> p c n", p=128)[:, :, 512 * n:512 * (n + 1)], Lc)
            k.wait("t", Lc, tcs)
            for m in range(KC):
                psm = PS[0] if m % 2 == 0 else PS[1]
                Lw = next_ld()
                k.wait("y", sT, sT.n)
                tww = k.load(lambda: WOS.rearrange("p (c n) -> p c n", n=128),
                             lambda m=m, wsrc=wsrc: wsrc.rearrange("(c p) n -> p c n", p=128)[:, :, 128 * m:128 * (m + 1)], Lw)
                k.wait("t", Lw, tww)
                for c in range(KC):
                    k.t(lambda t, c=c, psm=psm, m=m: t.matmul(psm[:, 0:512], WOS[:, 128 * c:128 * (c + 1)], CXT[:, 512 * c:512 * (c + 1)], start=(c == 0), stop=(c == KC - 1)), inc=(sT, 1))
                k.wait("v", sT, sT.n)
                k.v(lambda v, m=m, psm=psm: v.tensor_copy(out=OSb[:, 512 * m:512 * (m + 1)], in_=psm[:, 0:512]), inc=(sV, 1))
                k.wait("t", sV, sV.n)
            k.wait("y", sT, sT.n)
            for st4 in range(4):
                tt = 4 * n + st4
                row = T * b + 128 * tt
                Lx = next_ld()
                k.wait("y", sV, sV.n)
                tx = k.load(lambda: XT2, lambda row=row, src_x=src_x: src_x[row:row + 128, :], Lx)
                for m2 in range(2):
                    for cc in range(4):
                        mm = 4 * m2 + cc
                        k.t(lambda t, mm=mm, st4=st4, m2=m2, cc=cc: t.transpose(PS[2 + m2][:, 128 * cc:128 * (cc + 1)], OSb[:, 512 * mm + 128 * st4:512 * mm + 128 * (st4 + 1)], identf),
                            inc=(sT, 1) if cc == 3 else None)
                k.wait("v", sT, sT.n)
                k.wait("v", Lx, tx)
                k.v(lambda v: v.tensor_tensor(out=TT2[:, 0:512], in0=PS[2][:, 0:512], in1=BOC[:, 0:512], op=ALU.add))
                k.v(lambda v: v.tensor_tensor(out=TT2[:, 512:1024], in0=PS[3][:, 0:512], in1=BOC[:, 512:1024], op=ALU.add), inc=(sV, 1))
                k.wait("t", sV, sV.n)
                k.v(lambda v: v.tensor_tensor(out=TT2, in0=TT2, in1=GAC, op=ALU.mult))
                k.v(lambda v: v.tensor_tensor(out=XT2, in0=XT2, in1=TT2, op=ALU.add), inc=(sV, 1))
                k.wait("g", sV, sV.n)
                S = next_st()
                ts = k.store(lambda row=row: ap["xs"][row:row + 128, :], lambda: XT2, S)
                k.wait("v", S, ts)
                k.wait("y", S, ts)
        k.barrier(BAR)

    def ffn0_phase():
        xt = FP[:, 0:1024]
        zt = FP[:, 1024:2048]
        H1 = FP[:, 2048:6144]
        SQF = FP[:, 6144:10240]
        bc1 = FP[:, 10240:11264]
        bc2 = FP[:, 11264:12288]
        bcg = FP[:, 12288:13312]
        OS1 = FP[:, 13312:13824]
        g1b = BCP[:, 0:1024]
        b1b = BCP[:, 1024:2048]
        bh1b = BCP[:, 2048:6144]
        g2b = BCP[:, 6144:10240]
        b2b = BCP[:, 10240:11264]
        xq = XQb[:, 0:1024]
        xqf = XQb[:, 1024:2048]
        xq2 = XQb[:, 2048:6144]
        xq2f = XQb[:, 6144:10240]
        Lw = next_ld()
        tws = []
        for c in range(KC):
            tws.append(k.load(lambda c=c: W1Q[:, 4096 * c:4096 * (c + 1)], lambda c=c: ap["w1q"][128 * c:128 * (c + 1), :], Lw))
        for nmsrc, dst, wid in (("g1r", g1b, 1024), ("b1r", b1b, 1024)):
            Lb = next_ld()
            k.wait("y", sV, sV.n)
            tb = k.load(lambda: OO[0:1, 256:256 + 1024], lambda nmsrc=nmsrc: ap[nmsrc], Lb)
            k.wait("v", Lb, tb)
            k.v(lambda v: v.tensor_copy(out=OB[0:1, 0:1024], in_=OO[0:1, 256:256 + 1024]), inc=(sV, 1))
            k.wait("t", sV, sV.n)
            bcast_bf16(dst, OB[0:1, 0:1024], wid)
        for part in range(4):
            Lb = next_ld()
            k.wait("y", sV, sV.n)
            tb = k.load(lambda: OO[0:1, 256:256 + 1024], lambda part=part: ap["bh1"][0:1, 1024 * part:1024 * (part + 1)], Lb)
            k.wait("v", Lb, tb)
            k.v(lambda v: v.tensor_copy(out=OB[0:1, 0:1024], in_=OO[0:1, 256:256 + 1024]), inc=(sV, 1))
            k.wait("t", sV, sV.n)
            bcast_bf16(bh1b[:, 1024 * part:1024 * (part + 1)], OB[0:1, 0:1024], 1024)
        for part in range(4):
            Lb = next_ld()
            k.wait("y", sV, sV.n)
            tb = k.load(lambda: OO[0:1, 256:256 + 1024], lambda part=part: ap["g2r"][0:1, 1024 * part:1024 * (part + 1)], Lb)
            k.wait("v", Lb, tb)
            k.v(lambda v: v.tensor_copy(out=OB[0:1, 0:1024], in_=OO[0:1, 256:256 + 1024]), inc=(sV, 1))
            k.wait("t", sV, sV.n)
            bcast_bf16(g2b[:, 1024 * part:1024 * (part + 1)], OB[0:1, 0:1024], 1024)
        b2parts = []
        for part in range(4):
            Lb = next_ld()
            k.wait("y", sV, sV.n)
            tb = k.load(lambda: OO[0:1, 256:256 + 1024], lambda part=part: ap["b2r"][0:1, 1024 * part:1024 * (part + 1)], Lb)
            k.wait("v", Lb, tb)
            k.v(lambda v, part=part: v.tensor_copy(out=OB[0:1, 1024 * (part + 1):1024 * (part + 2)], in_=OO[0:1, 256:256 + 1024]), inc=(sV, 1))
            k.wait("t", sV, sV.n)
            k.v(lambda v, part=part: v.tensor_copy(out=OB[0:1, 1024 + 128 * 0:1024 + 128 * 0] if False else OB[0:1, 0:1024], in_=OO[0:1, 256:256 + 1024]), inc=(sV, 1))
        wm1 = SV[:, 304:305]
        wm2 = SV[:, 305:306]
        for wname, dstv, n_elem, nch, wid in (("w1", wm1, 4096.0 * 1024.0, 32, 1024), ("w2", wm2, 1024.0 * 4096.0, 8, 4096)):
            k.v(lambda v: v.memset(SV[:, 3:4], 0.0), inc=(sV, 1))
            for ch in range(nch):
                for sbk in range(wid // 1024):
                    L = next_ld()
                    tw = k.load(lambda: SQF[:, 0:1024], lambda ch=ch, sbk=sbk, wname=wname: ap[wname][128 * ch:128 * (ch + 1), 1024 * sbk:1024 * (sbk + 1)], L)
                    k.wait("v", L, tw)
                    k.v(lambda v: v.tensor_reduce(out=SV[:, 4:5], in_=SQF[:, 0:1024], axis=AXX, op=ALU.add, apply_absolute_value=True))
                    k.v(lambda v: v.tensor_tensor(out=SV[:, 3:4], in0=SV[:, 3:4], in1=SV[:, 4:5], op=ALU.add), inc=(sV, 1))
                    k.wait("y", sV, sV.n)
            k.wait("t", sV, sV.n)
            k.t(lambda t: t.matmul(PS[6][0:1, 0:1], onescol, SV[:, 3:4], start=True, stop=True), inc=(sT, 1))
            k.wait("v", sT, sT.n)
            k.v(lambda v, n_elem=n_elem: v.tensor_scalar(out=OO[0:1, 150:151], in0=PS[6][0:1, 0:1], scalar1=1.0 / n_elem, scalar2=1e-5, op0=ALU.mult, op1=ALU.max), inc=(sV, 1))
            k.wait("t", sV, sV.n)
            k.t(lambda t: t.matmul(PS[6][:, 0:1], onesf, OO[0:1, 150:151], start=True, stop=True), inc=(sT, 1))
            k.wait("v", sT, sT.n)
            k.v(lambda v, dstv=dstv: v.tensor_copy(out=dstv, in_=PS[6][:, 0:1]), inc=(sV, 1))
            k.wait("t", sV, sV.n)
        for c in range(KC):
            k.wait("t", Lw, tws[c])
        w2mark = {0: 0, 1: 0}
        for b in range(B):
            Lr = next_ld()
            t2 = k.load(lambda: OO[0:1, 256:1280], lambda b=b: ap["mods"][b:b + 1, 4 * D:5 * D], Lr)
            k.wait("t", Lr, t2)
            bcast_f32(bc1, OO[0:1, 256:1280], 1024, bias=1.0)
            Lr = next_ld()
            t2 = k.load(lambda: OO[0:1, 256:1280], lambda b=b: ap["mods"][b:b + 1, 3 * D:4 * D], Lr)
            k.wait("t", Lr, t2)
            bcast_f32(bc2, OO[0:1, 256:1280], 1024)
            Lr = next_ld()
            t2 = k.load(lambda: OO[0:1, 256:1280], lambda b=b: ap["mods"][b:b + 1, 5 * D:6 * D], Lr)
            k.wait("t", Lr, t2)
            bcast_f32(bcg, OO[0:1, 256:1280], 1024)
            for tt in range(16):
                row = T * b + 128 * tt
                L = next_ld()
                k.wait("y", sV, sV.n)
                tw = k.load(lambda: xt, lambda row=row: ap["xs"][row:row + 128, :], L)
                k.wait("v", L, tw)
                k.v(lambda v: v.tensor_reduce(out=SV[:, 0:1], in_=xt, axis=AXX, op=ALU.add))
                k.v(lambda v: v.tensor_scalar(out=SV[:, 0:1], in0=SV[:, 0:1], scalar1=1.0 / D, scalar2=None, op0=ALU.mult))
                k.v(lambda v: v.tensor_scalar(out=zt, in0=xt, scalar1=SV[:, 0:1], scalar2=None, op0=ALU.subtract), inc=(sV, 1))
                k.wait("s", sV, sV.n)
                k.s(lambda s: s.activation(out=SQF[:, 0:1024], in_=zt, func=AF.Square), inc=(sS, 1))
                k.wait("v", sS, sS.n)
                k.v(lambda v: v.tensor_reduce(out=SV[:, 1:2], in_=SQF[:, 0:1024], axis=AXX, op=ALU.add), inc=(sV, 1))
                k.wait("s", sV, sV.n)
                k.s(lambda s: s.activation(out=SV[:, 1:2], in_=SV[:, 1:2], func=AF.Sqrt, bias=1e-5, scale=1.0 / D), inc=(sS, 1))
                k.wait("v", sS, sS.n)
                k.v(lambda v: v.reciprocal(out=SV[:, 2:3], in_=SV[:, 1:2]))
                k.v(lambda v: v.tensor_scalar(out=zt, in0=zt, scalar1=SV[:, 2:3], scalar2=None, op0=ALU.mult))
                k.v(lambda v: v.tensor_tensor(out=zt, in0=zt, in1=bc1, op=ALU.mult))
                k.v(lambda v: v.tensor_tensor(out=zt, in0=zt, in1=bc2, op=ALU.add))
                k.v(lambda v: v.tensor_reduce(out=SV[:, 0:1], in_=zt, axis=AXX, op=ALU.add))
                k.v(lambda v: v.tensor_scalar(out=SV[:, 0:1], in0=SV[:, 0:1], scalar1=1.0 / D, scalar2=None, op0=ALU.mult))
                k.v(lambda v: v.tensor_scalar(out=zt, in0=zt, scalar1=SV[:, 0:1], scalar2=None, op0=ALU.subtract), inc=(sV, 1))
                k.wait("s", sV, sV.n)
                k.s(lambda s: s.activation(out=SQF[:, 0:1024], in_=zt, func=AF.Square), inc=(sS, 1))
                k.wait("v", sS, sS.n)
                k.v(lambda v: v.tensor_reduce(out=SV[:, 1:2], in_=SQF[:, 0:1024], axis=AXX, op=ALU.add), inc=(sV, 1))
                k.wait("s", sV, sV.n)
                k.s(lambda s: s.activation(out=SV[:, 1:2], in_=SV[:, 1:2], func=AF.Sqrt, bias=1e-5, scale=1.0 / D), inc=(sS, 1))
                k.wait("v", sS, sS.n)
                k.v(lambda v: v.reciprocal(out=SV[:, 2:3], in_=SV[:, 1:2]))
                k.v(lambda v: v.tensor_scalar(out=zt, in0=zt, scalar1=SV[:, 2:3], scalar2=None, op0=ALU.mult))
                k.v(lambda v: v.tensor_tensor(out=zt, in0=zt, in1=g1b, op=ALU.mult))
                k.v(lambda v: v.tensor_tensor(out=zt, in0=zt, in1=b1b, op=ALU.add))
                k.v(lambda v: v.tensor_reduce(out=SV[:, 3:4], in_=zt, axis=AXX, op=ALU.max, apply_absolute_value=True))
                k.v(lambda v: v.tensor_scalar(out=SV[:, 3:4], in0=SV[:, 3:4], scalar1=1e-5, scalar2=None, op0=ALU.max))
                k.v(lambda v: v.reciprocal(out=SV[:, 4:5], in_=SV[:, 3:4]))
                k.v(lambda v: v.tensor_scalar(out=SV[:, 4:5], in0=SV[:, 4:5], scalar1=127.0, scalar2=None, op0=ALU.mult))
                k.v(lambda v: v.reciprocal(out=SV[:, 5:6], in_=SV[:, 4:5]))
                k.v(lambda v: v.tensor_scalar(out=zt, in0=zt, scalar1=SV[:, 4:5], scalar2=0.5, op0=ALU.mult, op1=ALU.add))
                vfloor(zt, zt, 1024, SQF[:, 0:1024])
                k.v(lambda v: v.tensor_scalar(out=xq, in0=zt, scalar1=-128.0, scalar2=127.0, op0=ALU.max, op1=ALU.min), inc=(sV, 1))
                k.v(lambda v: v.tensor_tensor(out=SV[:, 9:10], in0=SV[:, 5:6], in1=wm1, op=ALU.mult), inc=(sV, 1))
                k.wait("t", sV, sV.n)
                for g4 in range(2):
                    for cc in range(4):
                        c = 4 * g4 + cc
                        k.t(lambda t, c=c, cc=cc: t.transpose(PSB[:, 128 * cc:128 * (cc + 1)], xq[:, 128 * c:128 * (c + 1)], identb), inc=(sT, 1) if cc == 3 else None)
                    k.wait("v", sT, sT.n)
                    k.v(lambda v, g4=g4: v.tensor_copy(out=xqf[:, 512 * g4:512 * (g4 + 1)], in_=PSB[:, 0:512]), inc=(sV, 1))
                    k.wait("t", sV, sV.n)
                for mg in range(8):
                    for cc in range(4):
                        m = 4 * mg + cc
                        for c in range(KC):
                            k.t(lambda t, m=m, cc=cc, c=c: t.matmul(PS[0][:, 128 * cc:128 * (cc + 1)], W1Q[:, 4096 * c + 128 * m:4096 * c + 128 * (m + 1)], xqf[:, 128 * c:128 * (c + 1)], start=(c == 0 and cc == 0), stop=(c == KC - 1 and cc == 3)),
                                inc=(sT, 1) if (c == KC - 1 and cc == 3) else None)
                    k.wait("v", sT, sT.n)
                    k.v(lambda v: v.tensor_copy(out=OS1, in_=PS[0][:, 0:512]), inc=(sV, 1))
                    k.wait("t", sV, sV.n)
                    for cc in range(4):
                        k.t(lambda t, cc=cc: t.transpose(PS[1][:, 128 * cc:128 * (cc + 1)], OS1[:, 128 * cc:128 * (cc + 1)], identf), inc=(sT, 1) if cc == 3 else None)
                    k.wait("v", sT, sT.n)
                    k.v(lambda v, mg=mg: v.scalar_tensor_tensor(out=H1[:, 512 * mg:512 * (mg + 1)], in0=PS[1][:, 0:512], scalar=SV[:, 9:10], in1=bh1b[:, 512 * mg:512 * (mg + 1)], op0=ALU.mult, op1=ALU.add), inc=(sV, 1))
                    k.wait("t", sV, sV.n)
                k.wait("s", sV, sV.n)
                k.s(lambda s: s.activation(out=H1, in_=H1, func=AF.Gelu), inc=(sS, 1))
                k.wait("v", sS, sS.n)
                k.v(lambda v: v.tensor_reduce(out=SV[:, 0:1], in_=H1, axis=AXX, op=ALU.add))
                k.v(lambda v: v.tensor_scalar(out=SV[:, 0:1], in0=SV[:, 0:1], scalar1=1.0 / 4096.0, scalar2=None, op0=ALU.mult))
                k.v(lambda v: v.tensor_scalar(out=H1, in0=H1, scalar1=SV[:, 0:1], scalar2=None, op0=ALU.subtract), inc=(sV, 1))
                k.wait("s", sV, sV.n)
                k.s(lambda s: s.activation(out=SQF, in_=H1, func=AF.Square), inc=(sS, 1))
                k.wait("v", sS, sS.n)
                k.v(lambda v: v.tensor_reduce(out=SV[:, 1:2], in_=SQF, axis=AXX, op=ALU.add), inc=(sV, 1))
                k.wait("s", sV, sV.n)
                k.s(lambda s: s.activation(out=SV[:, 1:2], in_=SV[:, 1:2], func=AF.Sqrt, bias=1e-5, scale=1.0 / 4096.0), inc=(sS, 1))
                k.wait("v", sS, sS.n)
                k.v(lambda v: v.reciprocal(out=SV[:, 2:3], in_=SV[:, 1:2]))
                k.v(lambda v: v.tensor_scalar(out=H1, in0=H1, scalar1=SV[:, 2:3], scalar2=None, op0=ALU.mult))
                k.v(lambda v: v.tensor_tensor(out=H1, in0=H1, in1=g2b, op=ALU.mult), inc=(sV, 1))
                for part in range(4):
                    bcast_bf16(b2b, OB[0:1, 1024 * (part + 1):1024 * (part + 2)], 1024)
                    k.wait("v", sS, sS.n)
                    k.v(lambda v, part=part: v.tensor_tensor(out=H1[:, 1024 * part:1024 * (part + 1)], in0=H1[:, 1024 * part:1024 * (part + 1)], in1=b2b, op=ALU.add), inc=(sV, 1))
                    k.wait("s", sV, sV.n)
                k.v(lambda v: v.tensor_reduce(out=SV[:, 3:4], in_=H1, axis=AXX, op=ALU.max, apply_absolute_value=True))
                k.v(lambda v: v.tensor_scalar(out=SV[:, 3:4], in0=SV[:, 3:4], scalar1=1e-5, scalar2=None, op0=ALU.max))
                k.v(lambda v: v.reciprocal(out=SV[:, 4:5], in_=SV[:, 3:4]))
                k.v(lambda v: v.tensor_scalar(out=SV[:, 4:5], in0=SV[:, 4:5], scalar1=127.0, scalar2=None, op0=ALU.mult))
                k.v(lambda v: v.reciprocal(out=SV[:, 5:6], in_=SV[:, 4:5]))
                k.v(lambda v: v.tensor_scalar(out=H1, in0=H1, scalar1=SV[:, 4:5], scalar2=0.5, op0=ALU.mult, op1=ALU.add))
                for part in range(4):
                    sl = slice(1024 * part, 1024 * (part + 1))
                    vfloor(H1[:, sl], H1[:, sl], 1024, SQF[:, 0:1024])
                k.v(lambda v: v.tensor_scalar(out=xq2, in0=H1, scalar1=-128.0, scalar2=127.0, op0=ALU.max, op1=ALU.min), inc=(sV, 1))
                k.v(lambda v: v.tensor_tensor(out=SV[:, 9:10], in0=SV[:, 5:6], in1=wm2, op=ALU.mult), inc=(sV, 1))
                k.wait("t", sV, sV.n)
                for g4 in range(8):
                    for cc in range(4):
                        c = 4 * g4 + cc
                        k.t(lambda t, c=c, cc=cc: t.transpose(PSB[:, 128 * cc:128 * (cc + 1)], xq2[:, 128 * c:128 * (c + 1)], identb), inc=(sT, 1) if cc == 3 else None)
                    k.wait("v", sT, sT.n)
                    k.v(lambda v, g4=g4: v.tensor_copy(out=xq2f[:, 512 * g4:512 * (g4 + 1)], in_=PSB[:, 0:512]), inc=(sV, 1))
                    k.wait("t", sV, sV.n)
                for mg in range(2):
                    for cc in range(4):
                        m = 4 * mg + cc
                        slot = m % 2
                        wslot = W2Q[:, 4096 * slot:4096 * (slot + 1)]
                        Lw2 = next_ld()
                        k.wait("y", sT, w2mark[slot])
                        tws2 = k.load(lambda wslot=wslot: wslot,
                                      lambda m=m: ap["w2q"][128 * m:128 * (m + 1), :], Lw2)
                        k.wait("t", Lw2, tws2)
                        for c in range(32):
                            k.t(lambda t, c=c, cc=cc, wslot=wslot: t.matmul(PS[0][:, 128 * cc:128 * (cc + 1)], wslot[:, 128 * c:128 * (c + 1)], xq2f[:, 128 * c:128 * (c + 1)], start=(c == 0 and cc == 0), stop=(c == 31 and cc == 3)),
                                inc=(sT, 1) if (c == 31) else None)
                        w2mark[slot] = sT.n
                    k.wait("v", sT, sT.n)
                    k.v(lambda v: v.tensor_copy(out=OS1, in_=PS[0][:, 0:512]), inc=(sV, 1))
                    k.wait("t", sV, sV.n)
                    for cc in range(4):
                        k.t(lambda t, cc=cc: t.transpose(PS[1][:, 128 * cc:128 * (cc + 1)], OS1[:, 128 * cc:128 * (cc + 1)], identf), inc=(sT, 1) if cc == 3 else None)
                    k.wait("v", sT, sT.n)
                    k.v(lambda v, mg=mg: v.scalar_tensor_tensor(out=zt[:, 512 * mg:512 * (mg + 1)], in0=PS[1][:, 0:512], scalar=SV[:, 9:10], in1=BCP[:, 11264 + 512 * mg:11264 + 512 * (mg + 1)], op0=ALU.mult, op1=ALU.add), inc=(sV, 1))
                    k.wait("t", sV, sV.n)
                k.v(lambda v: v.tensor_tensor(out=zt, in0=zt, in1=bcg, op=ALU.mult))
                k.v(lambda v: v.tensor_tensor(out=xt, in0=xt, in1=zt, op=ALU.add), inc=(sV, 1))
                k.wait("g", sV, sV.n)
                S = next_st()
                ts = k.store(lambda row=row: ap["xs"][row:row + 128, :], lambda: xt, S)
                k.wait("v", S, ts)
                k.wait("y", S, ts)
        k.barrier(BAR)

    def bh2_bcast():
        Lb = next_ld()
        k.wait("y", sV, sV.n)
        tb = k.load(lambda: OO[0:1, 256:256 + 1024], lambda: ap["bh2"], Lb)
        k.wait("v", Lb, tb)
        k.v(lambda v: v.tensor_copy(out=OB[0:1, 0:1024], in_=OO[0:1, 256:256 + 1024]), inc=(sV, 1))
        k.wait("t", sV, sV.n)
        bcast_bf16(BCP[:, 11264:12288], OB[0:1, 0:1024], 1024)

    def ffn1_phase():
        xt = FP[:, 0:1024]
        yt = FP[:, 1024:2048]
        at = FP[:, 2048:3072]
        et = FP[:, 3072:4096]
        st_ = FP[:, 4096:5120]
        ut = FP[:, 5120:6144]
        cm = FP[:, 6144:7168]
        zt = FP[:, 7168:8192]
        OS1 = FP[:, 8192:8704]
        bc1 = FP[:, 10240:11264]
        bc2 = FP[:, 11264:12288]
        bcg = FP[:, 12288:13312]
        bfb = FP[:, 8704:9728]
        xq = XQb[:, 0:1024]
        xqf = XQb[:, 1024:2048]
        Lw = next_ld()
        tws = []
        for c in range(KC):
            tws.append(k.load(lambda c=c: W2Q[:, 1024 * c:1024 * (c + 1)] if False else W1Q[:, 1024 * c:1024 * (c + 1)], lambda c=c: ap["wf8"][128 * c:128 * (c + 1), :], Lw))
        Lb = next_ld()
        tb = k.load(lambda: OO[0:1, 256:256 + 1024], lambda: ap["bf8"], Lb)
        k.wait("t", Lb, tb)
        bcast_f32(bfb, OO[0:1, 256:256 + 1024], 1024)
        k.t(lambda t: t.matmul(PS[6][:, 0:1], onesf, OO[0:1, 144:145], start=True, stop=True), inc=(sT, 1))
        k.wait("v", sT, sT.n)
        k.v(lambda v: v.tensor_copy(out=SV[:, 12:13], in_=PS[6][:, 0:1]), inc=(sV, 1))
        k.wait("t", sV, sV.n)
        for c in range(KC):
            k.wait("t", Lw, tws[c])
        for b in range(B):
            Lr = next_ld()
            t2 = k.load(lambda: OO[0:1, 256:1280], lambda b=b: ap["mods"][B + b:B + b + 1, 4 * D:5 * D], Lr)
            k.wait("t", Lr, t2)
            bcast_f32(bc1, OO[0:1, 256:1280], 1024, bias=1.0)
            Lr = next_ld()
            t2 = k.load(lambda: OO[0:1, 256:1280], lambda b=b: ap["mods"][B + b:B + b + 1, 3 * D:4 * D], Lr)
            k.wait("t", Lr, t2)
            bcast_f32(bc2, OO[0:1, 256:1280], 1024)
            Lr = next_ld()
            t2 = k.load(lambda: OO[0:1, 256:1280], lambda b=b: ap["mods"][B + b:B + b + 1, 5 * D:6 * D], Lr)
            k.wait("t", Lr, t2)
            bcast_f32(bcg, OO[0:1, 256:1280], 1024)
            for tt in range(16):
                row = T * b + 128 * tt
                L = next_ld()
                k.wait("y", sV, sV.n)
                tw = k.load(lambda: xt, lambda row=row: ap["xs"][row:row + 128, :], L)
                k.wait("v", L, tw)
                k.v(lambda v: v.tensor_reduce(out=SV[:, 0:1], in_=xt, axis=AXX, op=ALU.add))
                k.v(lambda v: v.tensor_scalar(out=SV[:, 0:1], in0=SV[:, 0:1], scalar1=1.0 / D, scalar2=None, op0=ALU.mult))
                k.v(lambda v: v.tensor_scalar(out=zt, in0=xt, scalar1=SV[:, 0:1], scalar2=None, op0=ALU.subtract), inc=(sV, 1))
                k.wait("s", sV, sV.n)
                k.s(lambda s: s.activation(out=cm, in_=zt, func=AF.Square), inc=(sS, 1))
                k.wait("v", sS, sS.n)
                k.v(lambda v: v.tensor_reduce(out=SV[:, 1:2], in_=cm, axis=AXX, op=ALU.add), inc=(sV, 1))
                k.wait("s", sV, sV.n)
                k.s(lambda s: s.activation(out=SV[:, 1:2], in_=SV[:, 1:2], func=AF.Sqrt, bias=1e-5, scale=1.0 / D), inc=(sS, 1))
                k.wait("v", sS, sS.n)
                k.v(lambda v: v.reciprocal(out=SV[:, 2:3], in_=SV[:, 1:2]))
                k.v(lambda v: v.tensor_scalar(out=zt, in0=zt, scalar1=SV[:, 2:3], scalar2=None, op0=ALU.mult))
                k.v(lambda v: v.tensor_tensor(out=zt, in0=zt, in1=bc1, op=ALU.mult))
                k.v(lambda v: v.tensor_tensor(out=zt, in0=zt, in1=bc2, op=ALU.add))
                k.v(lambda v: v.tensor_reduce(out=SV[:, 3:4], in_=zt, axis=AXX, op=ALU.max, apply_absolute_value=True))
                k.v(lambda v: v.tensor_scalar(out=SV[:, 3:4], in0=SV[:, 3:4], scalar1=1e-12, scalar2=1.0 / 448.0, op0=ALU.max, op1=ALU.mult))
                k.v(lambda v: v.reciprocal(out=SV[:, 4:5], in_=SV[:, 3:4]))
                k.v(lambda v: v.tensor_scalar(out=yt, in0=zt, scalar1=SV[:, 4:5], scalar2=None, op0=ALU.mult))
                k.v(lambda v: v.tensor_scalar(out=at, in0=yt, scalar1=-1.0, scalar2=None, op0=ALU.mult))
                k.v(lambda v: v.tensor_tensor(out=at, in0=at, in1=yt, op=ALU.max))
                k.v(lambda v: v.tensor_scalar(out=at, in0=at, scalar1=1e-30, scalar2=None, op0=ALU.max), inc=(sV, 1))
                k.wait("s", sV, sV.n)
                k.s(lambda s: s.activation(out=at, in_=at, func=AF.Ln), inc=(sS, 1))
                k.wait("v", sS, sS.n)
                k.v(lambda v: v.tensor_scalar(out=et, in0=at, scalar1=1.0 / LN2C, scalar2=None, op0=ALU.mult))
                vfloor(et, et, 1024, cm)
                k.v(lambda v: v.tensor_scalar(out=et, in0=et, scalar1=-6.0, scalar2=8.0, op0=ALU.max, op1=ALU.min), inc=(sV, 1))
                k.wait("s", sV, sV.n)
                k.s(lambda s: s.activation(out=st_, in_=et, func=AF.Exp, bias=-3.0 * LN2C, scale=LN2C), inc=(sS, 1))
                k.wait("v", sS, sS.n)
                k.v(lambda v: v.reciprocal(out=ut, in_=st_))
                k.v(lambda v: v.tensor_tensor(out=ut, in0=yt, in1=ut, op=ALU.mult))
                k.v(lambda v: v.tensor_scalar(out=ut, in0=ut, scalar1=0.5, scalar2=None, op0=ALU.add))
                vfloor(ut, ut, 1024, cm)
                k.v(lambda v: v.tensor_tensor(out=xq, in0=ut, in1=st_, op=ALU.mult), inc=(sV, 1))
                k.v(lambda v: v.tensor_tensor(out=SV[:, 9:10], in0=SV[:, 3:4], in1=SV[:, 12:13], op=ALU.mult), inc=(sV, 1))
                k.wait("t", sV, sV.n)
                for g4 in range(2):
                    for cc in range(4):
                        c = 4 * g4 + cc
                        k.t(lambda t, c=c, cc=cc: t.transpose(PSB[:, 128 * cc:128 * (cc + 1)], xq[:, 128 * c:128 * (c + 1)], identb), inc=(sT, 1) if cc == 3 else None)
                    k.wait("v", sT, sT.n)
                    k.v(lambda v, g4=g4: v.tensor_copy(out=xqf[:, 512 * g4:512 * (g4 + 1)], in_=PSB[:, 0:512]), inc=(sV, 1))
                    k.wait("t", sV, sV.n)
                for mg in range(2):
                    for cc in range(4):
                        m = 4 * mg + cc
                        for c in range(KC):
                            k.t(lambda t, m=m, cc=cc, c=c: t.matmul(PS[0][:, 128 * cc:128 * (cc + 1)], W1Q[:, 1024 * c + 128 * m:1024 * c + 128 * (m + 1)], xqf[:, 128 * c:128 * (c + 1)], start=(c == 0 and cc == 0), stop=(c == KC - 1 and cc == 3)),
                                inc=(sT, 1) if (c == KC - 1 and cc == 3) else None)
                    k.wait("v", sT, sT.n)
                    k.v(lambda v: v.tensor_copy(out=OS1, in_=PS[0][:, 0:512]), inc=(sV, 1))
                    k.wait("t", sV, sV.n)
                    for cc in range(4):
                        k.t(lambda t, cc=cc: t.transpose(PS[1][:, 128 * cc:128 * (cc + 1)], OS1[:, 128 * cc:128 * (cc + 1)], identf), inc=(sT, 1) if cc == 3 else None)
                    k.wait("v", sT, sT.n)
                    k.v(lambda v, mg=mg: v.scalar_tensor_tensor(out=zt[:, 512 * mg:512 * (mg + 1)], in0=PS[1][:, 0:512], scalar=SV[:, 9:10], in1=bfb[:, 512 * mg:512 * (mg + 1)], op0=ALU.mult, op1=ALU.add), inc=(sV, 1))
                    k.wait("t", sV, sV.n)
                k.v(lambda v: v.tensor_tensor(out=zt, in0=zt, in1=bcg, op=ALU.mult))
                k.v(lambda v: v.tensor_tensor(out=xt, in0=xt, in1=zt, op=ALU.add), inc=(sV, 1))
                k.wait("g", sV, sV.n)
                S = next_st()
                ts = k.store(lambda row=row: y_out[row:row + 128, :], lambda: xt, S)
                k.wait("v", S, ts)
                k.wait("y", S, ts)
        k.barrier(BAR)

    import os
    KPH = int(os.environ.get("KPH", "99"))
    bh2_bcast()
    for lay in range(2):
        if KPH >= (1 if lay == 0 else 3):
            for b in range(B):
                ln_mod_phase(lay, b, 0)
                attn_phase(lay, b)
                outproj_phase(lay, b)
        if lay == 0:
            if KPH >= 2:
                ffn0_phase()
        else:
            if KPH >= 4:
                ffn1_phase()

    for s_ in sts:
        k.wait("g", s_, s_.n)

    with nc.Block() as block:
        @block.tensor
        def _(t):
            for f in k.q["t"]:
                f(t)

        @block.vector
        def _(v):
            for f in k.q["v"]:
                f(v)

        @block.scalar
        def _(s):
            for f in k.q["s"]:
                f(s)

        @block.gpsimd
        def _(g):
            for f in k.q["g"]:
                f(g)

        @block.sync
        def _(y):
            for f in k.q["y"]:
                f(y)

    es.close()
    return nc


_NC_CACHE = [None]


def kernel(**inputs):
    n = 8
    xfull = np.ascontiguousarray(np.asarray(inputs["L0_attn_x"], dtype=np.float32))
    maskc = np.zeros((128, 4 * 512), dtype=np.float64)
    for m in range(4):
        i = np.arange(128)[:, None]
        j = np.arange(512)[None, :]
        maskc[:, 512 * m:512 * (m + 1)] = np.where(i + 128 * m > j, -1e10, 0.0)
    maskc = maskc.astype(ml_dtypes.bfloat16)
    ramp = (np.arange(128)[:, None] + 128 * np.arange(16)[None, :] - 2047).astype(np.float32)
    common = {
        "w_qkv": np.asarray(inputs["L0_attn_w_qkv"], np.float32),
        "w_out": np.asarray(inputs["L0_attn_w_out"], np.float32),
        "b_out": np.asarray(inputs["L0_attn_b_out"], np.float32).reshape(1, D),
        "g1r": np.asarray(inputs["L0_ffn_ln1_g"], np.float32).reshape(1, D),
        "b1r": np.asarray(inputs["L0_ffn_ln1_b"], np.float32).reshape(1, D),
        "w1": np.asarray(inputs["L0_ffn_w1"], np.float32),
        "bh1": np.asarray(inputs["L0_ffn_b1"], np.float32).reshape(1, 4 * D),
        "g2r": np.asarray(inputs["L0_ffn_ln2_g"], np.float32).reshape(1, 4 * D),
        "b2r": np.asarray(inputs["L0_ffn_ln2_b"], np.float32).reshape(1, 4 * D),
        "w2": np.asarray(inputs["L0_ffn_w2"], np.float32),
        "bh2": np.asarray(inputs["L0_ffn_b2"], np.float32).reshape(1, D),
        "wmod0": np.asarray(inputs["L0_wmod"], np.float32),
        "Wq": np.asarray(inputs["L1_attn_Wq"], np.float32),
        "bq": np.asarray(inputs["L1_attn_bq"], np.float32).reshape(D, 1),
        "Wk": np.asarray(inputs["L1_attn_Wk"], np.float32),
        "bk": np.asarray(inputs["L1_attn_bk"], np.float32).reshape(D, 1),
        "Wv": np.asarray(inputs["L1_attn_Wv"], np.float32),
        "bv": np.asarray(inputs["L1_attn_bv"], np.float32).reshape(D, 1),
        "Wo": np.asarray(inputs["L1_attn_Wo"], np.float32),
        "bo": np.asarray(inputs["L1_attn_bo"], np.float32).reshape(1, D),
        "slopes": np.asarray(inputs["L1_attn_slopes"], np.float32).reshape(1, 16),
        "w8": np.asarray(inputs["L1_ffn_weight"]).view(np.uint8),
        "wsc": np.asarray(inputs["L1_ffn_weight_scale"], np.float32).reshape(1, 1),
        "bf8": np.asarray(inputs["L1_ffn_bias"], np.float32).reshape(1, D),
        "wmod1": np.asarray(inputs["L1_wmod"], np.float32),
        "identf": np.eye(128, dtype=np.float32),
        "identb": np.eye(128, dtype=ml_dtypes.bfloat16),
        "maskc": maskc,
        "ramp": ramp,
    }
    c0 = np.asarray(inputs["L0_c"], np.float32)
    c1 = np.asarray(inputs["L1_c"], np.float32)
    in_maps = []
    for i in range(n):
        m = dict(common)
        m["x_in"] = np.ascontiguousarray(xfull[2 * i:2 * i + 2].reshape(B * T, D))
        m["c0"] = np.ascontiguousarray(c0[2 * i:2 * i + 2])
        m["c1"] = np.ascontiguousarray(c1[2 * i:2 * i + 2])
        in_maps.append(m)
    if _NC_CACHE[0] is None:
        _NC_CACHE[0] = build_program()
    nc = _NC_CACHE[0]
    res = run_bass_kernel_spmd(nc, in_maps, core_ids=list(range(n)))
    out = np.concatenate([r["y"].reshape(1, B, T, D) for r in res.results], axis=0)
    return np.ascontiguousarray(out.reshape(16, T, D)).astype(np.float32)
